# version 05
# baseline (speedup 1.0000x reference)
import os
import numpy as np
import ml_dtypes
from contextlib import ExitStack

import concourse.bass as bass
import concourse.mybir as mybir
from concourse.bass_utils import run_bass_kernel_spmd

F32 = mybir.dt.float32
BF16 = mybir.dt.bfloat16
I32 = mybir.dt.int32
AF = mybir.ActivationFunctionType
OP = mybir.AluOpType
AX = mybir.AxisListType.X
bf16 = ml_dtypes.bfloat16

B, S, E, H = 4, 4096, 1024, 16
HID = 3072
QT = 2048
NCH = E // 128
EPS = 1e-5
SC127 = float(np.float32(1.0) / np.float32(127.0))
SC2 = float(np.float32(SC127) * np.float32(SC127))

QC = 256
NQC = QT // QC
NKC = S // 128
NHP = H // 2
NT1 = int(os.environ.get("KNT1", S // 128))
NOWN = min(QT // 128, NT1)
NTC3 = QT // 256
NHC = HID // 128
VROW = H * 65


def build_program():
    nc = bass.Bass("TRN2", target_bir_lowering=False, detect_race_conditions=False,
                   dynamic_dma_scratch_size=4096)

    xs = nc.dram_tensor("xs", [S, E], F32, kind="ExternalInput")
    wqi = nc.dram_tensor("wqi", [E, E], F32, kind="ExternalInput")
    wki = nc.dram_tensor("wki", [E, E], F32, kind="ExternalInput")
    wvi = nc.dram_tensor("wvi", [E, E], F32, kind="ExternalInput")
    gbc = nc.dram_tensor("gbc", [128, E], F32, kind="ExternalInput")
    bbc = nc.dram_tensor("bbc", [128, E], F32, kind="ExternalInput")
    wgi = nc.dram_tensor("wgi", [HID, E], I32, kind="ExternalInput")
    wui = nc.dram_tensor("wui", [HID, E], I32, kind="ExternalInput")
    wdi = nc.dram_tensor("wdi", [E, HID], I32, kind="ExternalInput")
    dst = nc.dram_tensor("dst", [S, QT], F32, kind="ExternalInput")
    idn = nc.dram_tensor("idn", [128, 128], BF16, kind="ExternalInput")
    cst = nc.dram_tensor("cst", [128, 8], F32, kind="ExternalInput")
    y = nc.dram_tensor("y", [QT, E], F32, kind="ExternalOutput")

    wqb = nc.dram_tensor("wqb", [E, E], BF16)
    wkb = nc.dram_tensor("wkb", [E, E], BF16)
    wvb = nc.dram_tensor("wvb", [E, E], BF16)
    wgb = nc.dram_tensor("wgb", [HID, E], BF16)
    wub = nc.dram_tensor("wub", [HID, E], BF16)
    wdb = nc.dram_tensor("wdb", [E, HID], BF16)
    qh = nc.dram_tensor("qh", [NCH, 128, QT], BF16)
    xlh = nc.dram_tensor("xlh", [NCH, 128, S], BF16)
    x1h = nc.dram_tensor("x1h", [QT, E], F32)

    PH = float(os.environ.get("KPHASES", "9"))
    KDBG = int(os.environ.get("KDBG", "0"))
    if KDBG:
        dbg_avb = nc.dram_tensor("dbg_avb", [NQC * NHP, 128, 512], BF16, kind="ExternalOutput")
        dbg_qct = nc.dram_tensor("dbg_qct", [NQC, 128, 2048], BF16, kind="ExternalOutput")
        dbg_atm = nc.dram_tensor("dbg_atm", [NQC, 128, 2048], BF16, kind="ExternalOutput")
        dbg_xlh = nc.dram_tensor("dbg_xlh", [NCH, 128, S], BF16, kind="ExternalOutput")
        dbg_qh = nc.dram_tensor("dbg_qh", [NCH, 128, QT], BF16, kind="ExternalOutput")
        dbg_k = nc.dram_tensor("dbg_k", [128, NCH * S], BF16, kind="ExternalOutput")
        dbg_v = nc.dram_tensor("dbg_v", [128, NKC * VROW], BF16, kind="ExternalOutput")
        dbg_x1 = nc.dram_tensor("dbg_x1", [QT, E], F32, kind="ExternalOutput")
    KSUB = int(os.environ.get("KSUB", "9"))
    KSER = int(os.environ.get("KSER", "0"))
    KWT = int(os.environ.get("KWT", "1"))
    cur_phase = [0]
    phase_tot = {}
    sched = []

    def on(key):
        def deco(fn):
            sched.append((key, fn, cur_phase[0]))
        return deco

    with ExitStack() as ctx:
        Eo = ctx.enter_context

        gb_sb = Eo(nc.sbuf_tensor("gb_sb", [128, E], F32, side="left"))
        bb_sb = Eo(nc.sbuf_tensor("bb_sb", [128, E], F32, side="left"))
        id_sb = Eo(nc.sbuf_tensor("id_sb", [128, 128], BF16, side="left"))
        cs_sb = Eo(nc.sbuf_tensor("cs_sb", [128, 8], F32, side="left"))
        st3_sb = Eo(nc.sbuf_tensor("st3_sb", [128, 64], F32, side="left"))
        skv = ExitStack()
        k_sb = skv.enter_context(nc.sbuf_tensor("k_sb", [128, NCH * S], BF16, side="left"))
        v_sb = skv.enter_context(nc.sbuf_tensor("v_sb", [128, NKC * VROW], BF16, side="left"))

        nc.const_aps.register(F32, EPS, cs_sb[:, 0:1])
        nc.const_aps.register(F32, -0.5, cs_sb[:, 1:2])
        nc.const_aps.register(F32, 1.0 / 32.0, cs_sb[:, 2:3])
        nc.const_aps.register(F32, SC127, cs_sb[:, 3:4])
        nc.const_aps.register(F32, -1.0, cs_sb[:, 4:5])

        sem_t = Eo(nc.semaphore("sem_t")); nt = [0]
        sem_v = Eo(nc.semaphore("sem_v")); nv = [0]
        sem_s = Eo(nc.semaphore("sem_s")); ns = [0]
        sem_g = Eo(nc.semaphore("sem_g")); ng = [0]

        streams = {}

        def mksem(name):
            sem = Eo(nc.semaphore(name))
            ctr = [0]
            streams[name] = (sem, ctr)
            return sem, ctr

        s_c0, n_c0 = mksem("s_c0")
        s_castq, n_castq = mksem("s_castq")
        s_castf, n_castf = mksem("s_castf")
        s_wqk, n_wqk = mksem("s_wqk")
        s_wv, n_wv = mksem("s_wv")
        s_w3, n_w3 = mksem("s_w3")
        s_x0, n_x0 = mksem("s_x0")
        s_x1l, n_x1l = mksem("s_x1l")
        s_xlh0, n_xlh0 = mksem("s_xlh0")
        s_xlh1, n_xlh1 = mksem("s_xlh1")
        s_qst, n_qst = mksem("s_qst")
        s_xfb0, n_xfb0 = mksem("s_xfb0")
        s_xfb1, n_xfb1 = mksem("s_xfb1")
        s_qc, n_qc = mksem("s_qc")
        s_dist, n_dist = mksem("s_dist")
        s_xr0, n_xr0 = mksem("s_xr0")
        s_xr1, n_xr1 = mksem("s_xr1")
        s_xs0, n_xs0 = mksem("s_xs0")
        s_xs1, n_xs1 = mksem("s_xs1")
        s_x13, n_x13 = mksem("s_x13")
        s_y, n_y = mksem("s_y")
        s_dbg, n_dbg = mksem("s_dbg")
        s_xP = [(s_x0, n_x0), (s_x1l, n_x1l)]
        s_xlhP = [(s_xlh0, n_xlh0), (s_xlh1, n_xlh1)]
        s_xfbP = [(s_xfb0, n_xfb0), (s_xfb1, n_xfb1)]
        s_xrP = [(s_xr0, n_xr0), (s_xr1, n_xr1)]
        s_xsP = [(s_xs0, n_xs0), (s_xs1, n_xs1)]

        def bump(c, k=1):
            c[0] += k
            return c[0]

        def vec_lnstats(e, src, sq, s1, m_ap, s2_ap):
            e.reduce_sum(s1, src, axis=AX)
            e.tensor_tensor(sq, src, src, OP.mult)
            e.reduce_sum(s2_ap, sq, axis=AX)
            e.tensor_scalar(m_ap, s1, 1.0 / E, None, OP.mult)
            e.tensor_scalar(s2_ap, s2_ap, 1.0 / E, None, OP.mult)
            return e.scalar_tensor_tensor(s2_ap, m_ap, m_ap, s2_ap, OP.mult, OP.subtract)

        def scl_rstd(e, var_neg, r_ap):
            e.activation(r_ap, var_neg, AF.Ln, scale=-1.0, bias=EPS)
            return e.activation(r_ap, r_ap, AF.Exp, scale=-0.5)

        @on("sy")
        def _(e):
            e.dma_start(cs_sb[:, :], cst[:, :]).then_inc(s_c0, 16)
            e.dma_start(id_sb[:, :], idn[:, :]).then_inc(s_c0, 16)
            e.dma_start(gb_sb[:, :], gbc[:, :]).then_inc(s_c0, 16)
            e.dma_start(bb_sb[:, :], bbc[:, :]).then_inc(s_c0, 16)
        p0_a = bump(n_c0, 64)

        @on("g")
        def _(e):
            e.memset(v_sb[:, :], 1.0).then_inc(sem_g, 1)
            e.dma_start(wqb[:, :], wqi[:, :]).then_inc(s_castq, 16)
            e.dma_start(wkb[:, :], wki[:, :]).then_inc(s_castq, 16)
            e.dma_start(wvb[:, :], wvi[:, :]).then_inc(s_castq, 16)
            e.dma_start(wgb[:, :], wgi[:, :]).then_inc(s_castf, 16)
            e.dma_start(wub[:, :], wui[:, :]).then_inc(s_castf, 16)
            e.dma_start(wdb[:, :], wdi[:, :]).then_inc(s_castf, 16)
        memset_g = bump(ng, 1)
        qkv_cast = bump(n_castq, 48)
        ffn_cast = bump(n_castf, 48)
        phase_tot[0.5] = {k: c[0] for k, (s, c) in streams.items()}
        cur_phase[0] = 0.5

        s1a = ExitStack()
        A = s1a.enter_context
        wqT = A(nc.sbuf_tensor("wqT", [128, NCH * E], BF16, side="right"))
        wkT = A(nc.sbuf_tensor("wkT", [128, NCH * E], BF16, side="right"))
        x_t = A(nc.sbuf_tensor("x_t", [128, 2 * E], F32, side="right"))
        sq_t = A(nc.sbuf_tensor("sq_t", [128, E], BF16, side="right"))
        xln_t = A(nc.sbuf_tensor("xln_t", [128, 2 * E], BF16, side="right"))
        xfm_t = A(nc.sbuf_tensor("xfm_t", [128, 2 * 1024], BF16, side="right"))
        qev_t = A(nc.sbuf_tensor("qev_t", [128, 1024], BF16, side="right"))
        stat_t = A(nc.sbuf_tensor("stat_t", [128, 8], F32, side="right"))
        ptr = A(nc.psum_tensor("ptr", [128, 2048], F32))
        pk = A(nc.psum_tensor("pk", [128, 2048], F32))

        @on("sy")
        def _(e):
            if not KWT:
                return
            e.wait_ge(s_castq, qkv_cast)
            for fi in range(NCH):
                e.dma_start_transpose(wqT[:, fi * E:(fi + 1) * E], wqb[:, fi * 128:(fi + 1) * 128]).then_inc(s_wqk, 16)
            for fi in range(NCH):
                e.dma_start_transpose(wkT[:, fi * E:(fi + 1) * E], wkb[:, fi * 128:(fi + 1) * 128]).then_inc(s_wqk, 16)
        wqk_a = bump(n_wqk, 2 * NCH * 16) if KWT else 0

        x_a = [0] * NT1
        stats_v = [0] * NT1
        rstd_s = [0] * NT1
        apply_v = [0] * NT1
        tr_t = [[0, 0] for _ in range(NT1)]
        fm_s = [[0, 0] for _ in range(NT1)]
        xlh_o = [0] * NT1
        kmm_t = [[0, 0] for _ in range(NT1)]
        kev_s = [[0, 0] for _ in range(NT1)]
        qmm_t = [[0, 0] for _ in range(NT1)]
        qev_s = [[0, 0] for _ in range(NT1)]
        qst_o = [0] * NT1

        k3 = k_sb[:, :].rearrange("p (c s) -> p c s", s=S)
        pk3 = pk[:, :].rearrange("p (c t) -> p c t", t=512)
        ptr3v = ptr[:, :].rearrange("p (c t) -> p c t", t=512)

        for i in range(NT1):
            sl = i % 2
            own = i < NOWN

            @on("sy")
            def _(e, i=i, sl=sl):
                if i >= 2:
                    e.wait_ge(sem_v, apply_v[i - 2])
                e.dma_start(x_t[:, sl * E:(sl + 1) * E], xs[i * 128:(i + 1) * 128, :]).then_inc(s_xP[sl][0], 16)
            x_a[i] = bump(s_xP[sl][1], 16)

            @on("v")
            def _(e, i=i, sl=sl):
                e.wait_ge(s_xP[sl][0], x_a[i])
                if i >= 1:
                    e.wait_ge(sem_s, rstd_s[i - 1])
                vec_lnstats(e, x_t[:, sl * E:(sl + 1) * E], sq_t[:, :], stat_t[:, 0:1],
                            stat_t[:, 2:3], stat_t[:, 1:2]).then_inc(sem_v, 1)
            stats_v[i] = bump(nv)

            @on("s")
            def _(e, i=i):
                if i == 0:
                    e.wait_ge(s_c0, p0_a)
                e.wait_ge(sem_v, stats_v[i])
                scl_rstd(e, stat_t[:, 1:2], stat_t[:, 3:4]).then_inc(sem_s, 1)
            rstd_s[i] = bump(ns)

            @on("v")
            def _(e, i=i, sl=sl):
                e.wait_ge(sem_s, rstd_s[i])
                if i >= 2:
                    e.wait_ge(sem_t, tr_t[i - 2][1])
                e.tensor_scalar(xln_t[:, sl * E:(sl + 1) * E], x_t[:, sl * E:(sl + 1) * E],
                                stat_t[:, 2:3], stat_t[:, 3:4], OP.subtract, OP.mult).then_inc(sem_v, 1)
            apply_v[i] = bump(nv)

            if KSUB < 2:
                continue

            fsl = i % 2
            for r in range(2):
                @on("t")
                def _(e, i=i, sl=sl, r=r):
                    if r == 0:
                        if i == 0:
                            e.wait_ge(s_c0, p0_a)
                        e.wait_ge(sem_v, apply_v[i])
                        if i >= 1:
                            e.wait_ge(sem_s, fm_s[i - 1][1])
                    else:
                        e.wait_ge(sem_s, fm_s[i][0])
                    mm = None
                    for j in range(4 * r, 4 * r + 4):
                        mm = e.matmul(ptr[:, (j % 4) * 512:(j % 4) * 512 + 128],
                                      xln_t[:, sl * E + j * 128: sl * E + (j + 1) * 128],
                                      id_sb[:, :], start=True, stop=True)
                    mm.then_inc(sem_t, 1)
                tr_t[i][r] = bump(nt)

                @on("s")
                def _(e, i=i, fsl=fsl, r=r):
                    e.wait_ge(sem_t, tr_t[i][r])
                    if r == 0 and i >= 2:
                        e.wait_ge(sem_t, (qmm_t if i - 2 < NOWN else kmm_t)[i - 2][1])
                        e.wait_ge(s_xlhP[i % 2][0], xlh_o[i - 2])
                    e.copy(xfm_t[:, fsl * 1024 + r * 512: fsl * 1024 + (r + 1) * 512],
                           ptr3v[:, :, 0:128]).then_inc(sem_s, 1)
                fm_s[i][r] = bump(ns)

            @on("sy")
            def _(e, i=i, fsl=fsl):
                e.wait_ge(sem_s, fm_s[i][1])
                src = xfm_t[:, fsl * 1024:(fsl + 1) * 1024].rearrange("p (c t) -> p c t", c=NCH)
                d = xlh[:, :, i * 128:(i + 1) * 128].transpose([1, 0, 2])
                e.dma_start(d, src).then_inc(s_xlhP[i % 2][0], 16)
            xlh_o[i] = bump(s_xlhP[i % 2][1], 16)

            if KSUB < 3:
                continue

            for h in range(2):
                @on("t")
                def _(e, i=i, fsl=fsl, h=h):
                    if h == 0:
                        e.wait_ge(sem_s, fm_s[i][1])
                        e.wait_ge(s_wqk, wqk_a)
                        if i >= 1:
                            e.wait_ge(sem_s, (qev_s if i - 1 < NOWN else kev_s)[i - 1][1])
                    else:
                        e.wait_ge(sem_s, kev_s[i][0])
                    xf = xfm_t[:, fsl * 1024:(fsl + 1) * 1024]
                    mm = None
                    for fo in range(4 * h, 4 * h + 4):
                        for fi in range(NCH):
                            mm = e.matmul(pk[:, (fo % 4) * 512:(fo % 4) * 512 + 128],
                                          wkT[:, fi * E + fo * 128: fi * E + (fo + 1) * 128],
                                          xf[:, fi * 128:(fi + 1) * 128],
                                          start=(fi == 0), stop=(fi == NCH - 1))
                    mm.then_inc(sem_t, 1)
                kmm_t[i][h] = bump(nt)

                @on("s")
                def _(e, i=i, h=h):
                    e.wait_ge(sem_t, kmm_t[i][h])
                    d = k3[:, 4 * h:4 * h + 4, i * 128:(i + 1) * 128]
                    e.copy(d, pk3[:, :, 0:128]).then_inc(sem_s, 1)
                kev_s[i][h] = bump(ns)

            if own and KSUB >= 4:
                for h in range(2):
                    @on("t")
                    def _(e, i=i, fsl=fsl, h=h):
                        e.wait_ge(sem_s, kev_s[i][1] if h == 0 else qev_s[i][0])
                        xf = xfm_t[:, fsl * 1024:(fsl + 1) * 1024]
                        mm = None
                        for fo in range(4 * h, 4 * h + 4):
                            for fi in range(NCH):
                                mm = e.matmul(pk[:, (fo % 4) * 512:(fo % 4) * 512 + 128],
                                              wqT[:, fi * E + fo * 128: fi * E + (fo + 1) * 128],
                                              xf[:, fi * 128:(fi + 1) * 128],
                                              start=(fi == 0), stop=(fi == NCH - 1))
                        mm.then_inc(sem_t, 1)
                    qmm_t[i][h] = bump(nt)

                    @on("s")
                    def _(e, i=i, h=h):
                        e.wait_ge(sem_t, qmm_t[i][h])
                        if i >= 1 and h == 0:
                            e.wait_ge(s_qst, qst_o[i - 1])
                        e.copy(qev_t[:, h * 512:(h + 1) * 512].rearrange("p (c t) -> p c t", t=128),
                               pk3[:, :, 0:128]).then_inc(sem_s, 1)
                    qev_s[i][h] = bump(ns)

                @on("sy")
                def _(e, i=i):
                    e.wait_ge(sem_s, qev_s[i][1])
                    src = qev_t[:, :].rearrange("p (c t) -> p c t", c=NCH)
                    d = qh[:, :, i * 128:(i + 1) * 128].transpose([1, 0, 2])
                    e.dma_start(d, src).then_inc(s_qst, 16)
                qst_o[i] = bump(n_qst, 16)

        bar1_t = nt[0]
        bar1_v = nv[0]
        bar1_s = ns[0]
        phase_tot[1] = {k: c[0] for k, (s, c) in streams.items()}
        cur_phase[0] = 1

        s1a.close()

        s1b = ExitStack()
        A = s1b.enter_context
        wvT = A(nc.sbuf_tensor("wvT", [128, NCH * E], BF16, side="right"))
        xfb_t = A(nc.sbuf_tensor("xfb_t", [128, 2 * 1024], BF16, side="right"))
        pvA = A(nc.psum_tensor("pvA", [128, 512], F32))
        pvB = A(nc.psum_tensor("pvB", [128, 512], F32))

        v4 = v_sb[:, :].rearrange("p (k h d) -> p k h d", k=NKC, h=H)

        @on("sy")
        def _(e):
            e.wait_ge(sem_t, bar1_t)
            e.wait_ge(sem_v, bar1_v)
            e.wait_ge(sem_s, bar1_s)
            e.wait_ge(s_castq, qkv_cast)
            for fi in range(NCH):
                e.dma_start_transpose(wvT[:, fi * E:(fi + 1) * E], wvb[:, fi * 128:(fi + 1) * 128]).then_inc(s_wv, 16)
        wv_a = bump(n_wv, NCH * 16)

        xfb_a = [0] * NT1
        vmm_t = [[0, 0] for _ in range(NT1)]
        vev_s = [[0, 0] for _ in range(NT1)]

        for i in range(NT1):
            sl = i % 2

            @on("sy")
            def _(e, i=i, sl=sl):
                e.wait_ge(s_xlhP[i % 2][0], xlh_o[i])
                if i >= 2:
                    e.wait_ge(sem_t, vmm_t[i - 2][1])
                s = xlh[:, :, i * 128:(i + 1) * 128].transpose([1, 0, 2])
                e.dma_start(xfb_t[:, sl * 1024:(sl + 1) * 1024].rearrange("p (c t) -> p c t", c=NCH),
                            s).then_inc(s_xfbP[sl][0], 16)
            xfb_a[i] = bump(s_xfbP[sl][1], 16)

            for h in range(2):
                @on("t")
                def _(e, i=i, sl=sl, h=h):
                    if h == 0:
                        e.wait_ge(s_xfbP[sl][0], xfb_a[i])
                        e.wait_ge(s_wv, wv_a)
                    if i >= 1:
                        e.wait_ge(sem_s, vev_s[i - 1][h])
                    bank = pvA if h == 0 else pvB
                    mm = None
                    for fi in range(NCH):
                        mm = e.matmul(bank[:, :],
                                      xfb_t[:, sl * 1024 + fi * 128: sl * 1024 + (fi + 1) * 128],
                                      wvT[:, fi * E + h * 512: fi * E + h * 512 + 512],
                                      start=(fi == 0), stop=(fi == NCH - 1))
                    mm.then_inc(sem_t, 1)
                vmm_t[i][h] = bump(nt)

                @on("s")
                def _(e, i=i, h=h):
                    e.wait_ge(sem_t, vmm_t[i][h])
                    if i == 0 and h == 0:
                        e.wait_ge(sem_g, memset_g)
                    bank = pvA if h == 0 else pvB
                    d = v4[:, i, 8 * h:8 * h + 8, 0:64]
                    s = bank[:, :].rearrange("p (h d) -> p h d", d=64)
                    e.copy(d, s).then_inc(sem_s, 1)
                vev_s[i][h] = bump(ns)

        bar2_t = nt[0]
        bar2_v = nv[0]
        bar2_s = ns[0]
        phase_tot[2] = {k: c[0] for k, (s, c) in streams.items()}
        cur_phase[0] = 2

        s1b.close()

        s2 = ExitStack()
        A = s2.enter_context
        dist_t = A(nc.sbuf_tensor("dist_t", [128, NKC * QC], F32, side="right"))
        probs_t = A(nc.sbuf_tensor("probs_t", [128, 4 * 512], BF16, side="right"))
        qc_t = A(nc.sbuf_tensor("qc_t", [128, 2 * 2048], BF16, side="right"))
        atm_t = A(nc.sbuf_tensor("atm_t", [128, 2048], BF16, side="right"))
        avb_t = A(nc.sbuf_tensor("avb_t", [128, 512], BF16, side="right"))
        xr_t = A(nc.sbuf_tensor("xr_t", [128, 2 * E], F32, side="right"))
        x1_t = A(nc.sbuf_tensor("x1_t", [128, 2 * E], F32, side="right"))
        t1_t = A(nc.sbuf_tensor("t1_t", [128, E], F32, side="right"))
        sq2_t = A(nc.sbuf_tensor("sq2_t", [128, E], BF16, side="right"))
        st2_t = A(nc.sbuf_tensor("st2_t", [128, 16], F32, side="right"))
        pars = A(nc.psum_tensor("pars", [128, 4096], F32))

        qc_a = [0] * NQC
        dist_a = [0] * NQC
        qk_t, mult_v, exp_s, av_t, div_prev = {}, {}, {}, {}, {}
        epi_v = [0] * (NQC * NHP)
        epi_s = [0] * (NQC * NHP)
        ln2_v = [0] * NQC
        x1st_o = [0] * NQC
        xr_a = [0] * NQC
        PREV4 = {0: 28, 1: 29, 2: 30, 3: 31}

        for qc in range(NQC):
            qsl = qc % 2

            @on("sy")
            def _(e, qc=qc):
                if qc == 0:
                    e.wait_ge(sem_t, bar2_t)
                    e.wait_ge(sem_v, bar2_v)
                    e.wait_ge(sem_s, bar2_s)
                    e.wait_ge(s_qst, qst_o[NOWN - 1])
                if qc >= 2:
                    e.wait_ge(sem_t, qk_t[(qc - 2, NHP - 1, NKC - 1)])
                s = qh[:, :, qc * QC:(qc + 1) * QC].transpose([1, 0, 2])
                e.dma_start(qc_t[:, (qc % 2) * 2048:(qc % 2 + 1) * 2048].rearrange("p (c q) -> p c q", c=NCH),
                            s).then_inc(s_qc, 16)
            qc_a[qc] = bump(n_qc, 16)

            for dg in range(4):
                @on("sy")
                def _(e, qc=qc, dg=dg):
                    if qc >= 1:
                        e.wait_ge(sem_v, mult_v[(qc - 1, NHP - 1, dg * 8 + 7)])
                    s = dst[dg * 1024:(dg + 1) * 1024, qc * QC:(qc + 1) * QC]
                    s = s.rearrange("(c k) q -> c k q", k=128).transpose([1, 0, 2])
                    e.dma_start(dist_t[:, dg * 8 * QC:(dg + 1) * 8 * QC].rearrange("p (c q) -> p c q", c=8),
                                s).then_inc(s_dist, 16)
            dist_a[qc] = bump(n_dist, 64)

            @on("sy")
            def _(e, qc=qc, qsl=qsl):
                if qc >= 2:
                    e.wait_ge(sem_v, ln2_v[qc - 2])
                    e.wait_ge(s_xsP[qc % 2][0], x1st_o[qc - 2])
                e.dma_start(xr_t[:, qsl * E:(qsl + 1) * E], xs[qc * QC:qc * QC + 128, :]).then_inc(s_xrP[qsl][0], 16)
                e.dma_start(x1_t[:, qsl * E:(qsl + 1) * E], xs[qc * QC + 128:qc * QC + 256, :]).then_inc(s_xrP[qsl][0], 16)
            xr_a[qc] = bump(s_xrP[qsl][1], 32)

            for g in range(NHP):
                gi = qc * NHP + g
                hdA, hdB = 2 * g, 2 * g + 1

                SLOT_LAST = {0: 30, 1: 31, 2: 29}

                def em_qk(kc):
                    sl3 = kc % 3

                    @on("t")
                    def _(e, qc=qc, g=g, kc=kc, sl3=sl3):
                        if kc >= 3:
                            e.wait_ge(sem_s, exp_s[(qc, g, kc - 3)])
                        else:
                            pg = (qc, g - 1) if g >= 1 else ((qc - 1, NHP - 1) if qc >= 1 else None)
                            if pg is not None:
                                e.wait_ge(sem_s, exp_s[(pg[0], pg[1], SLOT_LAST[sl3])])
                                if sl3 < 2:
                                    e.wait_ge(sem_v, div_prev[(pg[0], pg[1], sl3)])
                        if kc == 0 and g == 0:
                            e.wait_ge(s_qc, qc_a[qc])
                        qb = (qc % 2) * 2048
                        st = k_sb[0:64, g * S + kc * 128: g * S + kc * 128 + 128]
                        mv = qc_t[0:64, qb + g * QC: qb + (g + 1) * QC]
                        e.matmul(pars[:, sl3 * 1024: sl3 * 1024 + QC], st, mv, start=True, stop=True,
                                 tile_position=(0, 0))
                        st = k_sb[64:128, g * S + kc * 128: g * S + kc * 128 + 128]
                        mv = qc_t[64:128, qb + g * QC: qb + (g + 1) * QC]
                        e.matmul(pars[:, sl3 * 1024 + 512: sl3 * 1024 + 512 + QC], st, mv, start=True, stop=True,
                                 tile_position=(64, 0)).then_inc(sem_t, 1)
                    qk_t[(qc, g, kc)] = bump(nt)

                def em_softmax(kc):
                    sl3 = kc % 3
                    psl = kc % 4

                    @on("v")
                    def _(e, qc=qc, g=g, kc=kc, sl3=sl3):
                        e.wait_ge(sem_t, qk_t[(qc, g, kc)])
                        if g == 0 and kc == 0:
                            e.wait_ge(s_dist, dist_a[qc])
                        sc2 = bass.AP(pars, sl3 * 1024, [[4096, 128], [512, 2], [1, QC]])
                        rep = bass.AP(dist_t, kc * QC, [[NKC * QC, 128], [0, 2], [1, QC]])
                        e.tensor_tensor(sc2, sc2, rep, OP.mult).then_inc(sem_v, 1)
                    mult_v[(qc, g, kc)] = bump(nv)

                    @on("s")
                    def _(e, qc=qc, g=g, kc=kc, sl3=sl3, psl=psl):
                        e.wait_ge(sem_v, mult_v[(qc, g, kc)])
                        if kc >= 4:
                            e.wait_ge(sem_t, av_t[(qc, g, kc - 4)])
                        elif g >= 1:
                            e.wait_ge(sem_t, av_t[(qc, g - 1, PREV4[kc])])
                        elif qc >= 1:
                            e.wait_ge(sem_t, av_t[(qc - 1, NHP - 1, PREV4[kc])])
                        sc2 = bass.AP(pars, sl3 * 1024, [[4096, 128], [512, 2], [1, QC]])
                        e.activation(probs_t[:, psl * 512:(psl + 1) * 512], sc2,
                                     AF.Exp, scale=1.0 / 32.0).then_inc(sem_s, 1)
                    exp_s[(qc, g, kc)] = bump(ns)

                def em_av(kc):
                    psl = kc % 4

                    @on("t")
                    def _(e, qc=qc, g=g, kc=kc, psl=psl, gi=gi, hdA=hdA, hdB=hdB):
                        e.wait_ge(sem_s, exp_s[(qc, g, kc)])
                        if kc == 0 and gi >= 1:
                            e.wait_ge(sem_v, epi_v[gi - 1])
                            e.wait_ge(sem_s, epi_s[gi - 1])
                        mm = None
                        for hx, hd in ((0, hdA), (1, hdB)):
                            st = v_sb[:, kc * VROW + hd * 65: kc * VROW + hd * 65 + 65]
                            mv = probs_t[:, psl * 512 + hx * 256: psl * 512 + hx * 256 + QC]
                            mm = e.matmul(pars[0:65, 3072 + hx * 512: 3072 + hx * 512 + QC], st, mv,
                                          start=(kc == 0), stop=(kc == NKC - 1))
                        mm.then_inc(sem_t, 1)
                    av_t[(qc, g, kc)] = bump(nt)

                for kc in range(NKC):
                    em_qk(kc)
                    em_softmax(kc)
                    if kc >= 1:
                        em_av(kc - 1)
                em_av(NKC - 1)

                ave_s = [0, 0]
                trm_t = [0, 0]
                div_v = [0, 0]
                for hx in range(2):
                    @on("s")
                    def _(e, hx=hx, w=av_t[(qc, g, NKC - 1)], gi=gi):
                        e.wait_ge(sem_t, w)
                        if KDBG and hx == 0 and gi >= 1:
                            e.wait_ge(s_dbg, gi * 16)
                        e.copy(avb_t[0:65, hx * 256:(hx + 1) * 256],
                               pars[0:65, 3072 + hx * 512: 3072 + hx * 512 + QC]).then_inc(sem_s, 1)
                    ave_s[hx] = bump(ns)

                if KDBG:
                    @on("sy")
                    def _(e, gi=gi, w=ave_s[1]):
                        e.wait_ge(sem_s, w)
                        e.dma_start(dbg_avb[gi, :, :], avb_t[:, :]).then_inc(s_dbg, 16)
                    bump(n_dbg, 16)

                for hx in range(2):
                    @on("t")
                    def _(e, hx=hx, w=ave_s[1]):
                        e.wait_ge(sem_s, w)
                        mm = None
                        for ts in range(2):
                            mm = e.matmul(pars[:, hx * 1024 + ts * 512: hx * 1024 + ts * 512 + 65],
                                          avb_t[0:65, hx * 256 + ts * 128: hx * 256 + ts * 128 + 128],
                                          id_sb[0:65, 0:65], start=True, stop=True)
                        mm.then_inc(sem_t, 1)
                    trm_t[hx] = bump(nt)

                    @on("v")
                    def _(e, g=g, hx=hx, w=trm_t[hx], wrc=nv[0] + 1):
                        e.wait_ge(sem_t, w)
                        mm = None
                        for ts in range(2):
                            c0 = hx * 1024 + ts * 512
                            mm = e.reciprocal(st2_t[:, 8 + ts:9 + ts], pars[:, c0 + 64: c0 + 65])
                        mm.then_inc(sem_v, 1)
                        e.wait_ge(sem_v, wrc)
                        mm = None
                        for ts in range(2):
                            c0 = hx * 1024 + ts * 512
                            hd = 2 * g + hx
                            mm = e.tensor_scalar(atm_t[:, ts * 1024 + hd * 64: ts * 1024 + hd * 64 + 64],
                                                 pars[:, c0: c0 + 64],
                                                 st2_t[:, 8 + ts:9 + ts], None, OP.mult)
                        mm.then_inc(sem_v, 1)
                    bump(nv)
                    div_v[hx] = bump(nv)
                    div_prev[(qc, g, hx)] = div_v[hx]

                epi_v[gi] = div_v[1]
                epi_s[gi] = ave_s[1]

            if KDBG:
                @on("sy")
                def _(e, qc=qc, w=epi_v[qc * NHP + NHP - 1], wq2=qk_t[(qc, NHP - 1, NKC - 1)]):
                    e.wait_ge(sem_v, w)
                    e.wait_ge(sem_t, wq2)
                    e.dma_start(dbg_atm[qc, :, :], atm_t[:, :]).then_inc(s_dbg, 16)
                    e.dma_start(dbg_qct[qc, :, :], qc_t[:, :]).then_inc(s_dbg, 16)
                bump(n_dbg, 32)

            ln2sub = [0, 0]
            for ts in range(2):
                @on("v")
                def _(e, qc=qc, ts=ts):
                    if qc == 0 and ts == 0:
                        e.wait_ge(s_c0, p0_a)
                    e.wait_ge(s_xrP[qc % 2][0], xr_a[qc])
                    src = atm_t[:, ts * 1024:(ts + 1) * 1024]
                    vec_lnstats(e, src, sq2_t[:, :], st2_t[:, 0:1],
                                st2_t[:, 2:3], st2_t[:, 1:2]).then_inc(sem_v, 1)
                st2v = bump(nv)

                @on("s")
                def _(e, w=st2v):
                    e.wait_ge(sem_v, w)
                    scl_rstd(e, st2_t[:, 1:2], st2_t[:, 3:4]).then_inc(sem_s, 1)
                st2s = bump(ns)

                @on("v")
                def _(e, qc=qc, qsl=qsl, ts=ts, w=st2s):
                    e.wait_ge(sem_s, w)
                    src = atm_t[:, ts * 1024:(ts + 1) * 1024]
                    xbuf = xr_t if ts == 0 else x1_t
                    xap = xbuf[:, qsl * E:(qsl + 1) * E]
                    e.tensor_scalar(t1_t[:, :], src, st2_t[:, 2:3], st2_t[:, 3:4], OP.subtract, OP.mult)
                    e.tensor_tensor(t1_t[:, :], t1_t[:, :], gb_sb[:, :], OP.mult)
                    e.tensor_tensor(t1_t[:, :], t1_t[:, :], bb_sb[:, :], OP.add)
                    e.tensor_tensor(xap, t1_t[:, :], xap, OP.add)
                    ti = qc * 2 + ts
                    vec_lnstats(e, xap, sq2_t[:, :], st2_t[:, 4:5],
                                st3_sb[:, ti * 2: ti * 2 + 1], st2_t[:, 5:6]).then_inc(sem_v, 1)
                st3v = bump(nv)
                ln2sub[ts] = st3v

                @on("s")
                def _(e, qc=qc, ts=ts, w=st3v):
                    e.wait_ge(sem_v, w)
                    ti = qc * 2 + ts
                    scl_rstd(e, st2_t[:, 5:6], st3_sb[:, ti * 2 + 1: ti * 2 + 2]).then_inc(sem_s, 1)
                bump(ns)

                @on("sy")
                def _(e, qc=qc, qsl=qsl, ts=ts, w=st3v):
                    e.wait_ge(sem_v, w)
                    xbuf = xr_t if ts == 0 else x1_t
                    e.dma_start(x1h[qc * QC + ts * 128: qc * QC + (ts + 1) * 128, :],
                                xbuf[:, qsl * E:(qsl + 1) * E]).then_inc(s_xsP[qc % 2][0], 16)
                x1st_o[qc] = bump(s_xsP[qc % 2][1], 16)
            ln2_v[qc] = ln2sub[1]

        bar3_t = nt[0]
        bar3_v = nv[0]
        bar3_s = ns[0]

        if KDBG:
            @on("sy")
            def _(e):
                e.wait_ge(sem_t, bar3_t)
                e.wait_ge(sem_v, bar3_v)
                e.wait_ge(sem_s, bar3_s)
                e.dma_start(dbg_k[:, :], k_sb[:, :]).then_inc(s_dbg, 16)
                e.dma_start(dbg_v[:, :], v_sb[:, :]).then_inc(s_dbg, 16)
                e.dma_start(dbg_xlh[:, :, :], xlh[:, :, :]).then_inc(s_dbg, 16)
                e.dma_start(dbg_qh[:, :, :], qh[:, :, :]).then_inc(s_dbg, 16)
                e.dma_start(dbg_x1[:, :], x1h[:, :]).then_inc(s_dbg, 16)
            bump(n_dbg, 80)
        phase_tot[3] = {k: c[0] for k, (s, c) in streams.items()}
        cur_phase[0] = 3

        s2.close()
        skv.close()

        s3 = ExitStack()
        A = s3.enter_context
        wgT = A(nc.sbuf_tensor("wgT", [128, NCH * HID], BF16, side="right"))
        wuT = A(nc.sbuf_tensor("wuT", [128, NCH * HID], BF16, side="right"))
        wdT = A(nc.sbuf_tensor("wdT", [128, NHC * E], BF16, side="right"))
        x13_t = A(nc.sbuf_tensor("x13_t", [128, 2 * E], F32, side="right"))
        xl3_t = A(nc.sbuf_tensor("xl3_t", [128, 2 * E], BF16, side="right"))
        xf3_t = A(nc.sbuf_tensor("xf3_t", [128, 2 * 2048], BF16, side="right"))
        act_t = A(nc.sbuf_tensor("act_t", [128, 2 * NHC * 256], BF16, side="right"))
        ssg_t = A(nc.sbuf_tensor("ssg_t", [128, 256], F32, side="right"))
        y_t = A(nc.sbuf_tensor("y_t", [128, 2 * E], F32, side="right"))
        ptr3 = A(nc.psum_tensor("ptr3", [128, 1024], F32))
        pg3 = [A(nc.psum_tensor(f"pg3{p}", [128, 256], F32)) for p in range(2)]
        pu3 = [A(nc.psum_tensor(f"pu3{p}", [128, 256], F32)) for p in range(2)]
        pd3 = [A(nc.psum_tensor(f"pd3{p}", [128, 512], F32)) for p in range(2)]

        @on("sy")
        def _(e):
            e.wait_ge(sem_t, bar3_t)
            e.wait_ge(sem_v, bar3_v)
            e.wait_ge(sem_s, bar3_s)
            e.wait_ge(s_castf, ffn_cast)
            for fi in range(NCH):
                e.dma_start_transpose(wgT[:, fi * HID:(fi + 1) * HID], wgb[:, fi * 128:(fi + 1) * 128]).then_inc(s_w3, 16)
            for fi in range(NCH):
                e.dma_start_transpose(wuT[:, fi * HID:(fi + 1) * HID], wub[:, fi * 128:(fi + 1) * 128]).then_inc(s_w3, 16)
            for hc in range(NHC):
                e.dma_start_transpose(wdT[:, hc * E:(hc + 1) * E], wdb[:, hc * 128:(hc + 1) * 128]).then_inc(s_w3, 16)
        wgu_a = bump(n_w3, (2 * NCH + NHC) * 16)
        wd_a = wgu_a

        x13_a = [0] * NTC3
        ln3_v = [[0, 0] for _ in range(NTC3)]
        tr3_t = {}
        fm3_s = {}
        gmm_t, umm_t, sil_s, act_v = {}, {}, {}, {}
        dmm_t, dev_v = {}, {}
        yst_o = [0] * NTC3

        for tc in range(NTC3):
            tsl = tc % 2

            @on("sy")
            def _(e, tc=tc):
                if tc == 0:
                    e.wait_ge(sem_t, bar3_t)
                    e.wait_ge(sem_v, bar3_v)
                    e.wait_ge(sem_s, bar3_s)
                e.wait_ge(s_xsP[tc % 2][0], x1st_o[tc])
                if tc >= 1:
                    e.wait_ge(sem_v, dev_v[(tc - 1, 1, 1)])
                e.dma_start(x13_t[:, 0:E], x1h[tc * 256:tc * 256 + 128, :]).then_inc(s_x13, 16)
                e.dma_start(x13_t[:, E:2 * E], x1h[tc * 256 + 128:tc * 256 + 256, :]).then_inc(s_x13, 16)
            x13_a[tc] = bump(n_x13, 32)

            for ts in range(2):
                @on("v")
                def _(e, tc=tc, ts=ts):
                    e.wait_ge(s_x13, x13_a[tc])
                    if tc >= 1:
                        e.wait_ge(sem_t, tr3_t[(tc - 1, ts, 3)])
                    ti = tc * 2 + ts
                    e.tensor_scalar(xl3_t[:, ts * E:(ts + 1) * E], x13_t[:, ts * E:(ts + 1) * E],
                                    st3_sb[:, ti * 2:ti * 2 + 1], st3_sb[:, ti * 2 + 1:ti * 2 + 2],
                                    OP.subtract, OP.mult).then_inc(sem_v, 1)
                ln3_v[tc][ts] = bump(nv)

                for r in range(4):
                    @on("t")
                    def _(e, tc=tc, ts=ts, r=r):
                        if r == 0:
                            e.wait_ge(sem_v, ln3_v[tc][ts])
                            if tc >= 1 or ts == 1:
                                pkey = (tc, 0, 3) if ts == 1 else (tc - 1, 1, 3)
                                e.wait_ge(sem_s, fm3_s[pkey])
                        else:
                            e.wait_ge(sem_s, fm3_s[(tc, ts, r - 1)])
                        mm = None
                        for j in (2 * r, 2 * r + 1):
                            mm = e.matmul(ptr3[:, (j % 2) * 512:(j % 2) * 512 + 128],
                                          xl3_t[:, ts * E + j * 128: ts * E + (j + 1) * 128],
                                          id_sb[:, :], start=True, stop=True)
                        mm.then_inc(sem_t, 1)
                    tr3_t[(tc, ts, r)] = bump(nt)

                    @on("s")
                    def _(e, tc=tc, tsl=tsl, ts=ts, r=r):
                        e.wait_ge(sem_t, tr3_t[(tc, ts, r)])
                        if tc >= 2 and ts == 0 and r == 0:
                            e.wait_ge(sem_t, umm_t[(tc - 2, NHC - 1)])
                        d = bass.AP(xf3_t, tsl * 2048 + (2 * r) * 256 + ts * 128,
                                    [[2 * 2048, 128], [256, 2], [1, 128]])
                        s = ptr3[:, :].rearrange("p (c t) -> p c t", t=512)[:, :, 0:128]
                        e.copy(d, s).then_inc(sem_s, 1)
                    fm3_s[(tc, ts, r)] = bump(ns)

            for hc in range(NHC):
                pp = hc % 2

                @on("t")
                def _(e, tc=tc, tsl=tsl, hc=hc, pp=pp):
                    if hc == 0:
                        e.wait_ge(sem_s, fm3_s[(tc, 1, 3)])
                        if tc == 0:
                            e.wait_ge(s_w3, wgu_a)
                        if tc >= 2:
                            e.wait_ge(sem_t, dmm_t[(tc - 2, 1, 1)])
                    if hc >= 2:
                        e.wait_ge(sem_s, sil_s[(tc, hc - 2)])
                    xf = xf3_t[:, tsl * 2048:(tsl + 1) * 2048]
                    mm = None
                    for fi in range(NCH):
                        mm = e.matmul(pg3[pp][:, :],
                                      wgT[:, fi * HID + hc * 128: fi * HID + (hc + 1) * 128],
                                      xf[:, fi * 256:(fi + 1) * 256],
                                      start=(fi == 0), stop=(fi == NCH - 1))
                    mm.then_inc(sem_t, 1)
                gmm_t[(tc, hc)] = bump(nt)

                @on("t")
                def _(e, tc=tc, tsl=tsl, hc=hc, pp=pp):
                    if hc >= 2:
                        e.wait_ge(sem_v, act_v[(tc, hc - 2)])
                    xf = xf3_t[:, tsl * 2048:(tsl + 1) * 2048]
                    mm = None
                    for fi in range(NCH):
                        mm = e.matmul(pu3[pp][:, :],
                                      wuT[:, fi * HID + hc * 128: fi * HID + (hc + 1) * 128],
                                      xf[:, fi * 256:(fi + 1) * 256],
                                      start=(fi == 0), stop=(fi == NCH - 1))
                    mm.then_inc(sem_t, 1)
                umm_t[(tc, hc)] = bump(nt)

                @on("s")
                def _(e, tc=tc, hc=hc, pp=pp):
                    e.wait_ge(sem_t, gmm_t[(tc, hc)])
                    if hc >= 1:
                        e.wait_ge(sem_v, act_v[(tc, hc - 1)])
                    e.activation(ssg_t[:, :], pg3[pp][:, :], AF.Silu, scale=SC127).then_inc(sem_s, 1)
                sil_s[(tc, hc)] = bump(ns)

                @on("v")
                def _(e, tc=tc, hc=hc, pp=pp):
                    e.wait_ge(sem_s, sil_s[(tc, hc)])
                    e.wait_ge(sem_t, umm_t[(tc, hc)])
                    e.tensor_tensor(act_t[:, (tc % 2) * NHC * 256 + hc * 256: (tc % 2) * NHC * 256 + (hc + 1) * 256],
                                    ssg_t[:, :], pu3[pp][:, :], OP.mult).then_inc(sem_v, 1)
                act_v[(tc, hc)] = bump(nv)

            for ts in range(2):
                for fb in range(2):
                    pp = fb

                    @on("t")
                    def _(e, tc=tc, ts=ts, fb=fb, pp=pp):
                        if ts == 0 and fb == 0:
                            e.wait_ge(sem_v, act_v[(tc, NHC - 1)])
                            if tc == 0:
                                e.wait_ge(s_w3, wd_a)
                        prev = (tc, ts - 1, fb) if ts >= 1 else (tc - 1, 1, fb)
                        if prev in dev_v:
                            e.wait_ge(sem_v, dev_v[prev])
                        mm = None
                        for hc in range(NHC):
                            mm = e.matmul(pd3[pp][:, :],
                                          act_t[:, (tc % 2) * NHC * 256 + hc * 256 + ts * 128: (tc % 2) * NHC * 256 + hc * 256 + ts * 128 + 128],
                                          wdT[:, hc * E + fb * 512: hc * E + fb * 512 + 512],
                                          start=(hc == 0), stop=(hc == NHC - 1))
                        mm.then_inc(sem_t, 1)
                    dmm_t[(tc, ts, fb)] = bump(nt)

                    @on("v")
                    def _(e, tc=tc, ts=ts, fb=fb, pp=pp):
                        e.wait_ge(sem_t, dmm_t[(tc, ts, fb)])
                        if fb == 0 and ts == 0 and tc >= 1:
                            e.wait_ge(s_y, yst_o[tc - 1])
                        e.scalar_tensor_tensor(y_t[:, ts * E + fb * 512: ts * E + fb * 512 + 512],
                                               pd3[pp][:, :], SC2,
                                               x13_t[:, ts * E + fb * 512: ts * E + fb * 512 + 512],
                                               OP.mult, OP.add).then_inc(sem_v, 1)
                    dev_v[(tc, ts, fb)] = bump(nv)

                @on("sy")
                def _(e, tc=tc, ts=ts):
                    e.wait_ge(sem_v, dev_v[(tc, ts, 1)])
                    e.dma_start(y[tc * 256 + ts * 128: tc * 256 + (ts + 1) * 128, :],
                                y_t[:, ts * E:(ts + 1) * E]).then_inc(s_y, 16)
                yst_o[tc] = bump(n_y, 16)

        @on("sy")
        def _(e, wy=n_y[0], wd2=n_dbg[0]):
            e.wait_ge(s_y, wy)
            if wd2:
                e.wait_ge(s_dbg, wd2)

        s3.close()

        phase_tot[9] = {k: c[0] for k, (s, c) in streams.items()}
        blk = Eo(nc.Block())
        keys = {"t": blk.tensor, "v": blk.vector, "s": blk.scalar, "sy": blk.sync, "g": blk.gpsimd}
        trunc = phase_tot.get(PH) if (PH in phase_tot and PH < 9) else None
        for key in ["sy", "g", "t", "v", "s"]:
            fns = [fn for k, fn, ph in sched if k == key and (PH >= 9 or ph < PH)]
            if trunc is not None and key == "sy":
                eng_snap = trunc.pop("__eng__", None)

                def fin(e, snap=dict(trunc)):
                    for nm, val in snap.items():
                        if val:
                            e.wait_ge(streams[nm][0], val)
                fns = fns + [fin]

            def run_all(e, fns=fns):
                for fn in fns:
                    fn(e)
            keys[key](run_all)

    return nc


_CACHE = {}


def _get_nc():
    if "nc" not in _CACHE:
        _CACHE["nc"] = build_program()
    return _CACHE["nc"]


def kernel(L0_attn_x, L0_attn_Wq, L0_attn_Wk, L0_attn_Wv, L0_attn_ln_gamma,
           L0_attn_ln_beta, L0_ffn_w_gate, L0_ffn_w_up, L0_ffn_w_down):
    x = np.ascontiguousarray(np.asarray(L0_attn_x, dtype=np.float32))
    wq = np.ascontiguousarray(np.asarray(L0_attn_Wq, dtype=np.float32))
    wk = np.ascontiguousarray(np.asarray(L0_attn_Wk, dtype=np.float32))
    wv = np.ascontiguousarray(np.asarray(L0_attn_Wv, dtype=np.float32))
    g = np.asarray(L0_attn_ln_gamma, dtype=np.float32)
    be = np.asarray(L0_attn_ln_beta, dtype=np.float32)
    wg = np.ascontiguousarray(np.asarray(L0_ffn_w_gate, dtype=np.int32))
    wu = np.ascontiguousarray(np.asarray(L0_ffn_w_up, dtype=np.int32))
    wd = np.ascontiguousarray(np.asarray(L0_ffn_w_down, dtype=np.int32))

    gbc = np.ascontiguousarray(np.broadcast_to(g[None, :], (128, E)))
    bbc = np.ascontiguousarray(np.broadcast_to(be[None, :], (128, E)))
    idn = np.eye(128, dtype=bf16)
    cstv = np.zeros((128, 8), np.float32)
    cstv[:, 0] = EPS
    cstv[:, 1] = -0.5
    cstv[:, 2] = 1.0 / 32.0
    cstv[:, 3] = SC127
    cstv[:, 4] = -1.0

    idx = np.arange(S, dtype=np.float64)
    in_maps = []
    for c in range(8):
        b, hq = divmod(c, 2)
        qlo = hq * QT
        perm = np.concatenate([np.arange(qlo, qlo + QT), np.arange(0, qlo),
                               np.arange(qlo + QT, S)])
        xp = np.ascontiguousarray(x[b][perm])
        dmat = np.abs(idx[perm][:, None] - idx[None, qlo:qlo + QT]) / S
        in_maps.append(dict(
            xs=xp, wqi=wq, wki=wk, wvi=wv, gbc=gbc, bbc=bbc,
            wgi=wg, wui=wu, wdi=wd,
            dst=np.ascontiguousarray(dmat.astype(np.float32)),
            idn=idn, cst=cstv,
        ))

    nc = _get_nc()
    res = run_bass_kernel_spmd(nc, in_maps, core_ids=list(range(8)))
    out = np.empty((B, S, E), dtype=np.float32)
    for c in range(8):
        b, hq = divmod(c, 2)
        out[b, hq * QT:(hq + 1) * QT, :] = res.results[c]["y"]
    return out


if __name__ == "__main__":
    nc = build_program()
    print("built ok; instructions:", nc.n_instructions())


# version 06
# speedup vs baseline: 1.2369x; 1.2369x over previous
import os
import numpy as np
import ml_dtypes
from contextlib import ExitStack

import concourse.bass as bass
import concourse.mybir as mybir
from concourse.bass_utils import run_bass_kernel_spmd

F32 = mybir.dt.float32
BF16 = mybir.dt.bfloat16
I32 = mybir.dt.int32
AF = mybir.ActivationFunctionType
OP = mybir.AluOpType
AX = mybir.AxisListType.X
bf16 = ml_dtypes.bfloat16

B, S, E, H = 4, 4096, 1024, 16
HID = 3072
QT = 2048
NCH = E // 128
EPS = 1e-5
SC127 = float(np.float32(1.0) / np.float32(127.0))
SC2 = float(np.float32(SC127) * np.float32(SC127))

QC = 256
NQC = QT // QC
NKC = S // 128
NHP = H // 2
NT1 = int(os.environ.get("KNT1", S // 128))
NOWN = min(QT // 128, NT1)
NTC3 = QT // 256
NHC = HID // 128
VROW = H * 65


def build_program():
    nc = bass.Bass("TRN2", target_bir_lowering=False, detect_race_conditions=False,
                   dynamic_dma_scratch_size=4096)

    xs = nc.dram_tensor("xs", [S, E], F32, kind="ExternalInput")
    wqi = nc.dram_tensor("wqi", [E, E], F32, kind="ExternalInput")
    wki = nc.dram_tensor("wki", [E, E], F32, kind="ExternalInput")
    wvi = nc.dram_tensor("wvi", [E, E], F32, kind="ExternalInput")
    gbc = nc.dram_tensor("gbc", [128, E], F32, kind="ExternalInput")
    bbc = nc.dram_tensor("bbc", [128, E], F32, kind="ExternalInput")
    wgi = nc.dram_tensor("wgi", [HID, E], I32, kind="ExternalInput")
    wui = nc.dram_tensor("wui", [HID, E], I32, kind="ExternalInput")
    wdi = nc.dram_tensor("wdi", [E, HID], I32, kind="ExternalInput")
    dst = nc.dram_tensor("dst", [S, QT], F32, kind="ExternalInput")
    idn = nc.dram_tensor("idn", [128, 128], BF16, kind="ExternalInput")
    cst = nc.dram_tensor("cst", [128, 8], F32, kind="ExternalInput")
    y = nc.dram_tensor("y", [QT, E], F32, kind="ExternalOutput")

    wqb = nc.dram_tensor("wqb", [E, E], BF16)
    wkb = nc.dram_tensor("wkb", [E, E], BF16)
    wvb = nc.dram_tensor("wvb", [E, E], BF16)
    wgb = nc.dram_tensor("wgb", [HID, E], BF16)
    wub = nc.dram_tensor("wub", [HID, E], BF16)
    wdb = nc.dram_tensor("wdb", [E, HID], BF16)
    qh = nc.dram_tensor("qh", [NCH, 128, QT], BF16)
    xlh = nc.dram_tensor("xlh", [NCH, 128, S], BF16)
    x1h = nc.dram_tensor("x1h", [QT, E], F32)

    PH = float(os.environ.get("KPHASES", "9"))
    KDBG = int(os.environ.get("KDBG", "0"))
    if KDBG:
        dbg_avb = nc.dram_tensor("dbg_avb", [NQC * NHP, 128, 512], BF16, kind="ExternalOutput")
        dbg_qct = nc.dram_tensor("dbg_qct", [NQC, 128, 2048], BF16, kind="ExternalOutput")
        dbg_atm = nc.dram_tensor("dbg_atm", [NQC, 128, 2048], BF16, kind="ExternalOutput")
        dbg_xlh = nc.dram_tensor("dbg_xlh", [NCH, 128, S], BF16, kind="ExternalOutput")
        dbg_qh = nc.dram_tensor("dbg_qh", [NCH, 128, QT], BF16, kind="ExternalOutput")
        dbg_k = nc.dram_tensor("dbg_k", [128, NCH * S], BF16, kind="ExternalOutput")
        dbg_v = nc.dram_tensor("dbg_v", [128, NKC * VROW], BF16, kind="ExternalOutput")
        dbg_x1 = nc.dram_tensor("dbg_x1", [QT, E], F32, kind="ExternalOutput")
    KSUB = int(os.environ.get("KSUB", "9"))
    KSER = int(os.environ.get("KSER", "0"))
    KWT = int(os.environ.get("KWT", "1"))
    cur_phase = [0]
    phase_tot = {}
    sched = []

    def on(key):
        def deco(fn):
            sched.append((key, fn, cur_phase[0]))
        return deco

    with ExitStack() as ctx:
        Eo = ctx.enter_context

        gb_sb = Eo(nc.sbuf_tensor("gb_sb", [128, E], F32, side="left"))
        bb_sb = Eo(nc.sbuf_tensor("bb_sb", [128, E], F32, side="left"))
        id_sb = Eo(nc.sbuf_tensor("id_sb", [128, 128], BF16, side="left"))
        cs_sb = Eo(nc.sbuf_tensor("cs_sb", [128, 8], F32, side="left"))
        st3_sb = Eo(nc.sbuf_tensor("st3_sb", [128, 64], F32, side="left"))
        skv = ExitStack()
        k_sb = skv.enter_context(nc.sbuf_tensor("k_sb", [128, NCH * S], BF16, side="left"))
        v_sb = skv.enter_context(nc.sbuf_tensor("v_sb", [128, NKC * VROW], BF16, side="left"))

        nc.const_aps.register(F32, EPS, cs_sb[:, 0:1])
        nc.const_aps.register(F32, -0.5, cs_sb[:, 1:2])
        nc.const_aps.register(F32, 1.0 / 32.0, cs_sb[:, 2:3])
        nc.const_aps.register(F32, SC127, cs_sb[:, 3:4])
        nc.const_aps.register(F32, -1.0, cs_sb[:, 4:5])

        sem_t = Eo(nc.semaphore("sem_t")); nt = [0]
        sem_v = Eo(nc.semaphore("sem_v")); nv = [0]
        sem_s = Eo(nc.semaphore("sem_s")); ns = [0]
        sem_g = Eo(nc.semaphore("sem_g")); ng = [0]

        streams = {}

        def mksem(name):
            sem = Eo(nc.semaphore(name))
            ctr = [0]
            streams[name] = (sem, ctr)
            return sem, ctr

        s_c0, n_c0 = mksem("s_c0")
        s_castq, n_castq = mksem("s_castq")
        s_castf, n_castf = mksem("s_castf")
        s_wqk, n_wqk = mksem("s_wqk")
        s_wv, n_wv = mksem("s_wv")
        s_w3, n_w3 = mksem("s_w3")
        s_x0, n_x0 = mksem("s_x0")
        s_x1l, n_x1l = mksem("s_x1l")
        s_xlh0, n_xlh0 = mksem("s_xlh0")
        s_xlh1, n_xlh1 = mksem("s_xlh1")
        s_qst, n_qst = mksem("s_qst")
        s_xfb0, n_xfb0 = mksem("s_xfb0")
        s_xfb1, n_xfb1 = mksem("s_xfb1")
        s_qc, n_qc = mksem("s_qc")
        s_dist, n_dist = mksem("s_dist")
        s_xr0, n_xr0 = mksem("s_xr0")
        s_xr1, n_xr1 = mksem("s_xr1")
        s_xs0, n_xs0 = mksem("s_xs0")
        s_xs1, n_xs1 = mksem("s_xs1")
        s_x13, n_x13 = mksem("s_x13")
        s_y, n_y = mksem("s_y")
        s_dbg, n_dbg = mksem("s_dbg")
        s_xP = [(s_x0, n_x0), (s_x1l, n_x1l)]
        s_xlhP = [(s_xlh0, n_xlh0), (s_xlh1, n_xlh1)]
        s_xfbP = [(s_xfb0, n_xfb0), (s_xfb1, n_xfb1)]
        s_xrP = [(s_xr0, n_xr0), (s_xr1, n_xr1)]
        s_xsP = [(s_xs0, n_xs0), (s_xs1, n_xs1)]

        def bump(c, k=1):
            c[0] += k
            return c[0]

        def vec_lnstats(e, src, sq, s1, m_ap, s2_ap):
            e.reduce_sum(s1, src, axis=AX)
            e.tensor_tensor(sq, src, src, OP.mult)
            e.reduce_sum(s2_ap, sq, axis=AX)
            e.tensor_scalar(m_ap, s1, 1.0 / E, None, OP.mult)
            e.tensor_scalar(s2_ap, s2_ap, 1.0 / E, None, OP.mult)
            return e.scalar_tensor_tensor(s2_ap, m_ap, m_ap, s2_ap, OP.mult, OP.subtract)

        def scl_rstd(e, var_neg, r_ap):
            e.activation(r_ap, var_neg, AF.Ln, scale=-1.0, bias=EPS)
            return e.activation(r_ap, r_ap, AF.Exp, scale=-0.5)

        @on("sy")
        def _(e):
            e.dma_start(cs_sb[:, :], cst[:, :]).then_inc(s_c0, 16)
            e.dma_start(id_sb[:, :], idn[:, :]).then_inc(s_c0, 16)
            e.dma_start(gb_sb[:, :], gbc[:, :]).then_inc(s_c0, 16)
            e.dma_start(bb_sb[:, :], bbc[:, :]).then_inc(s_c0, 16)
        p0_a = bump(n_c0, 64)

        @on("g")
        def _(e):
            e.memset(v_sb[:, :], 1.0).then_inc(sem_g, 1)
            e.dma_start(wqb[:, :], wqi[:, :]).then_inc(s_castq, 16)
            e.dma_start(wkb[:, :], wki[:, :]).then_inc(s_castq, 16)
            e.dma_start(wvb[:, :], wvi[:, :]).then_inc(s_castq, 16)
            e.dma_start(wgb[:, :], wgi[:, :]).then_inc(s_castf, 16)
            e.dma_start(wub[:, :], wui[:, :]).then_inc(s_castf, 16)
            e.dma_start(wdb[:, :], wdi[:, :]).then_inc(s_castf, 16)
        memset_g = bump(ng, 1)
        qkv_cast = bump(n_castq, 48)
        ffn_cast = bump(n_castf, 48)
        phase_tot[0.5] = {k: c[0] for k, (s, c) in streams.items()}
        cur_phase[0] = 0.5

        s1a = ExitStack()
        A = s1a.enter_context
        wqT = A(nc.sbuf_tensor("wqT", [128, NCH * E], BF16, side="right"))
        wkT = A(nc.sbuf_tensor("wkT", [128, NCH * E], BF16, side="right"))
        x_t = A(nc.sbuf_tensor("x_t", [128, 2 * E], F32, side="right"))
        sq_t = A(nc.sbuf_tensor("sq_t", [128, E], BF16, side="right"))
        xln_t = A(nc.sbuf_tensor("xln_t", [128, 2 * E], BF16, side="right"))
        xfm_t = A(nc.sbuf_tensor("xfm_t", [128, 2 * 1024], BF16, side="right"))
        qev_t = A(nc.sbuf_tensor("qev_t", [128, 1024], BF16, side="right"))
        stat_t = A(nc.sbuf_tensor("stat_t", [128, 8], F32, side="right"))
        ptr = A(nc.psum_tensor("ptr", [128, 2048], F32))
        pk = A(nc.psum_tensor("pk", [128, 2048], F32))

        @on("sy")
        def _(e):
            if not KWT:
                return
            e.wait_ge(s_castq, qkv_cast)
            for fi in range(NCH):
                e.dma_start_transpose(wqT[:, fi * E:(fi + 1) * E], wqb[:, fi * 128:(fi + 1) * 128]).then_inc(s_wqk, 16)
            for fi in range(NCH):
                e.dma_start_transpose(wkT[:, fi * E:(fi + 1) * E], wkb[:, fi * 128:(fi + 1) * 128]).then_inc(s_wqk, 16)
        wqk_a = bump(n_wqk, 2 * NCH * 16) if KWT else 0

        x_a = [0] * NT1
        stats_v = [0] * NT1
        rstd_s = [0] * NT1
        apply_v = [0] * NT1
        tr_t = [[0, 0] for _ in range(NT1)]
        fm_s = [[0, 0] for _ in range(NT1)]
        xlh_o = [0] * NT1
        kmm_t = [[0, 0] for _ in range(NT1)]
        kev_s = [[0, 0] for _ in range(NT1)]
        qmm_t = [[0, 0] for _ in range(NT1)]
        qev_s = [[0, 0] for _ in range(NT1)]
        qst_o = [0] * NT1

        k3 = k_sb[:, :].rearrange("p (c s) -> p c s", s=S)
        pk3 = pk[:, :].rearrange("p (c t) -> p c t", t=512)
        ptr3v = ptr[:, :].rearrange("p (c t) -> p c t", t=512)

        for i in range(NT1):
            sl = i % 2
            own = i < NOWN

            @on("sy")
            def _(e, i=i, sl=sl):
                if i >= 2:
                    e.wait_ge(sem_v, apply_v[i - 2])
                e.dma_start(x_t[:, sl * E:(sl + 1) * E], xs[i * 128:(i + 1) * 128, :]).then_inc(s_xP[sl][0], 16)
            x_a[i] = bump(s_xP[sl][1], 16)

            @on("v")
            def _(e, i=i, sl=sl):
                e.wait_ge(s_xP[sl][0], x_a[i])
                if i >= 1:
                    e.wait_ge(sem_s, rstd_s[i - 1])
                vec_lnstats(e, x_t[:, sl * E:(sl + 1) * E], sq_t[:, :], stat_t[:, 0:1],
                            stat_t[:, 2:3], stat_t[:, 1:2]).then_inc(sem_v, 1)
            stats_v[i] = bump(nv)

            @on("s")
            def _(e, i=i):
                if i == 0:
                    e.wait_ge(s_c0, p0_a)
                e.wait_ge(sem_v, stats_v[i])
                scl_rstd(e, stat_t[:, 1:2], stat_t[:, 3:4]).then_inc(sem_s, 1)
            rstd_s[i] = bump(ns)

            @on("v")
            def _(e, i=i, sl=sl):
                e.wait_ge(sem_s, rstd_s[i])
                if i >= 2:
                    e.wait_ge(sem_t, tr_t[i - 2][1])
                e.tensor_scalar(xln_t[:, sl * E:(sl + 1) * E], x_t[:, sl * E:(sl + 1) * E],
                                stat_t[:, 2:3], stat_t[:, 3:4], OP.subtract, OP.mult).then_inc(sem_v, 1)
            apply_v[i] = bump(nv)

            if KSUB < 2:
                continue

            fsl = i % 2
            for r in range(2):
                @on("t")
                def _(e, i=i, sl=sl, r=r):
                    if r == 0:
                        if i == 0:
                            e.wait_ge(s_c0, p0_a)
                        e.wait_ge(sem_v, apply_v[i])
                        if i >= 1:
                            e.wait_ge(sem_s, fm_s[i - 1][1])
                    else:
                        e.wait_ge(sem_s, fm_s[i][0])
                    mm = None
                    for j in range(4 * r, 4 * r + 4):
                        mm = e.matmul(ptr[:, (j % 4) * 512:(j % 4) * 512 + 128],
                                      xln_t[:, sl * E + j * 128: sl * E + (j + 1) * 128],
                                      id_sb[:, :], start=True, stop=True)
                    mm.then_inc(sem_t, 1)
                tr_t[i][r] = bump(nt)

                @on("s")
                def _(e, i=i, fsl=fsl, r=r):
                    e.wait_ge(sem_t, tr_t[i][r])
                    if r == 0 and i >= 2:
                        e.wait_ge(sem_t, (qmm_t if i - 2 < NOWN else kmm_t)[i - 2][1])
                        e.wait_ge(s_xlhP[i % 2][0], xlh_o[i - 2])
                    e.copy(xfm_t[:, fsl * 1024 + r * 512: fsl * 1024 + (r + 1) * 512],
                           ptr3v[:, :, 0:128]).then_inc(sem_s, 1)
                fm_s[i][r] = bump(ns)

            @on("sy")
            def _(e, i=i, fsl=fsl):
                e.wait_ge(sem_s, fm_s[i][1])
                src = xfm_t[:, fsl * 1024:(fsl + 1) * 1024].rearrange("p (c t) -> p c t", c=NCH)
                d = xlh[:, :, i * 128:(i + 1) * 128].transpose([1, 0, 2])
                e.dma_start(d, src).then_inc(s_xlhP[i % 2][0], 16)
            xlh_o[i] = bump(s_xlhP[i % 2][1], 16)

            if KSUB < 3:
                continue

            for h in range(2):
                @on("t")
                def _(e, i=i, fsl=fsl, h=h):
                    if h == 0:
                        e.wait_ge(sem_s, fm_s[i][1])
                        e.wait_ge(s_wqk, wqk_a)
                        if i >= 1:
                            e.wait_ge(sem_s, (qev_s if i - 1 < NOWN else kev_s)[i - 1][1])
                    else:
                        e.wait_ge(sem_s, kev_s[i][0])
                    xf = xfm_t[:, fsl * 1024:(fsl + 1) * 1024]
                    mm = None
                    for fo in range(4 * h, 4 * h + 4):
                        for fi in range(NCH):
                            mm = e.matmul(pk[:, (fo % 4) * 512:(fo % 4) * 512 + 128],
                                          wkT[:, fi * E + fo * 128: fi * E + (fo + 1) * 128],
                                          xf[:, fi * 128:(fi + 1) * 128],
                                          start=(fi == 0), stop=(fi == NCH - 1))
                    mm.then_inc(sem_t, 1)
                kmm_t[i][h] = bump(nt)

                @on("s")
                def _(e, i=i, h=h):
                    e.wait_ge(sem_t, kmm_t[i][h])
                    d = k3[:, 4 * h:4 * h + 4, i * 128:(i + 1) * 128]
                    e.copy(d, pk3[:, :, 0:128]).then_inc(sem_s, 1)
                kev_s[i][h] = bump(ns)

            if own and KSUB >= 4:
                for h in range(2):
                    @on("t")
                    def _(e, i=i, fsl=fsl, h=h):
                        e.wait_ge(sem_s, kev_s[i][1] if h == 0 else qev_s[i][0])
                        xf = xfm_t[:, fsl * 1024:(fsl + 1) * 1024]
                        mm = None
                        for fo in range(4 * h, 4 * h + 4):
                            for fi in range(NCH):
                                mm = e.matmul(pk[:, (fo % 4) * 512:(fo % 4) * 512 + 128],
                                              wqT[:, fi * E + fo * 128: fi * E + (fo + 1) * 128],
                                              xf[:, fi * 128:(fi + 1) * 128],
                                              start=(fi == 0), stop=(fi == NCH - 1))
                        mm.then_inc(sem_t, 1)
                    qmm_t[i][h] = bump(nt)

                    @on("s")
                    def _(e, i=i, h=h):
                        e.wait_ge(sem_t, qmm_t[i][h])
                        if i >= 1 and h == 0:
                            e.wait_ge(s_qst, qst_o[i - 1])
                        e.copy(qev_t[:, h * 512:(h + 1) * 512].rearrange("p (c t) -> p c t", t=128),
                               pk3[:, :, 0:128]).then_inc(sem_s, 1)
                    qev_s[i][h] = bump(ns)

                @on("sy")
                def _(e, i=i):
                    e.wait_ge(sem_s, qev_s[i][1])
                    src = qev_t[:, :].rearrange("p (c t) -> p c t", c=NCH)
                    d = qh[:, :, i * 128:(i + 1) * 128].transpose([1, 0, 2])
                    e.dma_start(d, src).then_inc(s_qst, 16)
                qst_o[i] = bump(n_qst, 16)

        bar1_t = nt[0]
        bar1_v = nv[0]
        bar1_s = ns[0]
        phase_tot[1] = {k: c[0] for k, (s, c) in streams.items()}
        cur_phase[0] = 1

        s1a.close()

        s1b = ExitStack()
        A = s1b.enter_context
        wvT = A(nc.sbuf_tensor("wvT", [128, NCH * E], BF16, side="right"))
        xfb_t = A(nc.sbuf_tensor("xfb_t", [128, 2 * 1024], BF16, side="right"))
        pvA = A(nc.psum_tensor("pvA", [128, 512], F32))
        pvB = A(nc.psum_tensor("pvB", [128, 512], F32))

        v4 = v_sb[:, :].rearrange("p (k h d) -> p k h d", k=NKC, h=H)

        @on("sy")
        def _(e):
            e.wait_ge(sem_t, bar1_t)
            e.wait_ge(sem_v, bar1_v)
            e.wait_ge(sem_s, bar1_s)
            e.wait_ge(s_castq, qkv_cast)
            for fi in range(NCH):
                e.dma_start_transpose(wvT[:, fi * E:(fi + 1) * E], wvb[:, fi * 128:(fi + 1) * 128]).then_inc(s_wv, 16)
        wv_a = bump(n_wv, NCH * 16)

        xfb_a = [0] * NT1
        vmm_t = [[0, 0] for _ in range(NT1)]
        vev_s = [[0, 0] for _ in range(NT1)]

        for i in range(NT1):
            sl = i % 2

            @on("sy")
            def _(e, i=i, sl=sl):
                e.wait_ge(s_xlhP[i % 2][0], xlh_o[i])
                if i >= 2:
                    e.wait_ge(sem_t, vmm_t[i - 2][1])
                s = xlh[:, :, i * 128:(i + 1) * 128].transpose([1, 0, 2])
                e.dma_start(xfb_t[:, sl * 1024:(sl + 1) * 1024].rearrange("p (c t) -> p c t", c=NCH),
                            s).then_inc(s_xfbP[sl][0], 16)
            xfb_a[i] = bump(s_xfbP[sl][1], 16)

            for h in range(2):
                @on("t")
                def _(e, i=i, sl=sl, h=h):
                    if h == 0:
                        e.wait_ge(s_xfbP[sl][0], xfb_a[i])
                        e.wait_ge(s_wv, wv_a)
                    if i >= 1:
                        e.wait_ge(sem_s, vev_s[i - 1][h])
                    bank = pvA if h == 0 else pvB
                    mm = None
                    for fi in range(NCH):
                        mm = e.matmul(bank[:, :],
                                      xfb_t[:, sl * 1024 + fi * 128: sl * 1024 + (fi + 1) * 128],
                                      wvT[:, fi * E + h * 512: fi * E + h * 512 + 512],
                                      start=(fi == 0), stop=(fi == NCH - 1))
                    mm.then_inc(sem_t, 1)
                vmm_t[i][h] = bump(nt)

                @on("s")
                def _(e, i=i, h=h):
                    e.wait_ge(sem_t, vmm_t[i][h])
                    if i == 0 and h == 0:
                        e.wait_ge(sem_g, memset_g)
                    bank = pvA if h == 0 else pvB
                    d = v4[:, i, 8 * h:8 * h + 8, 0:64]
                    s = bank[:, :].rearrange("p (h d) -> p h d", d=64)
                    e.copy(d, s).then_inc(sem_s, 1)
                vev_s[i][h] = bump(ns)

        bar2_t = nt[0]
        bar2_v = nv[0]
        bar2_s = ns[0]
        phase_tot[2] = {k: c[0] for k, (s, c) in streams.items()}
        cur_phase[0] = 2

        s1b.close()

        s2 = ExitStack()
        A = s2.enter_context
        dist_t = A(nc.sbuf_tensor("dist_t", [128, NKC * QC], F32, side="right"))
        probs_t = A(nc.sbuf_tensor("probs_t", [128, 4 * 512], BF16, side="right"))
        qc_t = A(nc.sbuf_tensor("qc_t", [128, 2 * 2048], BF16, side="right"))
        atm_t = A(nc.sbuf_tensor("atm_t", [128, 2048], BF16, side="right"))
        avb_t = A(nc.sbuf_tensor("avb_t", [128, 512], BF16, side="right"))
        xr_t = A(nc.sbuf_tensor("xr_t", [128, 2 * E], F32, side="right"))
        x1_t = A(nc.sbuf_tensor("x1_t", [128, 2 * E], F32, side="right"))
        t1_t = A(nc.sbuf_tensor("t1_t", [128, E], F32, side="right"))
        sq2_t = A(nc.sbuf_tensor("sq2_t", [128, E], BF16, side="right"))
        st2_t = A(nc.sbuf_tensor("st2_t", [128, 16], F32, side="right"))
        pars = A(nc.psum_tensor("pars", [128, 4096], F32))

        qc_a = [0] * NQC
        dist_a = [0] * NQC
        qk_t, mult_v, exp_s, av_t, div_prev = {}, {}, {}, {}, {}
        epi_v = [0] * (NQC * NHP)
        epi_s = [0] * (NQC * NHP)
        ln2_v = [0] * NQC
        x1st_o = [0] * NQC
        xr_a = [0] * NQC
        PREV4 = {0: 28, 1: 29, 2: 30, 3: 31}

        for qc in range(NQC):
            qsl = qc % 2

            @on("sy")
            def _(e, qc=qc):
                if qc == 0:
                    e.wait_ge(sem_t, bar2_t)
                    e.wait_ge(sem_v, bar2_v)
                    e.wait_ge(sem_s, bar2_s)
                    e.wait_ge(s_qst, qst_o[NOWN - 1])
                if qc >= 2:
                    e.wait_ge(sem_t, qk_t[(qc - 2, NHP - 1, NKC - 1)])
                s = qh[:, :, qc * QC:(qc + 1) * QC].transpose([1, 0, 2])
                e.dma_start(qc_t[:, (qc % 2) * 2048:(qc % 2 + 1) * 2048].rearrange("p (c q) -> p c q", c=NCH),
                            s).then_inc(s_qc, 16)
            qc_a[qc] = bump(n_qc, 16)

            for dg in range(4):
                @on("sy")
                def _(e, qc=qc, dg=dg):
                    if qc >= 1:
                        e.wait_ge(sem_v, mult_v[(qc - 1, NHP - 1, dg * 8 + 7)])
                    s = dst[dg * 1024:(dg + 1) * 1024, qc * QC:(qc + 1) * QC]
                    s = s.rearrange("(c k) q -> c k q", k=128).transpose([1, 0, 2])
                    e.dma_start(dist_t[:, dg * 8 * QC:(dg + 1) * 8 * QC].rearrange("p (c q) -> p c q", c=8),
                                s).then_inc(s_dist, 16)
            dist_a[qc] = bump(n_dist, 64)

            @on("sy")
            def _(e, qc=qc, qsl=qsl):
                if qc >= 2:
                    e.wait_ge(sem_v, ln2_v[qc - 2])
                    e.wait_ge(s_xsP[qc % 2][0], x1st_o[qc - 2])
                e.dma_start(xr_t[:, qsl * E:(qsl + 1) * E], xs[qc * QC:qc * QC + 128, :]).then_inc(s_xrP[qsl][0], 16)
                e.dma_start(x1_t[:, qsl * E:(qsl + 1) * E], xs[qc * QC + 128:qc * QC + 256, :]).then_inc(s_xrP[qsl][0], 16)
            xr_a[qc] = bump(s_xrP[qsl][1], 32)

            for g in range(NHP):
                gi = qc * NHP + g
                hdA, hdB = 2 * g, 2 * g + 1

                SLOT_LAST = {0: 30, 1: 31, 2: 29}

                def em_qk(kc):
                    sl3 = kc % 3

                    @on("t")
                    def _(e, qc=qc, g=g, kc=kc, sl3=sl3):
                        if kc >= 3:
                            e.wait_ge(sem_s, exp_s[(qc, g, kc - 3)])
                        else:
                            pg = (qc, g - 1) if g >= 1 else ((qc - 1, NHP - 1) if qc >= 1 else None)
                            if pg is not None:
                                e.wait_ge(sem_s, exp_s[(pg[0], pg[1], SLOT_LAST[sl3])])
                                if sl3 < 2:
                                    e.wait_ge(sem_v, div_prev[(pg[0], pg[1], sl3)])
                        if kc == 0 and g == 0:
                            e.wait_ge(s_qc, qc_a[qc])
                        qb = (qc % 2) * 2048
                        st = k_sb[0:64, g * S + kc * 128: g * S + kc * 128 + 128]
                        mv = qc_t[0:64, qb + g * QC: qb + (g + 1) * QC]
                        e.matmul(pars[:, sl3 * 1024: sl3 * 1024 + QC], st, mv, start=True, stop=True,
                                 tile_position=(0, 0))
                        st = k_sb[64:128, g * S + kc * 128: g * S + kc * 128 + 128]
                        mv = qc_t[64:128, qb + g * QC: qb + (g + 1) * QC]
                        e.matmul(pars[:, sl3 * 1024 + 512: sl3 * 1024 + 512 + QC], st, mv, start=True, stop=True,
                                 tile_position=(64, 0)).then_inc(sem_t, 1)
                    qk_t[(qc, g, kc)] = bump(nt)

                def em_softmax(kc):
                    sl3 = kc % 3
                    psl = kc % 4

                    @on("v")
                    def _(e, qc=qc, g=g, kc=kc, sl3=sl3):
                        e.wait_ge(sem_t, qk_t[(qc, g, kc)])
                        if g == 0 and kc == 0:
                            e.wait_ge(s_dist, dist_a[qc])
                        sc2 = bass.AP(pars, sl3 * 1024, [[4096, 128], [512, 2], [1, QC]])
                        rep = bass.AP(dist_t, kc * QC, [[NKC * QC, 128], [0, 2], [1, QC]])
                        e.tensor_tensor(sc2, sc2, rep, OP.mult).then_inc(sem_v, 1)
                    mult_v[(qc, g, kc)] = bump(nv)

                    @on("s")
                    def _(e, qc=qc, g=g, kc=kc, sl3=sl3, psl=psl):
                        e.wait_ge(sem_v, mult_v[(qc, g, kc)])
                        if kc >= 4:
                            e.wait_ge(sem_t, av_t[(qc, g, kc - 4)])
                        elif g >= 1:
                            e.wait_ge(sem_t, av_t[(qc, g - 1, PREV4[kc])])
                        elif qc >= 1:
                            e.wait_ge(sem_t, av_t[(qc - 1, NHP - 1, PREV4[kc])])
                        sc2 = bass.AP(pars, sl3 * 1024, [[4096, 128], [512, 2], [1, QC]])
                        e.activation(probs_t[:, psl * 512:(psl + 1) * 512], sc2,
                                     AF.Exp, scale=1.0 / 32.0).then_inc(sem_s, 1)
                    exp_s[(qc, g, kc)] = bump(ns)

                def em_av(kc):
                    psl = kc % 4

                    @on("t")
                    def _(e, qc=qc, g=g, kc=kc, psl=psl, gi=gi, hdA=hdA, hdB=hdB):
                        e.wait_ge(sem_s, exp_s[(qc, g, kc)])
                        if kc == 0 and gi >= 1:
                            e.wait_ge(sem_v, epi_v[gi - 1])
                            e.wait_ge(sem_s, epi_s[gi - 1])
                        mm = None
                        for hx, hd in ((0, hdA), (1, hdB)):
                            st = v_sb[:, kc * VROW + hd * 65: kc * VROW + hd * 65 + 65]
                            mv = probs_t[:, psl * 512 + hx * 256: psl * 512 + hx * 256 + QC]
                            mm = e.matmul(pars[0:65, 3072 + hx * 512: 3072 + hx * 512 + QC], st, mv,
                                          start=(kc == 0), stop=(kc == NKC - 1))
                        mm.then_inc(sem_t, 1)
                    av_t[(qc, g, kc)] = bump(nt)

                for kc in range(NKC):
                    em_qk(kc)
                    em_softmax(kc)
                    if kc >= 3:
                        em_av(kc - 3)
                for kc in (NKC - 3, NKC - 2, NKC - 1):
                    em_av(kc)

                ave_s = [0, 0]
                trm_t = [0, 0]
                div_v = [0, 0]
                for hx in range(2):
                    @on("s")
                    def _(e, hx=hx, w=av_t[(qc, g, NKC - 1)], gi=gi):
                        e.wait_ge(sem_t, w)
                        if KDBG and hx == 0 and gi >= 1:
                            e.wait_ge(s_dbg, gi * 16)
                        e.copy(avb_t[0:65, hx * 256:(hx + 1) * 256],
                               pars[0:65, 3072 + hx * 512: 3072 + hx * 512 + QC]).then_inc(sem_s, 1)
                    ave_s[hx] = bump(ns)

                if KDBG:
                    @on("sy")
                    def _(e, gi=gi, w=ave_s[1]):
                        e.wait_ge(sem_s, w)
                        e.dma_start(dbg_avb[gi, :, :], avb_t[:, :]).then_inc(s_dbg, 16)
                    bump(n_dbg, 16)

                for hx in range(2):
                    @on("t")
                    def _(e, hx=hx, w=ave_s[1]):
                        e.wait_ge(sem_s, w)
                        mm = None
                        for ts in range(2):
                            mm = e.matmul(pars[:, hx * 1024 + ts * 512: hx * 1024 + ts * 512 + 65],
                                          avb_t[0:65, hx * 256 + ts * 128: hx * 256 + ts * 128 + 128],
                                          id_sb[0:65, 0:65], start=True, stop=True)
                        mm.then_inc(sem_t, 1)
                    trm_t[hx] = bump(nt)

                    @on("v")
                    def _(e, g=g, hx=hx, w=trm_t[hx], wrc=nv[0] + 1):
                        e.wait_ge(sem_t, w)
                        mm = None
                        for ts in range(2):
                            c0 = hx * 1024 + ts * 512
                            mm = e.reciprocal(st2_t[:, 8 + ts:9 + ts], pars[:, c0 + 64: c0 + 65])
                        mm.then_inc(sem_v, 1)
                        e.wait_ge(sem_v, wrc)
                        mm = None
                        for ts in range(2):
                            c0 = hx * 1024 + ts * 512
                            hd = 2 * g + hx
                            mm = e.tensor_scalar(atm_t[:, ts * 1024 + hd * 64: ts * 1024 + hd * 64 + 64],
                                                 pars[:, c0: c0 + 64],
                                                 st2_t[:, 8 + ts:9 + ts], None, OP.mult)
                        mm.then_inc(sem_v, 1)
                    bump(nv)
                    div_v[hx] = bump(nv)
                    div_prev[(qc, g, hx)] = div_v[hx]

                epi_v[gi] = div_v[1]
                epi_s[gi] = ave_s[1]

            if KDBG:
                @on("sy")
                def _(e, qc=qc, w=epi_v[qc * NHP + NHP - 1], wq2=qk_t[(qc, NHP - 1, NKC - 1)]):
                    e.wait_ge(sem_v, w)
                    e.wait_ge(sem_t, wq2)
                    e.dma_start(dbg_atm[qc, :, :], atm_t[:, :]).then_inc(s_dbg, 16)
                    e.dma_start(dbg_qct[qc, :, :], qc_t[:, :]).then_inc(s_dbg, 16)
                bump(n_dbg, 32)

            ln2sub = [0, 0]
            for ts in range(2):
                @on("v")
                def _(e, qc=qc, ts=ts):
                    if qc == 0 and ts == 0:
                        e.wait_ge(s_c0, p0_a)
                    e.wait_ge(s_xrP[qc % 2][0], xr_a[qc])
                    src = atm_t[:, ts * 1024:(ts + 1) * 1024]
                    vec_lnstats(e, src, sq2_t[:, :], st2_t[:, 0:1],
                                st2_t[:, 2:3], st2_t[:, 1:2]).then_inc(sem_v, 1)
                st2v = bump(nv)

                @on("s")
                def _(e, w=st2v):
                    e.wait_ge(sem_v, w)
                    scl_rstd(e, st2_t[:, 1:2], st2_t[:, 3:4]).then_inc(sem_s, 1)
                st2s = bump(ns)

                @on("v")
                def _(e, qc=qc, qsl=qsl, ts=ts, w=st2s):
                    e.wait_ge(sem_s, w)
                    src = atm_t[:, ts * 1024:(ts + 1) * 1024]
                    xbuf = xr_t if ts == 0 else x1_t
                    xap = xbuf[:, qsl * E:(qsl + 1) * E]
                    e.tensor_scalar(t1_t[:, :], src, st2_t[:, 2:3], st2_t[:, 3:4], OP.subtract, OP.mult)
                    e.tensor_tensor(t1_t[:, :], t1_t[:, :], gb_sb[:, :], OP.mult)
                    e.tensor_tensor(t1_t[:, :], t1_t[:, :], bb_sb[:, :], OP.add)
                    e.tensor_tensor(xap, t1_t[:, :], xap, OP.add)
                    ti = qc * 2 + ts
                    vec_lnstats(e, xap, sq2_t[:, :], st2_t[:, 4:5],
                                st3_sb[:, ti * 2: ti * 2 + 1], st2_t[:, 5:6]).then_inc(sem_v, 1)
                st3v = bump(nv)
                ln2sub[ts] = st3v

                @on("s")
                def _(e, qc=qc, ts=ts, w=st3v):
                    e.wait_ge(sem_v, w)
                    ti = qc * 2 + ts
                    scl_rstd(e, st2_t[:, 5:6], st3_sb[:, ti * 2 + 1: ti * 2 + 2]).then_inc(sem_s, 1)
                bump(ns)

                @on("sy")
                def _(e, qc=qc, qsl=qsl, ts=ts, w=st3v):
                    e.wait_ge(sem_v, w)
                    xbuf = xr_t if ts == 0 else x1_t
                    e.dma_start(x1h[qc * QC + ts * 128: qc * QC + (ts + 1) * 128, :],
                                xbuf[:, qsl * E:(qsl + 1) * E]).then_inc(s_xsP[qc % 2][0], 16)
                x1st_o[qc] = bump(s_xsP[qc % 2][1], 16)
            ln2_v[qc] = ln2sub[1]

        bar3_t = nt[0]
        bar3_v = nv[0]
        bar3_s = ns[0]

        if KDBG:
            @on("sy")
            def _(e):
                e.wait_ge(sem_t, bar3_t)
                e.wait_ge(sem_v, bar3_v)
                e.wait_ge(sem_s, bar3_s)
                e.dma_start(dbg_k[:, :], k_sb[:, :]).then_inc(s_dbg, 16)
                e.dma_start(dbg_v[:, :], v_sb[:, :]).then_inc(s_dbg, 16)
                e.dma_start(dbg_xlh[:, :, :], xlh[:, :, :]).then_inc(s_dbg, 16)
                e.dma_start(dbg_qh[:, :, :], qh[:, :, :]).then_inc(s_dbg, 16)
                e.dma_start(dbg_x1[:, :], x1h[:, :]).then_inc(s_dbg, 16)
            bump(n_dbg, 80)
        phase_tot[3] = {k: c[0] for k, (s, c) in streams.items()}
        cur_phase[0] = 3

        s2.close()
        skv.close()

        s3 = ExitStack()
        A = s3.enter_context
        wgT = A(nc.sbuf_tensor("wgT", [128, NCH * HID], BF16, side="right"))
        wuT = A(nc.sbuf_tensor("wuT", [128, NCH * HID], BF16, side="right"))
        wdT = A(nc.sbuf_tensor("wdT", [128, NHC * E], BF16, side="right"))
        x13_t = A(nc.sbuf_tensor("x13_t", [128, 2 * E], F32, side="right"))
        xl3_t = A(nc.sbuf_tensor("xl3_t", [128, 2 * E], BF16, side="right"))
        xf3_t = A(nc.sbuf_tensor("xf3_t", [128, 2 * 2048], BF16, side="right"))
        act_t = A(nc.sbuf_tensor("act_t", [128, 2 * NHC * 256], BF16, side="right"))
        ssg_t = A(nc.sbuf_tensor("ssg_t", [128, 256], F32, side="right"))
        y_t = A(nc.sbuf_tensor("y_t", [128, 2 * E], F32, side="right"))
        ptr3 = A(nc.psum_tensor("ptr3", [128, 1024], F32))
        pg3 = [A(nc.psum_tensor(f"pg3{p}", [128, 256], F32)) for p in range(2)]
        pu3 = [A(nc.psum_tensor(f"pu3{p}", [128, 256], F32)) for p in range(2)]
        pd3 = [A(nc.psum_tensor(f"pd3{p}", [128, 512], F32)) for p in range(2)]

        @on("sy")
        def _(e):
            e.wait_ge(sem_t, bar3_t)
            e.wait_ge(sem_v, bar3_v)
            e.wait_ge(sem_s, bar3_s)
            e.wait_ge(s_castf, ffn_cast)
            for fi in range(NCH):
                e.dma_start_transpose(wgT[:, fi * HID:(fi + 1) * HID], wgb[:, fi * 128:(fi + 1) * 128]).then_inc(s_w3, 16)
            for fi in range(NCH):
                e.dma_start_transpose(wuT[:, fi * HID:(fi + 1) * HID], wub[:, fi * 128:(fi + 1) * 128]).then_inc(s_w3, 16)
            for hc in range(NHC):
                e.dma_start_transpose(wdT[:, hc * E:(hc + 1) * E], wdb[:, hc * 128:(hc + 1) * 128]).then_inc(s_w3, 16)
        wgu_a = bump(n_w3, (2 * NCH + NHC) * 16)
        wd_a = wgu_a

        x13_a = [0] * NTC3
        ln3_v = [[0, 0] for _ in range(NTC3)]
        tr3_t = {}
        fm3_s = {}
        gmm_t, umm_t, sil_s, act_v = {}, {}, {}, {}
        dmm_t, dev_v = {}, {}
        yst_o = [0] * NTC3

        for tc in range(NTC3):
            tsl = tc % 2

            @on("sy")
            def _(e, tc=tc):
                if tc == 0:
                    e.wait_ge(sem_t, bar3_t)
                    e.wait_ge(sem_v, bar3_v)
                    e.wait_ge(sem_s, bar3_s)
                e.wait_ge(s_xsP[tc % 2][0], x1st_o[tc])
                if tc >= 1:
                    e.wait_ge(sem_v, dev_v[(tc - 1, 1, 1)])
                e.dma_start(x13_t[:, 0:E], x1h[tc * 256:tc * 256 + 128, :]).then_inc(s_x13, 16)
                e.dma_start(x13_t[:, E:2 * E], x1h[tc * 256 + 128:tc * 256 + 256, :]).then_inc(s_x13, 16)
            x13_a[tc] = bump(n_x13, 32)

            for ts in range(2):
                @on("v")
                def _(e, tc=tc, ts=ts):
                    e.wait_ge(s_x13, x13_a[tc])
                    if tc >= 1:
                        e.wait_ge(sem_t, tr3_t[(tc - 1, ts, 3)])
                    ti = tc * 2 + ts
                    e.tensor_scalar(xl3_t[:, ts * E:(ts + 1) * E], x13_t[:, ts * E:(ts + 1) * E],
                                    st3_sb[:, ti * 2:ti * 2 + 1], st3_sb[:, ti * 2 + 1:ti * 2 + 2],
                                    OP.subtract, OP.mult).then_inc(sem_v, 1)
                ln3_v[tc][ts] = bump(nv)

                for r in range(4):
                    @on("t")
                    def _(e, tc=tc, ts=ts, r=r):
                        if r == 0:
                            e.wait_ge(sem_v, ln3_v[tc][ts])
                            if tc >= 1 or ts == 1:
                                pkey = (tc, 0, 3) if ts == 1 else (tc - 1, 1, 3)
                                e.wait_ge(sem_s, fm3_s[pkey])
                        else:
                            e.wait_ge(sem_s, fm3_s[(tc, ts, r - 1)])
                        mm = None
                        for j in (2 * r, 2 * r + 1):
                            mm = e.matmul(ptr3[:, (j % 2) * 512:(j % 2) * 512 + 128],
                                          xl3_t[:, ts * E + j * 128: ts * E + (j + 1) * 128],
                                          id_sb[:, :], start=True, stop=True)
                        mm.then_inc(sem_t, 1)
                    tr3_t[(tc, ts, r)] = bump(nt)

                    @on("s")
                    def _(e, tc=tc, tsl=tsl, ts=ts, r=r):
                        e.wait_ge(sem_t, tr3_t[(tc, ts, r)])
                        if tc >= 2 and ts == 0 and r == 0:
                            e.wait_ge(sem_t, umm_t[(tc - 2, NHC - 1)])
                        d = bass.AP(xf3_t, tsl * 2048 + (2 * r) * 256 + ts * 128,
                                    [[2 * 2048, 128], [256, 2], [1, 128]])
                        s = ptr3[:, :].rearrange("p (c t) -> p c t", t=512)[:, :, 0:128]
                        e.copy(d, s).then_inc(sem_s, 1)
                    fm3_s[(tc, ts, r)] = bump(ns)

            for hc in range(NHC):
                pp = hc % 2

                @on("t")
                def _(e, tc=tc, tsl=tsl, hc=hc, pp=pp):
                    if hc == 0:
                        e.wait_ge(sem_s, fm3_s[(tc, 1, 3)])
                        if tc == 0:
                            e.wait_ge(s_w3, wgu_a)
                        if tc >= 2:
                            e.wait_ge(sem_t, dmm_t[(tc - 2, 1, 1)])
                    if hc >= 2:
                        e.wait_ge(sem_s, sil_s[(tc, hc - 2)])
                    xf = xf3_t[:, tsl * 2048:(tsl + 1) * 2048]
                    mm = None
                    for fi in range(NCH):
                        mm = e.matmul(pg3[pp][:, :],
                                      wgT[:, fi * HID + hc * 128: fi * HID + (hc + 1) * 128],
                                      xf[:, fi * 256:(fi + 1) * 256],
                                      start=(fi == 0), stop=(fi == NCH - 1))
                    mm.then_inc(sem_t, 1)
                gmm_t[(tc, hc)] = bump(nt)

                @on("t")
                def _(e, tc=tc, tsl=tsl, hc=hc, pp=pp):
                    if hc >= 2:
                        e.wait_ge(sem_v, act_v[(tc, hc - 2)])
                    xf = xf3_t[:, tsl * 2048:(tsl + 1) * 2048]
                    mm = None
                    for fi in range(NCH):
                        mm = e.matmul(pu3[pp][:, :],
                                      wuT[:, fi * HID + hc * 128: fi * HID + (hc + 1) * 128],
                                      xf[:, fi * 256:(fi + 1) * 256],
                                      start=(fi == 0), stop=(fi == NCH - 1))
                    mm.then_inc(sem_t, 1)
                umm_t[(tc, hc)] = bump(nt)

                @on("s")
                def _(e, tc=tc, hc=hc, pp=pp):
                    e.wait_ge(sem_t, gmm_t[(tc, hc)])
                    if hc >= 1:
                        e.wait_ge(sem_v, act_v[(tc, hc - 1)])
                    e.activation(ssg_t[:, :], pg3[pp][:, :], AF.Silu, scale=SC127).then_inc(sem_s, 1)
                sil_s[(tc, hc)] = bump(ns)

                @on("v")
                def _(e, tc=tc, hc=hc, pp=pp):
                    e.wait_ge(sem_s, sil_s[(tc, hc)])
                    e.wait_ge(sem_t, umm_t[(tc, hc)])
                    e.tensor_tensor(act_t[:, (tc % 2) * NHC * 256 + hc * 256: (tc % 2) * NHC * 256 + (hc + 1) * 256],
                                    ssg_t[:, :], pu3[pp][:, :], OP.mult).then_inc(sem_v, 1)
                act_v[(tc, hc)] = bump(nv)

            for ts in range(2):
                for fb in range(2):
                    pp = fb

                    @on("t")
                    def _(e, tc=tc, ts=ts, fb=fb, pp=pp):
                        if ts == 0 and fb == 0:
                            e.wait_ge(sem_v, act_v[(tc, NHC - 1)])
                            if tc == 0:
                                e.wait_ge(s_w3, wd_a)
                        prev = (tc, ts - 1, fb) if ts >= 1 else (tc - 1, 1, fb)
                        if prev in dev_v:
                            e.wait_ge(sem_v, dev_v[prev])
                        mm = None
                        for hc in range(NHC):
                            mm = e.matmul(pd3[pp][:, :],
                                          act_t[:, (tc % 2) * NHC * 256 + hc * 256 + ts * 128: (tc % 2) * NHC * 256 + hc * 256 + ts * 128 + 128],
                                          wdT[:, hc * E + fb * 512: hc * E + fb * 512 + 512],
                                          start=(hc == 0), stop=(hc == NHC - 1))
                        mm.then_inc(sem_t, 1)
                    dmm_t[(tc, ts, fb)] = bump(nt)

                    @on("v")
                    def _(e, tc=tc, ts=ts, fb=fb, pp=pp):
                        e.wait_ge(sem_t, dmm_t[(tc, ts, fb)])
                        if fb == 0 and ts == 0 and tc >= 1:
                            e.wait_ge(s_y, yst_o[tc - 1])
                        e.scalar_tensor_tensor(y_t[:, ts * E + fb * 512: ts * E + fb * 512 + 512],
                                               pd3[pp][:, :], SC2,
                                               x13_t[:, ts * E + fb * 512: ts * E + fb * 512 + 512],
                                               OP.mult, OP.add).then_inc(sem_v, 1)
                    dev_v[(tc, ts, fb)] = bump(nv)

                @on("sy")
                def _(e, tc=tc, ts=ts):
                    e.wait_ge(sem_v, dev_v[(tc, ts, 1)])
                    e.dma_start(y[tc * 256 + ts * 128: tc * 256 + (ts + 1) * 128, :],
                                y_t[:, ts * E:(ts + 1) * E]).then_inc(s_y, 16)
                yst_o[tc] = bump(n_y, 16)

        @on("sy")
        def _(e, wy=n_y[0], wd2=n_dbg[0]):
            e.wait_ge(s_y, wy)
            if wd2:
                e.wait_ge(s_dbg, wd2)

        s3.close()

        phase_tot[9] = {k: c[0] for k, (s, c) in streams.items()}
        blk = Eo(nc.Block())
        keys = {"t": blk.tensor, "v": blk.vector, "s": blk.scalar, "sy": blk.sync, "g": blk.gpsimd}
        trunc = phase_tot.get(PH) if (PH in phase_tot and PH < 9) else None
        for key in ["sy", "g", "t", "v", "s"]:
            fns = [fn for k, fn, ph in sched if k == key and (PH >= 9 or ph < PH)]
            if trunc is not None and key == "sy":
                eng_snap = trunc.pop("__eng__", None)

                def fin(e, snap=dict(trunc)):
                    for nm, val in snap.items():
                        if val:
                            e.wait_ge(streams[nm][0], val)
                fns = fns + [fin]

            def run_all(e, fns=fns):
                for fn in fns:
                    fn(e)
            keys[key](run_all)

    return nc


_CACHE = {}


def _get_nc():
    if "nc" not in _CACHE:
        _CACHE["nc"] = build_program()
    return _CACHE["nc"]


def kernel(L0_attn_x, L0_attn_Wq, L0_attn_Wk, L0_attn_Wv, L0_attn_ln_gamma,
           L0_attn_ln_beta, L0_ffn_w_gate, L0_ffn_w_up, L0_ffn_w_down):
    x = np.ascontiguousarray(np.asarray(L0_attn_x, dtype=np.float32))
    wq = np.ascontiguousarray(np.asarray(L0_attn_Wq, dtype=np.float32))
    wk = np.ascontiguousarray(np.asarray(L0_attn_Wk, dtype=np.float32))
    wv = np.ascontiguousarray(np.asarray(L0_attn_Wv, dtype=np.float32))
    g = np.asarray(L0_attn_ln_gamma, dtype=np.float32)
    be = np.asarray(L0_attn_ln_beta, dtype=np.float32)
    wg = np.ascontiguousarray(np.asarray(L0_ffn_w_gate, dtype=np.int32))
    wu = np.ascontiguousarray(np.asarray(L0_ffn_w_up, dtype=np.int32))
    wd = np.ascontiguousarray(np.asarray(L0_ffn_w_down, dtype=np.int32))

    gbc = np.ascontiguousarray(np.broadcast_to(g[None, :], (128, E)))
    bbc = np.ascontiguousarray(np.broadcast_to(be[None, :], (128, E)))
    idn = np.eye(128, dtype=bf16)
    cstv = np.zeros((128, 8), np.float32)
    cstv[:, 0] = EPS
    cstv[:, 1] = -0.5
    cstv[:, 2] = 1.0 / 32.0
    cstv[:, 3] = SC127
    cstv[:, 4] = -1.0

    idx = np.arange(S, dtype=np.float64)
    in_maps = []
    for c in range(8):
        b, hq = divmod(c, 2)
        qlo = hq * QT
        perm = np.concatenate([np.arange(qlo, qlo + QT), np.arange(0, qlo),
                               np.arange(qlo + QT, S)])
        xp = np.ascontiguousarray(x[b][perm])
        dmat = np.abs(idx[perm][:, None] - idx[None, qlo:qlo + QT]) / S
        in_maps.append(dict(
            xs=xp, wqi=wq, wki=wk, wvi=wv, gbc=gbc, bbc=bbc,
            wgi=wg, wui=wu, wdi=wd,
            dst=np.ascontiguousarray(dmat.astype(np.float32)),
            idn=idn, cst=cstv,
        ))

    nc = _get_nc()
    res = run_bass_kernel_spmd(nc, in_maps, core_ids=list(range(8)))
    out = np.empty((B, S, E), dtype=np.float32)
    for c in range(8):
        b, hq = divmod(c, 2)
        out[b, hq * QT:(hq + 1) * QT, :] = res.results[c]["y"]
    return out


if __name__ == "__main__":
    nc = build_program()
    print("built ok; instructions:", nc.n_instructions())


# version 07
# speedup vs baseline: 1.3202x; 1.0673x over previous
import os
import numpy as np
import ml_dtypes
from contextlib import ExitStack

import concourse.bass as bass
import concourse.mybir as mybir
from concourse.bass_utils import run_bass_kernel_spmd

F32 = mybir.dt.float32
BF16 = mybir.dt.bfloat16
I32 = mybir.dt.int32
AF = mybir.ActivationFunctionType
OP = mybir.AluOpType
AX = mybir.AxisListType.X
bf16 = ml_dtypes.bfloat16

B, S, E, H = 4, 4096, 1024, 16
HID = 3072
QT = 2048
NCH = E // 128
EPS = 1e-5
SC127 = float(np.float32(1.0) / np.float32(127.0))
SC2 = float(np.float32(SC127) * np.float32(SC127))

QC = 256
NQC = QT // QC
NKC = S // 128
NHP = H // 2
NT1 = int(os.environ.get("KNT1", S // 128))
NOWN = min(QT // 128, NT1)
NTC3 = QT // 256
NHC = HID // 128
VROW = H * 65


def build_program():
    nc = bass.Bass("TRN2", target_bir_lowering=False, detect_race_conditions=False,
                   dynamic_dma_scratch_size=4096)

    xs = nc.dram_tensor("xs", [S, E], F32, kind="ExternalInput")
    wqi = nc.dram_tensor("wqi", [E, E], F32, kind="ExternalInput")
    wki = nc.dram_tensor("wki", [E, E], F32, kind="ExternalInput")
    wvi = nc.dram_tensor("wvi", [E, E], F32, kind="ExternalInput")
    gbc = nc.dram_tensor("gbc", [128, E], F32, kind="ExternalInput")
    bbc = nc.dram_tensor("bbc", [128, E], F32, kind="ExternalInput")
    wgi = nc.dram_tensor("wgi", [HID, E], I32, kind="ExternalInput")
    wui = nc.dram_tensor("wui", [HID, E], I32, kind="ExternalInput")
    wdi = nc.dram_tensor("wdi", [E, HID], I32, kind="ExternalInput")
    dst = nc.dram_tensor("dst", [S, QT], BF16, kind="ExternalInput")
    idn = nc.dram_tensor("idn", [128, 128], BF16, kind="ExternalInput")
    cst = nc.dram_tensor("cst", [128, 8], F32, kind="ExternalInput")
    y = nc.dram_tensor("y", [QT, E], F32, kind="ExternalOutput")

    wqb = nc.dram_tensor("wqb", [E, E], BF16)
    wkb = nc.dram_tensor("wkb", [E, E], BF16)
    wvb = nc.dram_tensor("wvb", [E, E], BF16)
    wgb = nc.dram_tensor("wgb", [HID, E], BF16)
    wub = nc.dram_tensor("wub", [HID, E], BF16)
    wdb = nc.dram_tensor("wdb", [E, HID], BF16)
    qh = nc.dram_tensor("qh", [NCH, 128, QT], BF16)
    xlh = nc.dram_tensor("xlh", [NCH, 128, S], BF16)
    x1h = nc.dram_tensor("x1h", [QT, E], F32)

    PH = float(os.environ.get("KPHASES", "9"))
    KDBG = int(os.environ.get("KDBG", "0"))
    if KDBG:
        dbg_xlh = nc.dram_tensor("dbg_xlh", [NCH, 128, S], BF16, kind="ExternalOutput")
        dbg_qh = nc.dram_tensor("dbg_qh", [NCH, 128, QT], BF16, kind="ExternalOutput")
        dbg_k = nc.dram_tensor("dbg_k", [128, NCH * S], BF16, kind="ExternalOutput")
        dbg_v = nc.dram_tensor("dbg_v", [128, NKC * VROW], BF16, kind="ExternalOutput")
        dbg_x1 = nc.dram_tensor("dbg_x1", [QT, E], F32, kind="ExternalOutput")
    KSUB = int(os.environ.get("KSUB", "9"))
    KSER = int(os.environ.get("KSER", "0"))
    KWT = int(os.environ.get("KWT", "1"))
    cur_phase = [0]
    phase_tot = {}
    sched = []

    def on(key):
        def deco(fn):
            sched.append((key, fn, cur_phase[0]))
        return deco

    with ExitStack() as ctx:
        Eo = ctx.enter_context

        gb_sb = Eo(nc.sbuf_tensor("gb_sb", [128, E], F32, side="left"))
        bb_sb = Eo(nc.sbuf_tensor("bb_sb", [128, E], F32, side="left"))
        id_sb = Eo(nc.sbuf_tensor("id_sb", [128, 128], BF16, side="left"))
        cs_sb = Eo(nc.sbuf_tensor("cs_sb", [128, 8], F32, side="left"))
        st3_sb = Eo(nc.sbuf_tensor("st3_sb", [128, 64], F32, side="left"))
        skv = ExitStack()
        k_sb = skv.enter_context(nc.sbuf_tensor("k_sb", [128, NCH * S], BF16, side="left"))
        v_sb = skv.enter_context(nc.sbuf_tensor("v_sb", [128, NKC * VROW], BF16, side="left"))

        nc.const_aps.register(F32, EPS, cs_sb[:, 0:1])
        nc.const_aps.register(F32, -0.5, cs_sb[:, 1:2])
        nc.const_aps.register(F32, 1.0 / 32.0, cs_sb[:, 2:3])
        nc.const_aps.register(F32, SC127, cs_sb[:, 3:4])
        nc.const_aps.register(F32, -1.0, cs_sb[:, 4:5])

        sem_t = Eo(nc.semaphore("sem_t")); nt = [0]
        sem_v = Eo(nc.semaphore("sem_v")); nv = [0]
        sem_s = Eo(nc.semaphore("sem_s")); ns = [0]
        sem_g = Eo(nc.semaphore("sem_g")); ng = [0]

        streams = {}

        def mksem(name):
            sem = Eo(nc.semaphore(name))
            ctr = [0]
            streams[name] = (sem, ctr)
            return sem, ctr

        s_c0, n_c0 = mksem("s_c0")
        s_castq, n_castq = mksem("s_castq")
        s_castf, n_castf = mksem("s_castf")
        s_wqk, n_wqk = mksem("s_wqk")
        s_wv, n_wv = mksem("s_wv")
        s_w3, n_w3 = mksem("s_w3")
        s_x0, n_x0 = mksem("s_x0")
        s_x1l, n_x1l = mksem("s_x1l")
        s_xlh0, n_xlh0 = mksem("s_xlh0")
        s_xlh1, n_xlh1 = mksem("s_xlh1")
        s_qst, n_qst = mksem("s_qst")
        s_xfb0, n_xfb0 = mksem("s_xfb0")
        s_xfb1, n_xfb1 = mksem("s_xfb1")
        s_qc, n_qc = mksem("s_qc")
        s_dist, n_dist = mksem("s_dist")
        s_xr0, n_xr0 = mksem("s_xr0")
        s_xr1, n_xr1 = mksem("s_xr1")
        s_xs0, n_xs0 = mksem("s_xs0")
        s_xs1, n_xs1 = mksem("s_xs1")
        s_x13, n_x13 = mksem("s_x13")
        s_y, n_y = mksem("s_y")
        s_dbg, n_dbg = mksem("s_dbg")
        s_xP = [(s_x0, n_x0), (s_x1l, n_x1l)]
        s_xlhP = [(s_xlh0, n_xlh0), (s_xlh1, n_xlh1)]
        s_xfbP = [(s_xfb0, n_xfb0), (s_xfb1, n_xfb1)]
        s_xrP = [(s_xr0, n_xr0), (s_xr1, n_xr1)]
        s_xsP = [(s_xs0, n_xs0), (s_xs1, n_xs1)]

        def bump(c, k=1):
            c[0] += k
            return c[0]

        def vec_lnstats(e, src, sq, s1, m_ap, s2_ap):
            e.reduce_sum(s1, src, axis=AX)
            e.tensor_tensor(sq, src, src, OP.mult)
            e.reduce_sum(s2_ap, sq, axis=AX)
            e.tensor_scalar(m_ap, s1, 1.0 / E, None, OP.mult)
            e.tensor_scalar(s2_ap, s2_ap, 1.0 / E, None, OP.mult)
            return e.scalar_tensor_tensor(s2_ap, m_ap, m_ap, s2_ap, OP.mult, OP.subtract)

        def scl_rstd(e, var_neg, r_ap):
            e.activation(r_ap, var_neg, AF.Ln, scale=-1.0, bias=EPS)
            return e.activation(r_ap, r_ap, AF.Exp, scale=-0.5)

        @on("sy")
        def _(e):
            e.dma_start(cs_sb[:, :], cst[:, :]).then_inc(s_c0, 16)
            e.dma_start(id_sb[:, :], idn[:, :]).then_inc(s_c0, 16)
            e.dma_start(gb_sb[:, :], gbc[:, :]).then_inc(s_c0, 16)
            e.dma_start(bb_sb[:, :], bbc[:, :]).then_inc(s_c0, 16)
        p0_a = bump(n_c0, 64)

        @on("g")
        def _(e):
            e.memset(v_sb[:, :], 1.0).then_inc(sem_g, 1)
            e.dma_start(wqb[:, :], wqi[:, :]).then_inc(s_castq, 16)
            e.dma_start(wkb[:, :], wki[:, :]).then_inc(s_castq, 16)
            e.dma_start(wvb[:, :], wvi[:, :]).then_inc(s_castq, 16)
            e.dma_start(wgb[:, :], wgi[:, :]).then_inc(s_castf, 16)
            e.dma_start(wub[:, :], wui[:, :]).then_inc(s_castf, 16)
            e.dma_start(wdb[:, :], wdi[:, :]).then_inc(s_castf, 16)
        memset_g = bump(ng, 1)
        qkv_cast = bump(n_castq, 48)
        ffn_cast = bump(n_castf, 48)
        phase_tot[0.5] = {k: c[0] for k, (s, c) in streams.items()}
        cur_phase[0] = 0.5

        s1a = ExitStack()
        A = s1a.enter_context
        wqT = A(nc.sbuf_tensor("wqT", [128, NCH * E], BF16, side="right"))
        wkT = A(nc.sbuf_tensor("wkT", [128, NCH * E], BF16, side="right"))
        x_t = A(nc.sbuf_tensor("x_t", [128, 2 * E], F32, side="right"))
        sq_t = A(nc.sbuf_tensor("sq_t", [128, E], BF16, side="right"))
        xln_t = A(nc.sbuf_tensor("xln_t", [128, 2 * E], BF16, side="right"))
        xfm_t = A(nc.sbuf_tensor("xfm_t", [128, 2 * 1024], BF16, side="right"))
        qev_t = A(nc.sbuf_tensor("qev_t", [128, 1024], BF16, side="right"))
        stat_t = A(nc.sbuf_tensor("stat_t", [128, 8], F32, side="right"))
        ptr = A(nc.psum_tensor("ptr", [128, 2048], F32))
        pk = A(nc.psum_tensor("pk", [128, 2048], F32))

        @on("sy")
        def _(e):
            if not KWT:
                return
            e.wait_ge(s_castq, qkv_cast)
            for fi in range(NCH):
                e.dma_start_transpose(wqT[:, fi * E:(fi + 1) * E], wqb[:, fi * 128:(fi + 1) * 128]).then_inc(s_wqk, 16)
            for fi in range(NCH):
                e.dma_start_transpose(wkT[:, fi * E:(fi + 1) * E], wkb[:, fi * 128:(fi + 1) * 128]).then_inc(s_wqk, 16)
        wqk_a = bump(n_wqk, 2 * NCH * 16) if KWT else 0

        x_a = [0] * NT1
        stats_v = [0] * NT1
        rstd_s = [0] * NT1
        apply_v = [0] * NT1
        tr_t = [[0, 0] for _ in range(NT1)]
        fm_s = [[0, 0] for _ in range(NT1)]
        xlh_o = [0] * NT1
        kmm_t = [[0, 0] for _ in range(NT1)]
        kev_s = [[0, 0] for _ in range(NT1)]
        qmm_t = [[0, 0] for _ in range(NT1)]
        qev_s = [[0, 0] for _ in range(NT1)]
        qst_o = [0] * NT1

        k3 = k_sb[:, :].rearrange("p (c s) -> p c s", s=S)
        pk3 = pk[:, :].rearrange("p (c t) -> p c t", t=512)
        ptr3v = ptr[:, :].rearrange("p (c t) -> p c t", t=512)

        for i in range(NT1):
            sl = i % 2
            own = i < NOWN

            @on("sy")
            def _(e, i=i, sl=sl):
                if i >= 2:
                    e.wait_ge(sem_v, apply_v[i - 2])
                e.dma_start(x_t[:, sl * E:(sl + 1) * E], xs[i * 128:(i + 1) * 128, :]).then_inc(s_xP[sl][0], 16)
            x_a[i] = bump(s_xP[sl][1], 16)

            @on("v")
            def _(e, i=i, sl=sl):
                e.wait_ge(s_xP[sl][0], x_a[i])
                if i >= 1:
                    e.wait_ge(sem_s, rstd_s[i - 1])
                vec_lnstats(e, x_t[:, sl * E:(sl + 1) * E], sq_t[:, :], stat_t[:, 0:1],
                            stat_t[:, 2:3], stat_t[:, 1:2]).then_inc(sem_v, 1)
            stats_v[i] = bump(nv)

            @on("s")
            def _(e, i=i):
                if i == 0:
                    e.wait_ge(s_c0, p0_a)
                e.wait_ge(sem_v, stats_v[i])
                scl_rstd(e, stat_t[:, 1:2], stat_t[:, 3:4]).then_inc(sem_s, 1)
            rstd_s[i] = bump(ns)

            @on("v")
            def _(e, i=i, sl=sl):
                e.wait_ge(sem_s, rstd_s[i])
                if i >= 2:
                    e.wait_ge(sem_t, tr_t[i - 2][1])
                e.tensor_scalar(xln_t[:, sl * E:(sl + 1) * E], x_t[:, sl * E:(sl + 1) * E],
                                stat_t[:, 2:3], stat_t[:, 3:4], OP.subtract, OP.mult).then_inc(sem_v, 1)
            apply_v[i] = bump(nv)

            if KSUB < 2:
                continue

            fsl = i % 2
            for r in range(2):
                @on("t")
                def _(e, i=i, sl=sl, r=r):
                    if r == 0:
                        if i == 0:
                            e.wait_ge(s_c0, p0_a)
                        e.wait_ge(sem_v, apply_v[i])
                        if i >= 1:
                            e.wait_ge(sem_s, fm_s[i - 1][1])
                    else:
                        e.wait_ge(sem_s, fm_s[i][0])
                    mm = None
                    for j in range(4 * r, 4 * r + 4):
                        mm = e.matmul(ptr[:, (j % 4) * 512:(j % 4) * 512 + 128],
                                      xln_t[:, sl * E + j * 128: sl * E + (j + 1) * 128],
                                      id_sb[:, :], start=True, stop=True)
                    mm.then_inc(sem_t, 1)
                tr_t[i][r] = bump(nt)

                @on("s")
                def _(e, i=i, fsl=fsl, r=r):
                    e.wait_ge(sem_t, tr_t[i][r])
                    if r == 0 and i >= 2:
                        e.wait_ge(sem_t, (qmm_t if i - 2 < NOWN else kmm_t)[i - 2][1])
                        e.wait_ge(s_xlhP[i % 2][0], xlh_o[i - 2])
                    e.copy(xfm_t[:, fsl * 1024 + r * 512: fsl * 1024 + (r + 1) * 512],
                           ptr3v[:, :, 0:128]).then_inc(sem_s, 1)
                fm_s[i][r] = bump(ns)

            @on("sy")
            def _(e, i=i, fsl=fsl):
                e.wait_ge(sem_s, fm_s[i][1])
                src = xfm_t[:, fsl * 1024:(fsl + 1) * 1024].rearrange("p (c t) -> p c t", c=NCH)
                d = xlh[:, :, i * 128:(i + 1) * 128].transpose([1, 0, 2])
                e.dma_start(d, src).then_inc(s_xlhP[i % 2][0], 16)
            xlh_o[i] = bump(s_xlhP[i % 2][1], 16)

            if KSUB < 3:
                continue

            for h in range(2):
                @on("t")
                def _(e, i=i, fsl=fsl, h=h):
                    if h == 0:
                        e.wait_ge(sem_s, fm_s[i][1])
                        e.wait_ge(s_wqk, wqk_a)
                        if i >= 1:
                            e.wait_ge(sem_s, (qev_s if i - 1 < NOWN else kev_s)[i - 1][1])
                    else:
                        e.wait_ge(sem_s, kev_s[i][0])
                    xf = xfm_t[:, fsl * 1024:(fsl + 1) * 1024]
                    mm = None
                    for fo in range(4 * h, 4 * h + 4):
                        for fi in range(NCH):
                            mm = e.matmul(pk[:, (fo % 4) * 512:(fo % 4) * 512 + 128],
                                          wkT[:, fi * E + fo * 128: fi * E + (fo + 1) * 128],
                                          xf[:, fi * 128:(fi + 1) * 128],
                                          start=(fi == 0), stop=(fi == NCH - 1))
                    mm.then_inc(sem_t, 1)
                kmm_t[i][h] = bump(nt)

                @on("s")
                def _(e, i=i, h=h):
                    e.wait_ge(sem_t, kmm_t[i][h])
                    d = k3[:, 4 * h:4 * h + 4, i * 128:(i + 1) * 128]
                    e.copy(d, pk3[:, :, 0:128]).then_inc(sem_s, 1)
                kev_s[i][h] = bump(ns)

            if own and KSUB >= 4:
                for h in range(2):
                    @on("t")
                    def _(e, i=i, fsl=fsl, h=h):
                        e.wait_ge(sem_s, kev_s[i][1] if h == 0 else qev_s[i][0])
                        xf = xfm_t[:, fsl * 1024:(fsl + 1) * 1024]
                        mm = None
                        for fo in range(4 * h, 4 * h + 4):
                            for fi in range(NCH):
                                mm = e.matmul(pk[:, (fo % 4) * 512:(fo % 4) * 512 + 128],
                                              wqT[:, fi * E + fo * 128: fi * E + (fo + 1) * 128],
                                              xf[:, fi * 128:(fi + 1) * 128],
                                              start=(fi == 0), stop=(fi == NCH - 1))
                        mm.then_inc(sem_t, 1)
                    qmm_t[i][h] = bump(nt)

                    @on("s")
                    def _(e, i=i, h=h):
                        e.wait_ge(sem_t, qmm_t[i][h])
                        if i >= 1 and h == 0:
                            e.wait_ge(s_qst, qst_o[i - 1])
                        e.copy(qev_t[:, h * 512:(h + 1) * 512].rearrange("p (c t) -> p c t", t=128),
                               pk3[:, :, 0:128]).then_inc(sem_s, 1)
                    qev_s[i][h] = bump(ns)

                @on("sy")
                def _(e, i=i):
                    e.wait_ge(sem_s, qev_s[i][1])
                    src = qev_t[:, :].rearrange("p (c t) -> p c t", c=NCH)
                    d = qh[:, :, i * 128:(i + 1) * 128].transpose([1, 0, 2])
                    e.dma_start(d, src).then_inc(s_qst, 16)
                qst_o[i] = bump(n_qst, 16)

        bar1_t = nt[0]
        bar1_v = nv[0]
        bar1_s = ns[0]
        phase_tot[1] = {k: c[0] for k, (s, c) in streams.items()}
        cur_phase[0] = 1

        s1a.close()

        s1b = ExitStack()
        A = s1b.enter_context
        wvT = A(nc.sbuf_tensor("wvT", [128, NCH * E], BF16, side="right"))
        xfb_t = A(nc.sbuf_tensor("xfb_t", [128, 2 * 1024], BF16, side="right"))
        pvA = A(nc.psum_tensor("pvA", [128, 512], F32))
        pvB = A(nc.psum_tensor("pvB", [128, 512], F32))

        v4 = v_sb[:, :].rearrange("p (k h d) -> p k h d", k=NKC, h=H)

        @on("sy")
        def _(e):
            e.wait_ge(sem_t, bar1_t)
            e.wait_ge(sem_v, bar1_v)
            e.wait_ge(sem_s, bar1_s)
            e.wait_ge(s_castq, qkv_cast)
            for fi in range(NCH):
                e.dma_start_transpose(wvT[:, fi * E:(fi + 1) * E], wvb[:, fi * 128:(fi + 1) * 128]).then_inc(s_wv, 16)
        wv_a = bump(n_wv, NCH * 16)

        xfb_a = [0] * NT1
        vmm_t = [[0, 0] for _ in range(NT1)]
        vev_s = [[0, 0] for _ in range(NT1)]

        for i in range(NT1):
            sl = i % 2

            @on("sy")
            def _(e, i=i, sl=sl):
                e.wait_ge(s_xlhP[i % 2][0], xlh_o[i])
                if i >= 2:
                    e.wait_ge(sem_t, vmm_t[i - 2][1])
                s = xlh[:, :, i * 128:(i + 1) * 128].transpose([1, 0, 2])
                e.dma_start(xfb_t[:, sl * 1024:(sl + 1) * 1024].rearrange("p (c t) -> p c t", c=NCH),
                            s).then_inc(s_xfbP[sl][0], 16)
            xfb_a[i] = bump(s_xfbP[sl][1], 16)

            for h in range(2):
                @on("t")
                def _(e, i=i, sl=sl, h=h):
                    if h == 0:
                        e.wait_ge(s_xfbP[sl][0], xfb_a[i])
                        e.wait_ge(s_wv, wv_a)
                    if i >= 1:
                        e.wait_ge(sem_s, vev_s[i - 1][h])
                    bank = pvA if h == 0 else pvB
                    mm = None
                    for fi in range(NCH):
                        mm = e.matmul(bank[:, :],
                                      xfb_t[:, sl * 1024 + fi * 128: sl * 1024 + (fi + 1) * 128],
                                      wvT[:, fi * E + h * 512: fi * E + h * 512 + 512],
                                      start=(fi == 0), stop=(fi == NCH - 1))
                    mm.then_inc(sem_t, 1)
                vmm_t[i][h] = bump(nt)

                @on("s")
                def _(e, i=i, h=h):
                    e.wait_ge(sem_t, vmm_t[i][h])
                    if i == 0 and h == 0:
                        e.wait_ge(sem_g, memset_g)
                    bank = pvA if h == 0 else pvB
                    d = v4[:, i, 8 * h:8 * h + 8, 0:64]
                    s = bank[:, :].rearrange("p (h d) -> p h d", d=64)
                    e.copy(d, s).then_inc(sem_s, 1)
                vev_s[i][h] = bump(ns)

        bar2_t = nt[0]
        bar2_v = nv[0]
        bar2_s = ns[0]
        phase_tot[2] = {k: c[0] for k, (s, c) in streams.items()}
        cur_phase[0] = 2

        s1b.close()

        s2 = ExitStack()
        A = s2.enter_context
        dist_t = A(nc.sbuf_tensor("dist_t", [128, NKC * QC], BF16, side="right"))
        probs_t = A(nc.sbuf_tensor("probs_t", [128, 4096], BF16, side="right"))
        qc_t = A(nc.sbuf_tensor("qc_t", [128, 2 * 2048], BF16, side="right"))
        o_t = A(nc.sbuf_tensor("o_t", [128, 2048], F32, side="right"))
        xr_t = A(nc.sbuf_tensor("xr_t", [128, 2 * E], F32, side="right"))
        x1_t = A(nc.sbuf_tensor("x1_t", [128, 2 * E], F32, side="right"))
        t1_t = A(nc.sbuf_tensor("t1_t", [128, E], F32, side="right"))
        st2_t = A(nc.sbuf_tensor("st2_t", [128, 16], F32, side="right"))
        pars = A(nc.psum_tensor("pars", [128, 4096], F32))

        NH = H
        qc_a = [0] * NQC
        dist_a = [0] * NQC
        xr_a = [0] * NQC
        qk_t, mult_v, exp_s, av_t, av_last, evac_v = {}, {}, {}, {}, {}, {}
        ln2_v = [0] * NQC
        x1st_o = [0] * NQC

        def av_col(kc):
            return ((kc // 4) % 3) * 1024 + (kc % 2) * 512 + ((kc % 4) // 2) * 256

        for qc in range(NQC):
            qsl = qc % 2

            @on("sy")
            def _(e, qc=qc):
                if qc == 0:
                    e.wait_ge(sem_t, bar2_t)
                    e.wait_ge(sem_v, bar2_v)
                    e.wait_ge(sem_s, bar2_s)
                    e.wait_ge(s_qst, qst_o[NOWN - 1])
                if qc >= 2:
                    e.wait_ge(sem_t, qk_t[(qc - 2, NH - 1, NKC - 1)])
                s = qh[:, :, qc * QC:(qc + 1) * QC].transpose([1, 0, 2])
                e.dma_start(qc_t[:, (qc % 2) * 2048:(qc % 2 + 1) * 2048].rearrange("p (c q) -> p c q", c=NCH),
                            s).then_inc(s_qc, 16)
            qc_a[qc] = bump(n_qc, 16)

            for dg in range(4):
                @on("sy")
                def _(e, qc=qc, dg=dg):
                    if qc >= 1:
                        e.wait_ge(sem_v, mult_v[(qc - 1, NH - 1, 4 * dg + 3)])
                    s = dst[dg * 1024:(dg + 1) * 1024, qc * QC:(qc + 1) * QC]
                    s = s.rearrange("(c k) q -> c k q", k=128).transpose([1, 0, 2])
                    e.dma_start(dist_t[:, dg * 8 * QC:(dg + 1) * 8 * QC].rearrange("p (c q) -> p c q", c=8),
                                s).then_inc(s_dist, 16)
            dist_a[qc] = bump(n_dist, 64)

            @on("sy")
            def _(e, qc=qc, qsl=qsl):
                if qc >= 2:
                    e.wait_ge(sem_v, ln2_v[qc - 2])
                    e.wait_ge(s_xsP[qc % 2][0], x1st_o[qc - 2])
                e.dma_start(xr_t[:, qsl * E:(qsl + 1) * E], xs[qc * QC:qc * QC + 128, :]).then_inc(s_xrP[qsl][0], 16)
                e.dma_start(x1_t[:, qsl * E:(qsl + 1) * E], xs[qc * QC + 128:qc * QC + 256, :]).then_inc(s_xrP[qsl][0], 16)
            xr_a[qc] = bump(s_xrP[qsl][1], 32)

            for h in range(NH):
                Gi = qc * NH + h
                hh = (h % 2) * 64
                g = h // 2

                def em_av(kc, qc=qc, h=h, Gi=Gi):
                    Qi = Gi * 8 + kc // 4

                    @on("t")
                    def _(e, qc=qc, h=h, kc=kc, Qi=Qi, Gi=Gi):
                        if kc % 4 == 0:
                            e.wait_ge(sem_s, exp_s[Qi])
                        if kc == 0 and Gi >= 1:
                            e.wait_ge(sem_v, evac_v[Gi - 1])
                        pcol = (Qi % 4) * 1024 + (kc % 2) * 512 + ((kc % 4) // 2) * 256
                        mm = None
                        for qh2 in range(2):
                            st = probs_t[:, pcol + qh2 * 128: pcol + qh2 * 128 + 128]
                            mv = v_sb[:, kc * VROW + h * 65: kc * VROW + (h + 1) * 65]
                            mm = e.matmul(pars[:, 3072 + qh2 * 512: 3072 + qh2 * 512 + 65], st, mv,
                                          start=(kc == 0), stop=(kc == NKC - 1))
                        mm.then_inc(sem_t, 1)
                    av_t[(qc, h, kc)] = bump(nt)
                    if kc % 4 == 3:
                        av_last[Qi] = av_t[(qc, h, kc)]

                for kc in range(NKC):
                    Qi = Gi * 8 + kc // 4
                    col = av_col(kc)

                    @on("t")
                    def _(e, qc=qc, h=h, kc=kc, col=col, Qi=Qi, hh=hh, g=g):
                        if kc == 0 and h == 0:
                            e.wait_ge(s_qc, qc_a[qc])
                        if kc % 4 == 0 and Qi >= 3:
                            e.wait_ge(sem_s, exp_s[Qi - 3])
                        st = k_sb[hh:hh + 64, g * S + kc * 128: g * S + kc * 128 + 128]
                        mv = qc_t[hh:hh + 64, (qc % 2) * 2048 + g * QC: (qc % 2) * 2048 + (g + 1) * QC]
                        e.matmul(pars[:, col: col + QC], st, mv, start=True, stop=True,
                                 tile_position=(hh, 0)).then_inc(sem_t, 1)
                    qk_t[(qc, h, kc)] = bump(nt)

                    if kc % 2 == 1:
                        p = kc // 2

                        @on("v")
                        def _(e, qc=qc, h=h, p=p, kc=kc):
                            e.wait_ge(sem_t, qk_t[(qc, h, kc)])
                            if h == 0 and kc == 1:
                                e.wait_ge(s_dist, dist_a[qc])
                            base = ((kc // 4) % 3) * 1024 + (p % 2) * 256
                            sc2 = bass.AP(pars, base, [[4096, 128], [512, 2], [1, QC]])
                            rep = bass.AP(dist_t, (2 * p) * QC, [[NKC * QC, 128], [QC, 2], [1, QC]])
                            e.tensor_tensor(sc2, sc2, rep, OP.mult).then_inc(sem_v, 1)
                        mult_v[(qc, h, p)] = bump(nv)

                    if kc % 4 == 3:
                        @on("s")
                        def _(e, qc=qc, h=h, kc=kc, Qi=Qi):
                            e.wait_ge(sem_v, mult_v[(qc, h, kc // 2)])
                            if Qi >= 4:
                                e.wait_ge(sem_t, av_last[Qi - 4])
                            qs3 = (kc // 4) % 3
                            src = bass.AP(pars, qs3 * 1024, [[4096, 128], [512, 2], [256, 2], [1, QC]])
                            dsx = bass.AP(probs_t, (Qi % 4) * 1024, [[4096, 128], [512, 2], [256, 2], [1, QC]])
                            e.activation(dsx, src, AF.Exp, scale=1.0 / 32.0).then_inc(sem_s, 1)
                        exp_s[Qi] = bump(ns)
                        if kc // 4 >= 2:
                            for k2 in range(4 * (kc // 4 - 2), 4 * (kc // 4 - 2) + 4):
                                em_av(k2)

                for k2 in range(NKC - 8, NKC):
                    em_av(k2)

                @on("v")
                def _(e, qc=qc, h=h, Gi=Gi, w=av_t[(qc, h, NKC - 1)], wrc=nv[0] + 1):
                    e.wait_ge(sem_t, w)
                    if h == 0 and qc >= 1:
                        e.wait_ge(sem_v, ln2_v[qc - 1])
                    src = bass.AP(pars, 3072 + 64, [[4096, 128], [512, 2], [1, 1]])
                    e.reciprocal(st2_t[:, 8:10], src).then_inc(sem_v, 1)
                    e.wait_ge(sem_v, wrc)
                    mm = None
                    for qh2 in range(2):
                        mm = e.tensor_scalar(o_t[:, qh2 * 1024 + h * 64: qh2 * 1024 + (h + 1) * 64],
                                             pars[:, 3072 + qh2 * 512: 3072 + qh2 * 512 + 64],
                                             st2_t[:, 8 + qh2: 9 + qh2], None, OP.mult)
                    mm.then_inc(sem_v, 1)
                bump(nv)
                evac_v[Gi] = bump(nv)

            ln2sub = [0, 0]
            for ts in range(2):
                @on("v")
                def _(e, qc=qc, ts=ts):
                    if qc == 0 and ts == 0:
                        e.wait_ge(s_c0, p0_a)
                    e.wait_ge(s_xrP[qc % 2][0], xr_a[qc])
                    src = o_t[:, ts * 1024:(ts + 1) * 1024]
                    vec_lnstats(e, src, t1_t[:, :], st2_t[:, 0:1],
                                st2_t[:, 2:3], st2_t[:, 1:2]).then_inc(sem_v, 1)
                st2v = bump(nv)

                @on("s")
                def _(e, w=st2v):
                    e.wait_ge(sem_v, w)
                    scl_rstd(e, st2_t[:, 1:2], st2_t[:, 3:4]).then_inc(sem_s, 1)
                st2s = bump(ns)

                @on("v")
                def _(e, qc=qc, qsl=qsl, ts=ts, w=st2s):
                    e.wait_ge(sem_s, w)
                    src = o_t[:, ts * 1024:(ts + 1) * 1024]
                    xbuf = xr_t if ts == 0 else x1_t
                    xap = xbuf[:, qsl * E:(qsl + 1) * E]
                    e.tensor_scalar(t1_t[:, :], src, st2_t[:, 2:3], st2_t[:, 3:4], OP.subtract, OP.mult)
                    e.tensor_tensor(t1_t[:, :], t1_t[:, :], gb_sb[:, :], OP.mult)
                    e.tensor_tensor(t1_t[:, :], t1_t[:, :], bb_sb[:, :], OP.add)
                    e.tensor_tensor(xap, t1_t[:, :], xap, OP.add)
                    ti = qc * 2 + ts
                    vec_lnstats(e, xap, t1_t[:, :], st2_t[:, 4:5],
                                st3_sb[:, ti * 2: ti * 2 + 1], st2_t[:, 5:6]).then_inc(sem_v, 1)
                st3v = bump(nv)
                ln2sub[ts] = st3v

                @on("s")
                def _(e, qc=qc, ts=ts, w=st3v):
                    e.wait_ge(sem_v, w)
                    ti = qc * 2 + ts
                    scl_rstd(e, st2_t[:, 5:6], st3_sb[:, ti * 2 + 1: ti * 2 + 2]).then_inc(sem_s, 1)
                bump(ns)

                @on("sy")
                def _(e, qc=qc, qsl=qsl, ts=ts, w=st3v):
                    e.wait_ge(sem_v, w)
                    xbuf = xr_t if ts == 0 else x1_t
                    e.dma_start(x1h[qc * QC + ts * 128: qc * QC + (ts + 1) * 128, :],
                                xbuf[:, qsl * E:(qsl + 1) * E]).then_inc(s_xsP[qc % 2][0], 16)
                x1st_o[qc] = bump(s_xsP[qc % 2][1], 16)
            ln2_v[qc] = ln2sub[1]

        bar3_t = nt[0]
        bar3_v = nv[0]
        bar3_s = ns[0]

        if KDBG:
            @on("sy")
            def _(e):
                e.wait_ge(sem_t, bar3_t)
                e.wait_ge(sem_v, bar3_v)
                e.wait_ge(sem_s, bar3_s)
                e.dma_start(dbg_k[:, :], k_sb[:, :]).then_inc(s_dbg, 16)
                e.dma_start(dbg_v[:, :], v_sb[:, :]).then_inc(s_dbg, 16)
                e.dma_start(dbg_xlh[:, :, :], xlh[:, :, :]).then_inc(s_dbg, 16)
                e.dma_start(dbg_qh[:, :, :], qh[:, :, :]).then_inc(s_dbg, 16)
                e.dma_start(dbg_x1[:, :], x1h[:, :]).then_inc(s_dbg, 16)
            bump(n_dbg, 80)
        phase_tot[3] = {k: c[0] for k, (s, c) in streams.items()}
        cur_phase[0] = 3

        s2.close()
        skv.close()

        s3 = ExitStack()
        A = s3.enter_context
        wgT = A(nc.sbuf_tensor("wgT", [128, NCH * HID], BF16, side="right"))
        wuT = A(nc.sbuf_tensor("wuT", [128, NCH * HID], BF16, side="right"))
        wdT = A(nc.sbuf_tensor("wdT", [128, NHC * E], BF16, side="right"))
        x13_t = A(nc.sbuf_tensor("x13_t", [128, 2 * E], F32, side="right"))
        xl3_t = A(nc.sbuf_tensor("xl3_t", [128, 2 * E], BF16, side="right"))
        xf3_t = A(nc.sbuf_tensor("xf3_t", [128, 2 * 2048], BF16, side="right"))
        act_t = A(nc.sbuf_tensor("act_t", [128, 2 * NHC * 256], BF16, side="right"))
        ssg_t = A(nc.sbuf_tensor("ssg_t", [128, 256], F32, side="right"))
        y_t = A(nc.sbuf_tensor("y_t", [128, 2 * E], F32, side="right"))
        ptr3 = A(nc.psum_tensor("ptr3", [128, 1024], F32))
        pg3 = [A(nc.psum_tensor(f"pg3{p}", [128, 256], F32)) for p in range(2)]
        pu3 = [A(nc.psum_tensor(f"pu3{p}", [128, 256], F32)) for p in range(2)]
        pd3 = [A(nc.psum_tensor(f"pd3{p}", [128, 512], F32)) for p in range(2)]

        @on("sy")
        def _(e):
            e.wait_ge(sem_t, bar3_t)
            e.wait_ge(sem_v, bar3_v)
            e.wait_ge(sem_s, bar3_s)
            e.wait_ge(s_castf, ffn_cast)
            for fi in range(NCH):
                e.dma_start_transpose(wgT[:, fi * HID:(fi + 1) * HID], wgb[:, fi * 128:(fi + 1) * 128]).then_inc(s_w3, 16)
            for fi in range(NCH):
                e.dma_start_transpose(wuT[:, fi * HID:(fi + 1) * HID], wub[:, fi * 128:(fi + 1) * 128]).then_inc(s_w3, 16)
            for hc in range(NHC):
                e.dma_start_transpose(wdT[:, hc * E:(hc + 1) * E], wdb[:, hc * 128:(hc + 1) * 128]).then_inc(s_w3, 16)
        wgu_a = bump(n_w3, (2 * NCH + NHC) * 16)
        wd_a = wgu_a

        x13_a = [0] * NTC3
        ln3_v = [[0, 0] for _ in range(NTC3)]
        tr3_t = {}
        fm3_s = {}
        gmm_t, umm_t, sil_s, act_v = {}, {}, {}, {}
        dmm_t, dev_v = {}, {}
        yst_o = [0] * NTC3

        for tc in range(NTC3):
            tsl = tc % 2

            @on("sy")
            def _(e, tc=tc):
                if tc == 0:
                    e.wait_ge(sem_t, bar3_t)
                    e.wait_ge(sem_v, bar3_v)
                    e.wait_ge(sem_s, bar3_s)
                e.wait_ge(s_xsP[tc % 2][0], x1st_o[tc])
                if tc >= 1:
                    e.wait_ge(sem_v, dev_v[(tc - 1, 1, 1)])
                e.dma_start(x13_t[:, 0:E], x1h[tc * 256:tc * 256 + 128, :]).then_inc(s_x13, 16)
                e.dma_start(x13_t[:, E:2 * E], x1h[tc * 256 + 128:tc * 256 + 256, :]).then_inc(s_x13, 16)
            x13_a[tc] = bump(n_x13, 32)

            for ts in range(2):
                @on("v")
                def _(e, tc=tc, ts=ts):
                    e.wait_ge(s_x13, x13_a[tc])
                    if tc >= 1:
                        e.wait_ge(sem_t, tr3_t[(tc - 1, ts, 3)])
                    ti = tc * 2 + ts
                    e.tensor_scalar(xl3_t[:, ts * E:(ts + 1) * E], x13_t[:, ts * E:(ts + 1) * E],
                                    st3_sb[:, ti * 2:ti * 2 + 1], st3_sb[:, ti * 2 + 1:ti * 2 + 2],
                                    OP.subtract, OP.mult).then_inc(sem_v, 1)
                ln3_v[tc][ts] = bump(nv)

                for r in range(4):
                    @on("t")
                    def _(e, tc=tc, ts=ts, r=r):
                        if r == 0:
                            e.wait_ge(sem_v, ln3_v[tc][ts])
                            if tc >= 1 or ts == 1:
                                pkey = (tc, 0, 3) if ts == 1 else (tc - 1, 1, 3)
                                e.wait_ge(sem_s, fm3_s[pkey])
                        else:
                            e.wait_ge(sem_s, fm3_s[(tc, ts, r - 1)])
                        mm = None
                        for j in (2 * r, 2 * r + 1):
                            mm = e.matmul(ptr3[:, (j % 2) * 512:(j % 2) * 512 + 128],
                                          xl3_t[:, ts * E + j * 128: ts * E + (j + 1) * 128],
                                          id_sb[:, :], start=True, stop=True)
                        mm.then_inc(sem_t, 1)
                    tr3_t[(tc, ts, r)] = bump(nt)

                    @on("s")
                    def _(e, tc=tc, tsl=tsl, ts=ts, r=r):
                        e.wait_ge(sem_t, tr3_t[(tc, ts, r)])
                        if tc >= 2 and ts == 0 and r == 0:
                            e.wait_ge(sem_t, umm_t[(tc - 2, NHC - 1)])
                        d = bass.AP(xf3_t, tsl * 2048 + (2 * r) * 256 + ts * 128,
                                    [[2 * 2048, 128], [256, 2], [1, 128]])
                        s = ptr3[:, :].rearrange("p (c t) -> p c t", t=512)[:, :, 0:128]
                        e.copy(d, s).then_inc(sem_s, 1)
                    fm3_s[(tc, ts, r)] = bump(ns)

            for hc in range(NHC):
                pp = hc % 2

                @on("t")
                def _(e, tc=tc, tsl=tsl, hc=hc, pp=pp):
                    if hc == 0:
                        e.wait_ge(sem_s, fm3_s[(tc, 1, 3)])
                        if tc == 0:
                            e.wait_ge(s_w3, wgu_a)
                        if tc >= 2:
                            e.wait_ge(sem_t, dmm_t[(tc - 2, 1, 1)])
                    if hc >= 2:
                        e.wait_ge(sem_s, sil_s[(tc, hc - 2)])
                    xf = xf3_t[:, tsl * 2048:(tsl + 1) * 2048]
                    mm = None
                    for fi in range(NCH):
                        mm = e.matmul(pg3[pp][:, :],
                                      wgT[:, fi * HID + hc * 128: fi * HID + (hc + 1) * 128],
                                      xf[:, fi * 256:(fi + 1) * 256],
                                      start=(fi == 0), stop=(fi == NCH - 1))
                    mm.then_inc(sem_t, 1)
                gmm_t[(tc, hc)] = bump(nt)

                @on("t")
                def _(e, tc=tc, tsl=tsl, hc=hc, pp=pp):
                    if hc >= 2:
                        e.wait_ge(sem_v, act_v[(tc, hc - 2)])
                    xf = xf3_t[:, tsl * 2048:(tsl + 1) * 2048]
                    mm = None
                    for fi in range(NCH):
                        mm = e.matmul(pu3[pp][:, :],
                                      wuT[:, fi * HID + hc * 128: fi * HID + (hc + 1) * 128],
                                      xf[:, fi * 256:(fi + 1) * 256],
                                      start=(fi == 0), stop=(fi == NCH - 1))
                    mm.then_inc(sem_t, 1)
                umm_t[(tc, hc)] = bump(nt)

                @on("s")
                def _(e, tc=tc, hc=hc, pp=pp):
                    e.wait_ge(sem_t, gmm_t[(tc, hc)])
                    if hc >= 1:
                        e.wait_ge(sem_v, act_v[(tc, hc - 1)])
                    e.activation(ssg_t[:, :], pg3[pp][:, :], AF.Silu, scale=SC127).then_inc(sem_s, 1)
                sil_s[(tc, hc)] = bump(ns)

                @on("v")
                def _(e, tc=tc, hc=hc, pp=pp):
                    e.wait_ge(sem_s, sil_s[(tc, hc)])
                    e.wait_ge(sem_t, umm_t[(tc, hc)])
                    e.tensor_tensor(act_t[:, (tc % 2) * NHC * 256 + hc * 256: (tc % 2) * NHC * 256 + (hc + 1) * 256],
                                    ssg_t[:, :], pu3[pp][:, :], OP.mult).then_inc(sem_v, 1)
                act_v[(tc, hc)] = bump(nv)

            for ts in range(2):
                for fb in range(2):
                    pp = fb

                    @on("t")
                    def _(e, tc=tc, ts=ts, fb=fb, pp=pp):
                        if ts == 0 and fb == 0:
                            e.wait_ge(sem_v, act_v[(tc, NHC - 1)])
                            if tc == 0:
                                e.wait_ge(s_w3, wd_a)
                        prev = (tc, ts - 1, fb) if ts >= 1 else (tc - 1, 1, fb)
                        if prev in dev_v:
                            e.wait_ge(sem_v, dev_v[prev])
                        mm = None
                        for hc in range(NHC):
                            mm = e.matmul(pd3[pp][:, :],
                                          act_t[:, (tc % 2) * NHC * 256 + hc * 256 + ts * 128: (tc % 2) * NHC * 256 + hc * 256 + ts * 128 + 128],
                                          wdT[:, hc * E + fb * 512: hc * E + fb * 512 + 512],
                                          start=(hc == 0), stop=(hc == NHC - 1))
                        mm.then_inc(sem_t, 1)
                    dmm_t[(tc, ts, fb)] = bump(nt)

                    @on("v")
                    def _(e, tc=tc, ts=ts, fb=fb, pp=pp):
                        e.wait_ge(sem_t, dmm_t[(tc, ts, fb)])
                        if fb == 0 and ts == 0 and tc >= 1:
                            e.wait_ge(s_y, yst_o[tc - 1])
                        e.scalar_tensor_tensor(y_t[:, ts * E + fb * 512: ts * E + fb * 512 + 512],
                                               pd3[pp][:, :], SC2,
                                               x13_t[:, ts * E + fb * 512: ts * E + fb * 512 + 512],
                                               OP.mult, OP.add).then_inc(sem_v, 1)
                    dev_v[(tc, ts, fb)] = bump(nv)

                @on("sy")
                def _(e, tc=tc, ts=ts):
                    e.wait_ge(sem_v, dev_v[(tc, ts, 1)])
                    e.dma_start(y[tc * 256 + ts * 128: tc * 256 + (ts + 1) * 128, :],
                                y_t[:, ts * E:(ts + 1) * E]).then_inc(s_y, 16)
                yst_o[tc] = bump(n_y, 16)

        @on("sy")
        def _(e, wy=n_y[0], wd2=n_dbg[0]):
            e.wait_ge(s_y, wy)
            if wd2:
                e.wait_ge(s_dbg, wd2)

        s3.close()

        phase_tot[9] = {k: c[0] for k, (s, c) in streams.items()}
        blk = Eo(nc.Block())
        keys = {"t": blk.tensor, "v": blk.vector, "s": blk.scalar, "sy": blk.sync, "g": blk.gpsimd}
        trunc = phase_tot.get(PH) if (PH in phase_tot and PH < 9) else None
        for key in ["sy", "g", "t", "v", "s"]:
            fns = [fn for k, fn, ph in sched if k == key and (PH >= 9 or ph < PH)]
            if trunc is not None and key == "sy":
                eng_snap = trunc.pop("__eng__", None)

                def fin(e, snap=dict(trunc)):
                    for nm, val in snap.items():
                        if val:
                            e.wait_ge(streams[nm][0], val)
                fns = fns + [fin]

            def run_all(e, fns=fns):
                for fn in fns:
                    fn(e)
            keys[key](run_all)

    return nc


_CACHE = {}


def _get_nc():
    if "nc" not in _CACHE:
        _CACHE["nc"] = build_program()
    return _CACHE["nc"]


def kernel(L0_attn_x, L0_attn_Wq, L0_attn_Wk, L0_attn_Wv, L0_attn_ln_gamma,
           L0_attn_ln_beta, L0_ffn_w_gate, L0_ffn_w_up, L0_ffn_w_down):
    x = np.ascontiguousarray(np.asarray(L0_attn_x, dtype=np.float32))
    wq = np.ascontiguousarray(np.asarray(L0_attn_Wq, dtype=np.float32))
    wk = np.ascontiguousarray(np.asarray(L0_attn_Wk, dtype=np.float32))
    wv = np.ascontiguousarray(np.asarray(L0_attn_Wv, dtype=np.float32))
    g = np.asarray(L0_attn_ln_gamma, dtype=np.float32)
    be = np.asarray(L0_attn_ln_beta, dtype=np.float32)
    wg = np.ascontiguousarray(np.asarray(L0_ffn_w_gate, dtype=np.int32))
    wu = np.ascontiguousarray(np.asarray(L0_ffn_w_up, dtype=np.int32))
    wd = np.ascontiguousarray(np.asarray(L0_ffn_w_down, dtype=np.int32))

    gbc = np.ascontiguousarray(np.broadcast_to(g[None, :], (128, E)))
    bbc = np.ascontiguousarray(np.broadcast_to(be[None, :], (128, E)))
    idn = np.eye(128, dtype=bf16)
    cstv = np.zeros((128, 8), np.float32)
    cstv[:, 0] = EPS
    cstv[:, 1] = -0.5
    cstv[:, 2] = 1.0 / 32.0
    cstv[:, 3] = SC127
    cstv[:, 4] = -1.0

    idx = np.arange(S, dtype=np.float64)
    in_maps = []
    for c in range(8):
        b, hq = divmod(c, 2)
        qlo = hq * QT
        perm = np.concatenate([np.arange(qlo, qlo + QT), np.arange(0, qlo),
                               np.arange(qlo + QT, S)])
        xp = np.ascontiguousarray(x[b][perm])
        dmat = np.abs(idx[perm][:, None] - idx[None, qlo:qlo + QT]) / S
        in_maps.append(dict(
            xs=xp, wqi=wq, wki=wk, wvi=wv, gbc=gbc, bbc=bbc,
            wgi=wg, wui=wu, wdi=wd,
            dst=np.ascontiguousarray(dmat.astype(bf16)),
            idn=idn, cst=cstv,
        ))

    nc = _get_nc()
    res = run_bass_kernel_spmd(nc, in_maps, core_ids=list(range(8)))
    out = np.empty((B, S, E), dtype=np.float32)
    for c in range(8):
        b, hq = divmod(c, 2)
        out[b, hq * QT:(hq + 1) * QT, :] = res.results[c]["y"]
    return out


if __name__ == "__main__":
    nc = build_program()
    print("built ok; instructions:", nc.n_instructions())


# version 09
# speedup vs baseline: 1.3257x; 1.0041x over previous
import os
import numpy as np
import ml_dtypes
from contextlib import ExitStack

import concourse.bass as bass
import concourse.mybir as mybir
from concourse.bass_utils import run_bass_kernel_spmd

F32 = mybir.dt.float32
BF16 = mybir.dt.bfloat16
I32 = mybir.dt.int32
AF = mybir.ActivationFunctionType
OP = mybir.AluOpType
AX = mybir.AxisListType.X
bf16 = ml_dtypes.bfloat16

B, S, E, H = 4, 4096, 1024, 16
HID = 3072
QT = 2048
NCH = E // 128
EPS = 1e-5
SC127 = float(np.float32(1.0) / np.float32(127.0))
SC2 = float(np.float32(SC127) * np.float32(SC127))

QC = 256
NQC = QT // QC
NKC = S // 128
NHP = H // 2
NT1 = int(os.environ.get("KNT1", S // 128))
NOWN = min(QT // 128, NT1)
NTC3 = QT // 256
NHC = HID // 128
VROW = H * 65


def build_program():
    nc = bass.Bass("TRN2", target_bir_lowering=False, detect_race_conditions=False,
                   dynamic_dma_scratch_size=4096)

    xs = nc.dram_tensor("xs", [S, E], F32, kind="ExternalInput")
    wqi = nc.dram_tensor("wqi", [E, E], F32, kind="ExternalInput")
    wki = nc.dram_tensor("wki", [E, E], F32, kind="ExternalInput")
    wvi = nc.dram_tensor("wvi", [E, E], F32, kind="ExternalInput")
    gbc = nc.dram_tensor("gbc", [128, E], F32, kind="ExternalInput")
    bbc = nc.dram_tensor("bbc", [128, E], F32, kind="ExternalInput")
    wgi = nc.dram_tensor("wgi", [HID, E], I32, kind="ExternalInput")
    wui = nc.dram_tensor("wui", [HID, E], I32, kind="ExternalInput")
    wdi = nc.dram_tensor("wdi", [E, HID], I32, kind="ExternalInput")
    dst = nc.dram_tensor("dst", [S, QT], BF16, kind="ExternalInput")
    idn = nc.dram_tensor("idn", [128, 128], BF16, kind="ExternalInput")
    cst = nc.dram_tensor("cst", [128, 8], F32, kind="ExternalInput")
    y = nc.dram_tensor("y", [QT, E], F32, kind="ExternalOutput")

    wqb = nc.dram_tensor("wqb", [E, E], BF16)
    wkb = nc.dram_tensor("wkb", [E, E], BF16)
    wvb = nc.dram_tensor("wvb", [E, E], BF16)
    wgb = nc.dram_tensor("wgb", [HID, E], BF16)
    wub = nc.dram_tensor("wub", [HID, E], BF16)
    wdb = nc.dram_tensor("wdb", [E, HID], BF16)
    qh = nc.dram_tensor("qh", [NCH, 128, QT], BF16)
    xlh = nc.dram_tensor("xlh", [NCH, 128, S], BF16)
    x1h = nc.dram_tensor("x1h", [QT, E], F32)

    PH = float(os.environ.get("KPHASES", "9"))
    KDBG = int(os.environ.get("KDBG", "0"))
    if KDBG:
        dbg_xlh = nc.dram_tensor("dbg_xlh", [NCH, 128, S], BF16, kind="ExternalOutput")
        dbg_qh = nc.dram_tensor("dbg_qh", [NCH, 128, QT], BF16, kind="ExternalOutput")
        dbg_k = nc.dram_tensor("dbg_k", [128, NCH * S], BF16, kind="ExternalOutput")
        dbg_v = nc.dram_tensor("dbg_v", [128, NKC * VROW], BF16, kind="ExternalOutput")
        dbg_x1 = nc.dram_tensor("dbg_x1", [QT, E], F32, kind="ExternalOutput")
    KSUB = int(os.environ.get("KSUB", "9"))
    KSER = int(os.environ.get("KSER", "0"))
    KWT = int(os.environ.get("KWT", "1"))
    cur_phase = [0]
    phase_tot = {}
    sched = []

    def on(key):
        def deco(fn):
            sched.append((key, fn, cur_phase[0]))
        return deco

    with ExitStack() as ctx:
        Eo = ctx.enter_context

        gb_sb = Eo(nc.sbuf_tensor("gb_sb", [128, E], F32, side="left"))
        bb_sb = Eo(nc.sbuf_tensor("bb_sb", [128, E], F32, side="left"))
        id_sb = Eo(nc.sbuf_tensor("id_sb", [128, 128], BF16, side="left"))
        cs_sb = Eo(nc.sbuf_tensor("cs_sb", [128, 8], F32, side="left"))
        st3_sb = Eo(nc.sbuf_tensor("st3_sb", [128, 64], F32, side="left"))
        skv = ExitStack()
        k_sb = skv.enter_context(nc.sbuf_tensor("k_sb", [128, NCH * S], BF16, side="left"))
        v_sb = skv.enter_context(nc.sbuf_tensor("v_sb", [128, NKC * VROW], BF16, side="left"))

        nc.const_aps.register(F32, EPS, cs_sb[:, 0:1])
        nc.const_aps.register(F32, -0.5, cs_sb[:, 1:2])
        nc.const_aps.register(F32, 1.0 / 32.0, cs_sb[:, 2:3])
        nc.const_aps.register(F32, SC127, cs_sb[:, 3:4])
        nc.const_aps.register(F32, -1.0, cs_sb[:, 4:5])

        sem_t = Eo(nc.semaphore("sem_t")); nt = [0]
        sem_v = Eo(nc.semaphore("sem_v")); nv = [0]
        sem_s = Eo(nc.semaphore("sem_s")); ns = [0]
        sem_g = Eo(nc.semaphore("sem_g")); ng = [0]

        streams = {}

        def mksem(name):
            sem = Eo(nc.semaphore(name))
            ctr = [0]
            streams[name] = (sem, ctr)
            return sem, ctr

        s_c0, n_c0 = mksem("s_c0")
        s_castq, n_castq = mksem("s_castq")
        s_castv, n_castv = mksem("s_castv")
        s_castf, n_castf = mksem("s_castf")
        s_wq, n_wq = mksem("s_wq")
        s_wk, n_wk = mksem("s_wk")
        s_wv, n_wv = mksem("s_wv")
        s_w3, n_w3 = mksem("s_w3")
        s_w3u, n_w3u = mksem("s_w3u")
        s_w3d, n_w3d = mksem("s_w3d")
        s_x0, n_x0 = mksem("s_x0")
        s_x1l, n_x1l = mksem("s_x1l")
        s_xlh0, n_xlh0 = mksem("s_xlh0")
        s_xlh1, n_xlh1 = mksem("s_xlh1")
        s_qst, n_qst = mksem("s_qst")
        s_xfb0, n_xfb0 = mksem("s_xfb0")
        s_xfb1, n_xfb1 = mksem("s_xfb1")
        s_qc, n_qc = mksem("s_qc")
        s_dist, n_dist = mksem("s_dist")
        s_xr0, n_xr0 = mksem("s_xr0")
        s_xr1, n_xr1 = mksem("s_xr1")
        s_xs0, n_xs0 = mksem("s_xs0")
        s_xs1, n_xs1 = mksem("s_xs1")
        s_x13, n_x13 = mksem("s_x13")
        s_y, n_y = mksem("s_y")
        s_dbg, n_dbg = mksem("s_dbg")
        s_xP = [(s_x0, n_x0), (s_x1l, n_x1l)]
        s_xlhP = [(s_xlh0, n_xlh0), (s_xlh1, n_xlh1)]
        s_xfbP = [(s_xfb0, n_xfb0), (s_xfb1, n_xfb1)]
        s_xrP = [(s_xr0, n_xr0), (s_xr1, n_xr1)]
        s_xsP = [(s_xs0, n_xs0), (s_xs1, n_xs1)]

        def bump(c, k=1):
            c[0] += k
            return c[0]

        def vec_lnstats(e, src, sq, s1, m_ap, s2_ap):
            e.reduce_sum(s1, src, axis=AX)
            e.tensor_tensor(sq, src, src, OP.mult)
            e.reduce_sum(s2_ap, sq, axis=AX)
            e.tensor_scalar(m_ap, s1, 1.0 / E, None, OP.mult)
            e.tensor_scalar(s2_ap, s2_ap, 1.0 / E, None, OP.mult)
            return e.scalar_tensor_tensor(s2_ap, m_ap, m_ap, s2_ap, OP.mult, OP.subtract)

        def scl_rstd(e, var_neg, r_ap):
            e.activation(r_ap, var_neg, AF.Ln, scale=-1.0, bias=EPS)
            return e.activation(r_ap, r_ap, AF.Exp, scale=-0.5)

        @on("sy")
        def _(e):
            e.dma_start(cs_sb[:, :], cst[:, :]).then_inc(s_c0, 16)
            e.dma_start(id_sb[:, :], idn[:, :]).then_inc(s_c0, 16)
            e.dma_start(gb_sb[:, :], gbc[:, :]).then_inc(s_c0, 16)
            e.dma_start(bb_sb[:, :], bbc[:, :]).then_inc(s_c0, 16)
        p0_a = bump(n_c0, 64)

        @on("g")
        def _(e):
            e.dma_start(wqb[:, :], wqi[:, :]).then_inc(s_castq, 16)
            e.dma_start(wkb[:, :], wki[:, :]).then_inc(s_castq, 16)
            e.dma_start(wvb[:, :], wvi[:, :]).then_inc(s_castv, 16)
            e.dma_start(wgb[:, :], wgi[:, :]).then_inc(s_castf, 16)
            e.dma_start(wub[:, :], wui[:, :]).then_inc(s_castf, 16)
            e.dma_start(wdb[:, :], wdi[:, :]).then_inc(s_castf, 16)
            e.memset(v_sb[:, :].rearrange("p (k h d) -> p k h d", k=NKC, h=H)[:, :, :, 64:65],
                     1.0).then_inc(sem_g, 1)
        memset_g = bump(ng, 1)
        qkv_cast = bump(n_castq, 32)
        castv_c = bump(n_castv, 16)
        ffn_cast = bump(n_castf, 48)
        phase_tot[0.5] = {k: c[0] for k, (s, c) in streams.items()}
        cur_phase[0] = 0.5

        s1a = ExitStack()
        A = s1a.enter_context
        wqT = A(nc.sbuf_tensor("wqT", [128, NCH * E], BF16, side="right"))
        wkT = A(nc.sbuf_tensor("wkT", [128, NCH * E], BF16, side="right"))
        x_t = A(nc.sbuf_tensor("x_t", [128, 2 * E], F32, side="right"))
        sq_t = A(nc.sbuf_tensor("sq_t", [128, E], BF16, side="right"))
        xln_t = A(nc.sbuf_tensor("xln_t", [128, 2 * E], BF16, side="right"))
        xfm_t = A(nc.sbuf_tensor("xfm_t", [128, 2 * 1024], BF16, side="right"))
        qev_t = A(nc.sbuf_tensor("qev_t", [128, 1024], BF16, side="right"))
        stat_t = A(nc.sbuf_tensor("stat_t", [128, 8], F32, side="right"))
        ptr = A(nc.psum_tensor("ptr", [128, 2048], F32))
        pk = A(nc.psum_tensor("pk", [128, 2048], F32))

        @on("sy")
        def _(e):
            if not KWT:
                return
            e.wait_ge(s_castq, qkv_cast)
            for fi in range(NCH):
                e.dma_start_transpose(wkT[:, fi * E:(fi + 1) * E], wkb[:, fi * 128:(fi + 1) * 128]).then_inc(s_wk, 16)
            for fi in range(NCH):
                e.dma_start_transpose(wqT[:, fi * E:(fi + 1) * E], wqb[:, fi * 128:(fi + 1) * 128]).then_inc(s_wq, 16)
        wk_a = bump(n_wk, NCH * 16) if KWT else 0
        wq_a = bump(n_wq, NCH * 16) if KWT else 0

        x_a = [0] * NT1
        stats_v = [0] * NT1
        rstd_s = [0] * NT1
        apply_v = [0] * NT1
        tr_t = [[0, 0] for _ in range(NT1)]
        fm_s = [[0, 0] for _ in range(NT1)]
        xlh_o = [0] * NT1
        kmm_t = [[0, 0] for _ in range(NT1)]
        kev_s = [[0, 0] for _ in range(NT1)]
        qmm_t = [[0, 0] for _ in range(NT1)]
        qev_s = [[0, 0] for _ in range(NT1)]
        qst_o = [0] * NT1

        k3 = k_sb[:, :].rearrange("p (c s) -> p c s", s=S)
        pk3 = pk[:, :].rearrange("p (c t) -> p c t", t=512)
        ptr3v = ptr[:, :].rearrange("p (c t) -> p c t", t=512)

        for i in range(NT1):
            sl = i % 2
            own = i < NOWN

            @on("sy")
            def _(e, i=i, sl=sl):
                if i >= 2:
                    e.wait_ge(sem_v, apply_v[i - 2])
                e.dma_start(x_t[:, sl * E:(sl + 1) * E], xs[i * 128:(i + 1) * 128, :]).then_inc(s_xP[sl][0], 16)
            x_a[i] = bump(s_xP[sl][1], 16)

            @on("v")
            def _(e, i=i, sl=sl):
                e.wait_ge(s_xP[sl][0], x_a[i])
                if i >= 1:
                    e.wait_ge(sem_s, rstd_s[i - 1])
                vec_lnstats(e, x_t[:, sl * E:(sl + 1) * E], sq_t[:, :], stat_t[:, 0:1],
                            stat_t[:, 2:3], stat_t[:, 1:2]).then_inc(sem_v, 1)
            stats_v[i] = bump(nv)

            @on("s")
            def _(e, i=i):
                if i == 0:
                    e.wait_ge(s_c0, p0_a)
                e.wait_ge(sem_v, stats_v[i])
                scl_rstd(e, stat_t[:, 1:2], stat_t[:, 3:4]).then_inc(sem_s, 1)
            rstd_s[i] = bump(ns)

            @on("v")
            def _(e, i=i, sl=sl):
                e.wait_ge(sem_s, rstd_s[i])
                if i >= 2:
                    e.wait_ge(sem_t, tr_t[i - 2][1])
                e.tensor_scalar(xln_t[:, sl * E:(sl + 1) * E], x_t[:, sl * E:(sl + 1) * E],
                                stat_t[:, 2:3], stat_t[:, 3:4], OP.subtract, OP.mult).then_inc(sem_v, 1)
            apply_v[i] = bump(nv)

            if KSUB < 2:
                continue

            fsl = i % 2
            for r in range(2):
                @on("t")
                def _(e, i=i, sl=sl, r=r):
                    if r == 0:
                        if i == 0:
                            e.wait_ge(s_c0, p0_a)
                        e.wait_ge(sem_v, apply_v[i])
                        if i >= 1:
                            e.wait_ge(sem_s, fm_s[i - 1][1])
                    else:
                        e.wait_ge(sem_s, fm_s[i][0])
                    mm = None
                    for j in range(4 * r, 4 * r + 4):
                        mm = e.matmul(ptr[:, (j % 4) * 512:(j % 4) * 512 + 128],
                                      xln_t[:, sl * E + j * 128: sl * E + (j + 1) * 128],
                                      id_sb[:, :], start=True, stop=True)
                    mm.then_inc(sem_t, 1)
                tr_t[i][r] = bump(nt)

                @on("s")
                def _(e, i=i, fsl=fsl, r=r):
                    e.wait_ge(sem_t, tr_t[i][r])
                    if r == 0 and i >= 2:
                        e.wait_ge(sem_t, (qmm_t if i - 2 < NOWN else kmm_t)[i - 2][1])
                        e.wait_ge(s_xlhP[i % 2][0], xlh_o[i - 2])
                    e.copy(xfm_t[:, fsl * 1024 + r * 512: fsl * 1024 + (r + 1) * 512],
                           ptr3v[:, :, 0:128]).then_inc(sem_s, 1)
                fm_s[i][r] = bump(ns)

            @on("sy")
            def _(e, i=i, fsl=fsl):
                e.wait_ge(sem_s, fm_s[i][1])
                src = xfm_t[:, fsl * 1024:(fsl + 1) * 1024].rearrange("p (c t) -> p c t", c=NCH)
                d = xlh[:, :, i * 128:(i + 1) * 128].transpose([1, 0, 2])
                e.dma_start(d, src).then_inc(s_xlhP[i % 2][0], 16)
            xlh_o[i] = bump(s_xlhP[i % 2][1], 16)

            if KSUB < 3:
                continue

            for h in range(2):
                @on("t")
                def _(e, i=i, fsl=fsl, h=h):
                    if h == 0:
                        e.wait_ge(sem_s, fm_s[i][1])
                        e.wait_ge(s_wk, wk_a)
                        if i >= 1:
                            e.wait_ge(sem_s, (qev_s if i - 1 < NOWN else kev_s)[i - 1][1])
                    else:
                        e.wait_ge(sem_s, kev_s[i][0])
                    xf = xfm_t[:, fsl * 1024:(fsl + 1) * 1024]
                    mm = None
                    for fo in range(4 * h, 4 * h + 4):
                        for fi in range(NCH):
                            mm = e.matmul(pk[:, (fo % 4) * 512:(fo % 4) * 512 + 128],
                                          wkT[:, fi * E + fo * 128: fi * E + (fo + 1) * 128],
                                          xf[:, fi * 128:(fi + 1) * 128],
                                          start=(fi == 0), stop=(fi == NCH - 1))
                    mm.then_inc(sem_t, 1)
                kmm_t[i][h] = bump(nt)

                @on("s")
                def _(e, i=i, h=h):
                    e.wait_ge(sem_t, kmm_t[i][h])
                    d = k3[:, 4 * h:4 * h + 4, i * 128:(i + 1) * 128]
                    e.copy(d, pk3[:, :, 0:128]).then_inc(sem_s, 1)
                kev_s[i][h] = bump(ns)

            if own and KSUB >= 4:
                for h in range(2):
                    @on("t")
                    def _(e, i=i, fsl=fsl, h=h):
                        e.wait_ge(sem_s, kev_s[i][1] if h == 0 else qev_s[i][0])
                        if h == 0 and i == 0:
                            e.wait_ge(s_wq, wq_a)
                        xf = xfm_t[:, fsl * 1024:(fsl + 1) * 1024]
                        mm = None
                        for fo in range(4 * h, 4 * h + 4):
                            for fi in range(NCH):
                                mm = e.matmul(pk[:, (fo % 4) * 512:(fo % 4) * 512 + 128],
                                              wqT[:, fi * E + fo * 128: fi * E + (fo + 1) * 128],
                                              xf[:, fi * 128:(fi + 1) * 128],
                                              start=(fi == 0), stop=(fi == NCH - 1))
                        mm.then_inc(sem_t, 1)
                    qmm_t[i][h] = bump(nt)

                    @on("s")
                    def _(e, i=i, h=h):
                        e.wait_ge(sem_t, qmm_t[i][h])
                        if i >= 1 and h == 0:
                            e.wait_ge(s_qst, qst_o[i - 1])
                        e.copy(qev_t[:, h * 512:(h + 1) * 512].rearrange("p (c t) -> p c t", t=128),
                               pk3[:, :, 0:128]).then_inc(sem_s, 1)
                    qev_s[i][h] = bump(ns)

                @on("sy")
                def _(e, i=i):
                    e.wait_ge(sem_s, qev_s[i][1])
                    src = qev_t[:, :].rearrange("p (c t) -> p c t", c=NCH)
                    d = qh[:, :, i * 128:(i + 1) * 128].transpose([1, 0, 2])
                    e.dma_start(d, src).then_inc(s_qst, 16)
                qst_o[i] = bump(n_qst, 16)

        bar1_t = nt[0]
        bar1_v = nv[0]
        bar1_s = ns[0]
        phase_tot[1] = {k: c[0] for k, (s, c) in streams.items()}
        cur_phase[0] = 1

        s1a.close()

        s1b = ExitStack()
        A = s1b.enter_context
        wvT = A(nc.sbuf_tensor("wvT", [128, NCH * E], BF16, side="right"))
        xfb_t = A(nc.sbuf_tensor("xfb_t", [128, 2 * 1024], BF16, side="right"))
        pvA = A(nc.psum_tensor("pvA", [128, 512], F32))
        pvB = A(nc.psum_tensor("pvB", [128, 512], F32))

        v4 = v_sb[:, :].rearrange("p (k h d) -> p k h d", k=NKC, h=H)

        @on("sy")
        def _(e):
            e.wait_ge(sem_t, bar1_t)
            e.wait_ge(sem_v, bar1_v)
            e.wait_ge(sem_s, bar1_s)
            e.wait_ge(s_castv, castv_c)
            for fi in range(NCH):
                e.dma_start_transpose(wvT[:, fi * E:(fi + 1) * E], wvb[:, fi * 128:(fi + 1) * 128]).then_inc(s_wv, 16)
        wv_a = bump(n_wv, NCH * 16)

        xfb_a = [0] * NT1
        vmm_t = [[0, 0] for _ in range(NT1)]
        vev_s = [[0, 0] for _ in range(NT1)]

        for i in range(NT1):
            sl = i % 2

            @on("sy")
            def _(e, i=i, sl=sl):
                e.wait_ge(s_xlhP[i % 2][0], xlh_o[i])
                if i >= 2:
                    e.wait_ge(sem_t, vmm_t[i - 2][1])
                s = xlh[:, :, i * 128:(i + 1) * 128].transpose([1, 0, 2])
                e.dma_start(xfb_t[:, sl * 1024:(sl + 1) * 1024].rearrange("p (c t) -> p c t", c=NCH),
                            s).then_inc(s_xfbP[sl][0], 16)
            xfb_a[i] = bump(s_xfbP[sl][1], 16)

            for h in range(2):
                @on("t")
                def _(e, i=i, sl=sl, h=h):
                    if h == 0:
                        e.wait_ge(s_xfbP[sl][0], xfb_a[i])
                        e.wait_ge(s_wv, wv_a)
                    if i >= 1:
                        e.wait_ge(sem_s, vev_s[i - 1][h])
                    bank = pvA if h == 0 else pvB
                    mm = None
                    for fi in range(NCH):
                        mm = e.matmul(bank[:, :],
                                      xfb_t[:, sl * 1024 + fi * 128: sl * 1024 + (fi + 1) * 128],
                                      wvT[:, fi * E + h * 512: fi * E + h * 512 + 512],
                                      start=(fi == 0), stop=(fi == NCH - 1))
                    mm.then_inc(sem_t, 1)
                vmm_t[i][h] = bump(nt)

                @on("s")
                def _(e, i=i, h=h):
                    e.wait_ge(sem_t, vmm_t[i][h])
                    if i == 0 and h == 0:
                        e.wait_ge(sem_g, memset_g)
                    bank = pvA if h == 0 else pvB
                    d = v4[:, i, 8 * h:8 * h + 8, 0:64]
                    s = bank[:, :].rearrange("p (h d) -> p h d", d=64)
                    e.copy(d, s).then_inc(sem_s, 1)
                vev_s[i][h] = bump(ns)

        bar2_t = nt[0]
        bar2_v = nv[0]
        bar2_s = ns[0]
        phase_tot[2] = {k: c[0] for k, (s, c) in streams.items()}
        cur_phase[0] = 2

        s1b.close()

        s2 = ExitStack()
        A = s2.enter_context
        dist_t = A(nc.sbuf_tensor("dist_t", [128, NKC * QC], BF16, side="right"))
        probs_t = A(nc.sbuf_tensor("probs_t", [128, 4096], BF16, side="right"))
        qc_t = A(nc.sbuf_tensor("qc_t", [128, 2 * 2048], BF16, side="right"))
        o_t = A(nc.sbuf_tensor("o_t", [128, 2048], F32, side="right"))
        xr_t = A(nc.sbuf_tensor("xr_t", [128, 2 * E], F32, side="right"))
        x1_t = A(nc.sbuf_tensor("x1_t", [128, 2 * E], F32, side="right"))
        t1_t = A(nc.sbuf_tensor("t1_t", [128, E], F32, side="right"))
        st2_t = A(nc.sbuf_tensor("st2_t", [128, 16], F32, side="right"))
        pars = A(nc.psum_tensor("pars", [128, 4096], F32))

        NH = H
        qc_a = [0] * NQC
        dist_a = [0] * NQC
        xr_a = [0] * NQC
        qk_t, mult_v, exp_s, av_t, av_last, evac_v = {}, {}, {}, {}, {}, {}
        ln2_v = [0] * NQC
        x1st_o = [0] * NQC

        def av_col(kc):
            return ((kc // 4) % 3) * 1024 + (kc % 2) * 512 + ((kc % 4) // 2) * 256

        for qc in range(NQC):
            qsl = qc % 2

            @on("sy")
            def _(e, qc=qc):
                if qc == 0:
                    e.wait_ge(sem_t, bar2_t)
                    e.wait_ge(sem_v, bar2_v)
                    e.wait_ge(sem_s, bar2_s)
                    e.wait_ge(s_qst, qst_o[NOWN - 1])
                if qc >= 2:
                    e.wait_ge(sem_t, qk_t[(qc - 2, NH - 1, NKC - 1)])
                s = qh[:, :, qc * QC:(qc + 1) * QC].transpose([1, 0, 2])
                e.dma_start(qc_t[:, (qc % 2) * 2048:(qc % 2 + 1) * 2048].rearrange("p (c q) -> p c q", c=NCH),
                            s).then_inc(s_qc, 16)
            qc_a[qc] = bump(n_qc, 16)

            for dg in range(4):
                @on("sy")
                def _(e, qc=qc, dg=dg):
                    if qc >= 1:
                        e.wait_ge(sem_v, mult_v[(qc - 1, NH - 1, 4 * dg + 3)])
                    s = dst[dg * 1024:(dg + 1) * 1024, qc * QC:(qc + 1) * QC]
                    s = s.rearrange("(c k) q -> c k q", k=128).transpose([1, 0, 2])
                    e.dma_start(dist_t[:, dg * 8 * QC:(dg + 1) * 8 * QC].rearrange("p (c q) -> p c q", c=8),
                                s).then_inc(s_dist, 16)
            dist_a[qc] = bump(n_dist, 64)

            @on("sy")
            def _(e, qc=qc, qsl=qsl):
                if qc >= 2:
                    e.wait_ge(sem_v, ln2_v[qc - 2])
                    e.wait_ge(s_xsP[qc % 2][0], x1st_o[qc - 2])
                e.dma_start(xr_t[:, qsl * E:(qsl + 1) * E], xs[qc * QC:qc * QC + 128, :]).then_inc(s_xrP[qsl][0], 16)
                e.dma_start(x1_t[:, qsl * E:(qsl + 1) * E], xs[qc * QC + 128:qc * QC + 256, :]).then_inc(s_xrP[qsl][0], 16)
            xr_a[qc] = bump(s_xrP[qsl][1], 32)

            for h in range(NH):
                Gi = qc * NH + h
                hh = (h % 2) * 64
                g = h // 2

                def em_av(kc, qc=qc, h=h, Gi=Gi):
                    Qi = Gi * 8 + kc // 4

                    @on("t")
                    def _(e, qc=qc, h=h, kc=kc, Qi=Qi, Gi=Gi):
                        if kc % 4 == 0:
                            e.wait_ge(sem_s, exp_s[Qi])
                        if kc == 0 and Gi >= 1:
                            e.wait_ge(sem_v, evac_v[Gi - 1])
                        pcol = (Qi % 4) * 1024 + (kc % 2) * 512 + ((kc % 4) // 2) * 256
                        mm = None
                        for qh2 in range(2):
                            st = probs_t[:, pcol + qh2 * 128: pcol + qh2 * 128 + 128]
                            mv = v_sb[:, kc * VROW + h * 65: kc * VROW + (h + 1) * 65]
                            mm = e.matmul(pars[:, 3072 + qh2 * 512: 3072 + qh2 * 512 + 65], st, mv,
                                          start=(kc == 0), stop=(kc == NKC - 1))
                        mm.then_inc(sem_t, 1)
                    av_t[(qc, h, kc)] = bump(nt)
                    if kc % 4 == 3:
                        av_last[Qi] = av_t[(qc, h, kc)]

                for kc in range(NKC):
                    Qi = Gi * 8 + kc // 4
                    col = av_col(kc)

                    @on("t")
                    def _(e, qc=qc, h=h, kc=kc, col=col, Qi=Qi, hh=hh, g=g):
                        if kc == 0 and h == 0:
                            e.wait_ge(s_qc, qc_a[qc])
                        if kc % 4 == 0 and Qi >= 3:
                            e.wait_ge(sem_s, exp_s[Qi - 3])
                        st = k_sb[hh:hh + 64, g * S + kc * 128: g * S + kc * 128 + 128]
                        mv = qc_t[hh:hh + 64, (qc % 2) * 2048 + g * QC: (qc % 2) * 2048 + (g + 1) * QC]
                        e.matmul(pars[:, col: col + QC], st, mv, start=True, stop=True,
                                 tile_position=(hh, 0)).then_inc(sem_t, 1)
                    qk_t[(qc, h, kc)] = bump(nt)

                    if kc % 2 == 1:
                        p = kc // 2

                        @on("v")
                        def _(e, qc=qc, h=h, p=p, kc=kc):
                            e.wait_ge(sem_t, qk_t[(qc, h, kc)])
                            if h == 0 and kc == 1:
                                e.wait_ge(s_dist, dist_a[qc])
                            base = ((kc // 4) % 3) * 1024 + (p % 2) * 256
                            sc2 = bass.AP(pars, base, [[4096, 128], [512, 2], [1, QC]])
                            rep = bass.AP(dist_t, (2 * p) * QC, [[NKC * QC, 128], [QC, 2], [1, QC]])
                            e.tensor_tensor(sc2, sc2, rep, OP.mult).then_inc(sem_v, 1)
                        mult_v[(qc, h, p)] = bump(nv)

                    if kc % 4 == 3:
                        @on("s")
                        def _(e, qc=qc, h=h, kc=kc, Qi=Qi):
                            e.wait_ge(sem_v, mult_v[(qc, h, kc // 2)])
                            if Qi >= 4:
                                e.wait_ge(sem_t, av_last[Qi - 4])
                            qs3 = (kc // 4) % 3
                            src = bass.AP(pars, qs3 * 1024, [[4096, 128], [512, 2], [256, 2], [1, QC]])
                            dsx = bass.AP(probs_t, (Qi % 4) * 1024, [[4096, 128], [512, 2], [256, 2], [1, QC]])
                            e.activation(dsx, src, AF.Exp, scale=1.0 / 32.0).then_inc(sem_s, 1)
                        exp_s[Qi] = bump(ns)
                        if kc // 4 >= 2:
                            for k2 in range(4 * (kc // 4 - 2), 4 * (kc // 4 - 2) + 4):
                                em_av(k2)

                for k2 in range(NKC - 8, NKC):
                    em_av(k2)

                @on("v")
                def _(e, qc=qc, h=h, Gi=Gi, w=av_t[(qc, h, NKC - 1)], wrc=nv[0] + 1):
                    e.wait_ge(sem_t, w)
                    if h == 0 and qc >= 1:
                        e.wait_ge(sem_v, ln2_v[qc - 1])
                    src = bass.AP(pars, 3072 + 64, [[4096, 128], [512, 2], [1, 1]])
                    e.reciprocal(st2_t[:, 8:10], src).then_inc(sem_v, 1)
                    e.wait_ge(sem_v, wrc)
                    mm = None
                    for qh2 in range(2):
                        mm = e.tensor_scalar(o_t[:, qh2 * 1024 + h * 64: qh2 * 1024 + (h + 1) * 64],
                                             pars[:, 3072 + qh2 * 512: 3072 + qh2 * 512 + 64],
                                             st2_t[:, 8 + qh2: 9 + qh2], None, OP.mult)
                    mm.then_inc(sem_v, 1)
                bump(nv)
                evac_v[Gi] = bump(nv)

            ln2sub = [0, 0]
            for ts in range(2):
                @on("v")
                def _(e, qc=qc, ts=ts):
                    if qc == 0 and ts == 0:
                        e.wait_ge(s_c0, p0_a)
                    e.wait_ge(s_xrP[qc % 2][0], xr_a[qc])
                    src = o_t[:, ts * 1024:(ts + 1) * 1024]
                    vec_lnstats(e, src, t1_t[:, :], st2_t[:, 0:1],
                                st2_t[:, 2:3], st2_t[:, 1:2]).then_inc(sem_v, 1)
                st2v = bump(nv)

                @on("s")
                def _(e, w=st2v):
                    e.wait_ge(sem_v, w)
                    scl_rstd(e, st2_t[:, 1:2], st2_t[:, 3:4]).then_inc(sem_s, 1)
                st2s = bump(ns)

                @on("v")
                def _(e, qc=qc, qsl=qsl, ts=ts, w=st2s):
                    e.wait_ge(sem_s, w)
                    src = o_t[:, ts * 1024:(ts + 1) * 1024]
                    xbuf = xr_t if ts == 0 else x1_t
                    xap = xbuf[:, qsl * E:(qsl + 1) * E]
                    e.tensor_scalar(t1_t[:, :], src, st2_t[:, 2:3], st2_t[:, 3:4], OP.subtract, OP.mult)
                    e.tensor_tensor(t1_t[:, :], t1_t[:, :], gb_sb[:, :], OP.mult)
                    e.tensor_tensor(t1_t[:, :], t1_t[:, :], bb_sb[:, :], OP.add)
                    e.tensor_tensor(xap, t1_t[:, :], xap, OP.add)
                    ti = qc * 2 + ts
                    vec_lnstats(e, xap, t1_t[:, :], st2_t[:, 4:5],
                                st3_sb[:, ti * 2: ti * 2 + 1], st2_t[:, 5:6]).then_inc(sem_v, 1)
                st3v = bump(nv)
                ln2sub[ts] = st3v

                @on("s")
                def _(e, qc=qc, ts=ts, w=st3v):
                    e.wait_ge(sem_v, w)
                    ti = qc * 2 + ts
                    scl_rstd(e, st2_t[:, 5:6], st3_sb[:, ti * 2 + 1: ti * 2 + 2]).then_inc(sem_s, 1)
                bump(ns)

                @on("sy")
                def _(e, qc=qc, qsl=qsl, ts=ts, w=st3v):
                    e.wait_ge(sem_v, w)
                    xbuf = xr_t if ts == 0 else x1_t
                    e.dma_start(x1h[qc * QC + ts * 128: qc * QC + (ts + 1) * 128, :],
                                xbuf[:, qsl * E:(qsl + 1) * E]).then_inc(s_xsP[qc % 2][0], 16)
                x1st_o[qc] = bump(s_xsP[qc % 2][1], 16)
            ln2_v[qc] = ln2sub[1]

        bar3_t = nt[0]
        bar3_v = nv[0]
        bar3_s = ns[0]

        if KDBG:
            @on("sy")
            def _(e):
                e.wait_ge(sem_t, bar3_t)
                e.wait_ge(sem_v, bar3_v)
                e.wait_ge(sem_s, bar3_s)
                e.dma_start(dbg_k[:, :], k_sb[:, :]).then_inc(s_dbg, 16)
                e.dma_start(dbg_v[:, :], v_sb[:, :]).then_inc(s_dbg, 16)
                e.dma_start(dbg_xlh[:, :, :], xlh[:, :, :]).then_inc(s_dbg, 16)
                e.dma_start(dbg_qh[:, :, :], qh[:, :, :]).then_inc(s_dbg, 16)
                e.dma_start(dbg_x1[:, :], x1h[:, :]).then_inc(s_dbg, 16)
            bump(n_dbg, 80)
        phase_tot[3] = {k: c[0] for k, (s, c) in streams.items()}
        cur_phase[0] = 3

        s2.close()
        skv.close()

        s3 = ExitStack()
        A = s3.enter_context
        wgT = A(nc.sbuf_tensor("wgT", [128, NCH * HID], BF16, side="right"))
        wuT = A(nc.sbuf_tensor("wuT", [128, NCH * HID], BF16, side="right"))
        wdT = A(nc.sbuf_tensor("wdT", [128, NHC * E], BF16, side="right"))
        x13_t = A(nc.sbuf_tensor("x13_t", [128, 2 * E], F32, side="right"))
        xl3_t = A(nc.sbuf_tensor("xl3_t", [128, 2 * E], BF16, side="right"))
        xf3_t = A(nc.sbuf_tensor("xf3_t", [128, 2 * 2048], BF16, side="right"))
        act_t = A(nc.sbuf_tensor("act_t", [128, 2 * NHC * 256], BF16, side="right"))
        ssg_t = A(nc.sbuf_tensor("ssg_t", [128, 256], F32, side="right"))
        y_t = A(nc.sbuf_tensor("y_t", [128, 2 * E], F32, side="right"))
        ptr3 = A(nc.psum_tensor("ptr3", [128, 1024], F32))
        pg3 = [A(nc.psum_tensor(f"pg3{p}", [128, 256], F32)) for p in range(2)]
        pu3 = [A(nc.psum_tensor(f"pu3{p}", [128, 256], F32)) for p in range(2)]
        pd3 = [A(nc.psum_tensor(f"pd3{p}", [128, 512], F32)) for p in range(2)]

        @on("sy")
        def _(e):
            e.wait_ge(sem_t, bar3_t)
            e.wait_ge(sem_v, bar3_v)
            e.wait_ge(sem_s, bar3_s)
            e.wait_ge(s_castf, ffn_cast)
            for fi in range(NCH):
                e.dma_start_transpose(wgT[:, fi * HID:(fi + 1) * HID], wgb[:, fi * 128:(fi + 1) * 128]).then_inc(s_w3, 16)
            for fi in range(NCH):
                e.dma_start_transpose(wuT[:, fi * HID:(fi + 1) * HID], wub[:, fi * 128:(fi + 1) * 128]).then_inc(s_w3u, 16)
            for hc in range(NHC):
                e.dma_start_transpose(wdT[:, hc * E:(hc + 1) * E], wdb[:, hc * 128:(hc + 1) * 128]).then_inc(s_w3d, 16)
        wg_a = bump(n_w3, NCH * 16)
        wu_a = bump(n_w3u, NCH * 16)
        wd_a = bump(n_w3d, NHC * 16)

        x13_a = [0] * NTC3
        ln3_v = [[0, 0] for _ in range(NTC3)]
        tr3_t = {}
        fm3_s = {}
        gmm_t, umm_t, sil_s, act_v = {}, {}, {}, {}
        dmm_t, dev_v = {}, {}
        yst_o = [0] * NTC3

        for tc in range(NTC3):
            tsl = tc % 2

            @on("sy")
            def _(e, tc=tc):
                if tc == 0:
                    e.wait_ge(sem_t, bar3_t)
                    e.wait_ge(sem_v, bar3_v)
                    e.wait_ge(sem_s, bar3_s)
                e.wait_ge(s_xsP[tc % 2][0], x1st_o[tc])
                if tc >= 1:
                    e.wait_ge(sem_v, dev_v[(tc - 1, 1, 1)])
                e.dma_start(x13_t[:, 0:E], x1h[tc * 256:tc * 256 + 128, :]).then_inc(s_x13, 16)
                e.dma_start(x13_t[:, E:2 * E], x1h[tc * 256 + 128:tc * 256 + 256, :]).then_inc(s_x13, 16)
            x13_a[tc] = bump(n_x13, 32)

            for ts in range(2):
                @on("v")
                def _(e, tc=tc, ts=ts):
                    e.wait_ge(s_x13, x13_a[tc])
                    if tc >= 1:
                        e.wait_ge(sem_t, tr3_t[(tc - 1, ts, 3)])
                    ti = tc * 2 + ts
                    e.tensor_scalar(xl3_t[:, ts * E:(ts + 1) * E], x13_t[:, ts * E:(ts + 1) * E],
                                    st3_sb[:, ti * 2:ti * 2 + 1], st3_sb[:, ti * 2 + 1:ti * 2 + 2],
                                    OP.subtract, OP.mult).then_inc(sem_v, 1)
                ln3_v[tc][ts] = bump(nv)

                for r in range(4):
                    @on("t")
                    def _(e, tc=tc, ts=ts, r=r):
                        if r == 0:
                            e.wait_ge(sem_v, ln3_v[tc][ts])
                            if tc >= 1 or ts == 1:
                                pkey = (tc, 0, 3) if ts == 1 else (tc - 1, 1, 3)
                                e.wait_ge(sem_s, fm3_s[pkey])
                        else:
                            e.wait_ge(sem_s, fm3_s[(tc, ts, r - 1)])
                        mm = None
                        for j in (2 * r, 2 * r + 1):
                            mm = e.matmul(ptr3[:, (j % 2) * 512:(j % 2) * 512 + 128],
                                          xl3_t[:, ts * E + j * 128: ts * E + (j + 1) * 128],
                                          id_sb[:, :], start=True, stop=True)
                        mm.then_inc(sem_t, 1)
                    tr3_t[(tc, ts, r)] = bump(nt)

                    @on("s")
                    def _(e, tc=tc, tsl=tsl, ts=ts, r=r):
                        e.wait_ge(sem_t, tr3_t[(tc, ts, r)])
                        if tc >= 2 and ts == 0 and r == 0:
                            e.wait_ge(sem_t, umm_t[(tc - 2, NHC - 1)])
                        d = bass.AP(xf3_t, tsl * 2048 + (2 * r) * 256 + ts * 128,
                                    [[2 * 2048, 128], [256, 2], [1, 128]])
                        s = ptr3[:, :].rearrange("p (c t) -> p c t", t=512)[:, :, 0:128]
                        e.copy(d, s).then_inc(sem_s, 1)
                    fm3_s[(tc, ts, r)] = bump(ns)

            for hc in range(NHC):
                pp = hc % 2

                @on("t")
                def _(e, tc=tc, tsl=tsl, hc=hc, pp=pp):
                    if hc == 0:
                        e.wait_ge(sem_s, fm3_s[(tc, 1, 3)])
                        if tc == 0:
                            e.wait_ge(s_w3, wg_a)
                        if tc >= 2:
                            e.wait_ge(sem_t, dmm_t[(tc - 2, 1, 1)])
                    if hc >= 2:
                        e.wait_ge(sem_s, sil_s[(tc, hc - 2)])
                    xf = xf3_t[:, tsl * 2048:(tsl + 1) * 2048]
                    mm = None
                    for fi in range(NCH):
                        mm = e.matmul(pg3[pp][:, :],
                                      wgT[:, fi * HID + hc * 128: fi * HID + (hc + 1) * 128],
                                      xf[:, fi * 256:(fi + 1) * 256],
                                      start=(fi == 0), stop=(fi == NCH - 1))
                    mm.then_inc(sem_t, 1)
                gmm_t[(tc, hc)] = bump(nt)

                @on("t")
                def _(e, tc=tc, tsl=tsl, hc=hc, pp=pp):
                    if hc == 0 and tc == 0:
                        e.wait_ge(s_w3u, wu_a)
                    if hc >= 2:
                        e.wait_ge(sem_v, act_v[(tc, hc - 2)])
                    xf = xf3_t[:, tsl * 2048:(tsl + 1) * 2048]
                    mm = None
                    for fi in range(NCH):
                        mm = e.matmul(pu3[pp][:, :],
                                      wuT[:, fi * HID + hc * 128: fi * HID + (hc + 1) * 128],
                                      xf[:, fi * 256:(fi + 1) * 256],
                                      start=(fi == 0), stop=(fi == NCH - 1))
                    mm.then_inc(sem_t, 1)
                umm_t[(tc, hc)] = bump(nt)

                @on("s")
                def _(e, tc=tc, hc=hc, pp=pp):
                    e.wait_ge(sem_t, gmm_t[(tc, hc)])
                    if hc >= 1:
                        e.wait_ge(sem_v, act_v[(tc, hc - 1)])
                    e.activation(ssg_t[:, :], pg3[pp][:, :], AF.Silu, scale=SC127).then_inc(sem_s, 1)
                sil_s[(tc, hc)] = bump(ns)

                @on("v")
                def _(e, tc=tc, hc=hc, pp=pp):
                    e.wait_ge(sem_s, sil_s[(tc, hc)])
                    e.wait_ge(sem_t, umm_t[(tc, hc)])
                    e.tensor_tensor(act_t[:, (tc % 2) * NHC * 256 + hc * 256: (tc % 2) * NHC * 256 + (hc + 1) * 256],
                                    ssg_t[:, :], pu3[pp][:, :], OP.mult).then_inc(sem_v, 1)
                act_v[(tc, hc)] = bump(nv)

            for ts in range(2):
                for fb in range(2):
                    pp = fb

                    @on("t")
                    def _(e, tc=tc, ts=ts, fb=fb, pp=pp):
                        if ts == 0 and fb == 0:
                            e.wait_ge(sem_v, act_v[(tc, NHC - 1)])
                            if tc == 0:
                                e.wait_ge(s_w3d, wd_a)
                        prev = (tc, ts - 1, fb) if ts >= 1 else (tc - 1, 1, fb)
                        if prev in dev_v:
                            e.wait_ge(sem_v, dev_v[prev])
                        mm = None
                        for hc in range(NHC):
                            mm = e.matmul(pd3[pp][:, :],
                                          act_t[:, (tc % 2) * NHC * 256 + hc * 256 + ts * 128: (tc % 2) * NHC * 256 + hc * 256 + ts * 128 + 128],
                                          wdT[:, hc * E + fb * 512: hc * E + fb * 512 + 512],
                                          start=(hc == 0), stop=(hc == NHC - 1))
                        mm.then_inc(sem_t, 1)
                    dmm_t[(tc, ts, fb)] = bump(nt)

                    @on("v")
                    def _(e, tc=tc, ts=ts, fb=fb, pp=pp):
                        e.wait_ge(sem_t, dmm_t[(tc, ts, fb)])
                        if fb == 0 and ts == 0 and tc >= 1:
                            e.wait_ge(s_y, yst_o[tc - 1])
                        e.scalar_tensor_tensor(y_t[:, ts * E + fb * 512: ts * E + fb * 512 + 512],
                                               pd3[pp][:, :], SC2,
                                               x13_t[:, ts * E + fb * 512: ts * E + fb * 512 + 512],
                                               OP.mult, OP.add).then_inc(sem_v, 1)
                    dev_v[(tc, ts, fb)] = bump(nv)

                @on("sy")
                def _(e, tc=tc, ts=ts):
                    e.wait_ge(sem_v, dev_v[(tc, ts, 1)])
                    e.dma_start(y[tc * 256 + ts * 128: tc * 256 + (ts + 1) * 128, :],
                                y_t[:, ts * E:(ts + 1) * E]).then_inc(s_y, 16)
                yst_o[tc] = bump(n_y, 16)

        @on("sy")
        def _(e, wy=n_y[0], wd2=n_dbg[0]):
            e.wait_ge(s_y, wy)
            if wd2:
                e.wait_ge(s_dbg, wd2)

        s3.close()

        phase_tot[9] = {k: c[0] for k, (s, c) in streams.items()}
        blk = Eo(nc.Block())
        keys = {"t": blk.tensor, "v": blk.vector, "s": blk.scalar, "sy": blk.sync, "g": blk.gpsimd}
        trunc = phase_tot.get(PH) if (PH in phase_tot and PH < 9) else None
        for key in ["sy", "g", "t", "v", "s"]:
            fns = [fn for k, fn, ph in sched if k == key and (PH >= 9 or ph < PH)]
            if trunc is not None and key == "sy":
                eng_snap = trunc.pop("__eng__", None)

                def fin(e, snap=dict(trunc)):
                    for nm, val in snap.items():
                        if val:
                            e.wait_ge(streams[nm][0], val)
                fns = fns + [fin]

            def run_all(e, fns=fns):
                for fn in fns:
                    fn(e)
            keys[key](run_all)

    return nc


_CACHE = {}


def _get_nc():
    if "nc" not in _CACHE:
        _CACHE["nc"] = build_program()
    return _CACHE["nc"]


def kernel(L0_attn_x, L0_attn_Wq, L0_attn_Wk, L0_attn_Wv, L0_attn_ln_gamma,
           L0_attn_ln_beta, L0_ffn_w_gate, L0_ffn_w_up, L0_ffn_w_down):
    x = np.ascontiguousarray(np.asarray(L0_attn_x, dtype=np.float32))
    wq = np.ascontiguousarray(np.asarray(L0_attn_Wq, dtype=np.float32))
    wk = np.ascontiguousarray(np.asarray(L0_attn_Wk, dtype=np.float32))
    wv = np.ascontiguousarray(np.asarray(L0_attn_Wv, dtype=np.float32))
    g = np.asarray(L0_attn_ln_gamma, dtype=np.float32)
    be = np.asarray(L0_attn_ln_beta, dtype=np.float32)
    wg = np.ascontiguousarray(np.asarray(L0_ffn_w_gate, dtype=np.int32))
    wu = np.ascontiguousarray(np.asarray(L0_ffn_w_up, dtype=np.int32))
    wd = np.ascontiguousarray(np.asarray(L0_ffn_w_down, dtype=np.int32))

    gbc = np.ascontiguousarray(np.broadcast_to(g[None, :], (128, E)))
    bbc = np.ascontiguousarray(np.broadcast_to(be[None, :], (128, E)))
    idn = np.eye(128, dtype=bf16)
    cstv = np.zeros((128, 8), np.float32)
    cstv[:, 0] = EPS
    cstv[:, 1] = -0.5
    cstv[:, 2] = 1.0 / 32.0
    cstv[:, 3] = SC127
    cstv[:, 4] = -1.0

    idx = np.arange(S, dtype=np.float64)
    in_maps = []
    for c in range(8):
        b, hq = divmod(c, 2)
        qlo = hq * QT
        perm = np.concatenate([np.arange(qlo, qlo + QT), np.arange(0, qlo),
                               np.arange(qlo + QT, S)])
        xp = np.ascontiguousarray(x[b][perm])
        dmat = np.abs(idx[perm][:, None] - idx[None, qlo:qlo + QT]) / S
        in_maps.append(dict(
            xs=xp, wqi=wq, wki=wk, wvi=wv, gbc=gbc, bbc=bbc,
            wgi=wg, wui=wu, wdi=wd,
            dst=np.ascontiguousarray(dmat.astype(bf16)),
            idn=idn, cst=cstv,
        ))

    nc = _get_nc()
    res = run_bass_kernel_spmd(nc, in_maps, core_ids=list(range(8)))
    out = np.empty((B, S, E), dtype=np.float32)
    for c in range(8):
        b, hq = divmod(c, 2)
        out[b, hq * QT:(hq + 1) * QT, :] = res.results[c]["y"]
    return out


if __name__ == "__main__":
    nc = build_program()
    print("built ok; instructions:", nc.n_instructions())


# version 10
# speedup vs baseline: 1.3651x; 1.0297x over previous
import os
import numpy as np
import ml_dtypes
from contextlib import ExitStack

import concourse.bass as bass
import concourse.mybir as mybir
from concourse.bass_utils import run_bass_kernel_spmd

F32 = mybir.dt.float32
BF16 = mybir.dt.bfloat16
I32 = mybir.dt.int32
AF = mybir.ActivationFunctionType
OP = mybir.AluOpType
AX = mybir.AxisListType.X
bf16 = ml_dtypes.bfloat16

B, S, E, H = 4, 4096, 1024, 16
HID = 3072
QT = 2048
NCH = E // 128
EPS = 1e-5
SC127 = float(np.float32(1.0) / np.float32(127.0))
SC2 = float(np.float32(SC127) * np.float32(SC127))

QC = 256
NQC = QT // QC
NKC = S // 128
NHP = H // 2
NT1 = int(os.environ.get("KNT1", S // 128))
NOWN = min(QT // 128, NT1)
NTC3 = QT // 256
NHC = HID // 128
VROW = H * 65


def build_program():
    nc = bass.Bass("TRN2", target_bir_lowering=False, detect_race_conditions=False,
                   dynamic_dma_scratch_size=4096)

    xs = nc.dram_tensor("xs", [S, E], F32, kind="ExternalInput")
    wqi = nc.dram_tensor("wqi", [E, E], F32, kind="ExternalInput")
    wki = nc.dram_tensor("wki", [E, E], F32, kind="ExternalInput")
    wvi = nc.dram_tensor("wvi", [E, E], F32, kind="ExternalInput")
    gbc = nc.dram_tensor("gbc", [128, E], F32, kind="ExternalInput")
    bbc = nc.dram_tensor("bbc", [128, E], F32, kind="ExternalInput")
    wgi = nc.dram_tensor("wgi", [HID, E], I32, kind="ExternalInput")
    wui = nc.dram_tensor("wui", [HID, E], I32, kind="ExternalInput")
    wdi = nc.dram_tensor("wdi", [E, HID], I32, kind="ExternalInput")
    dst = nc.dram_tensor("dst", [NQC * 4, 128, 8 * QC], BF16, kind="ExternalInput")
    idn = nc.dram_tensor("idn", [128, 128], BF16, kind="ExternalInput")
    cst = nc.dram_tensor("cst", [128, 8], F32, kind="ExternalInput")
    y = nc.dram_tensor("y", [QT, E], F32, kind="ExternalOutput")

    wqb = nc.dram_tensor("wqb", [E, E], BF16)
    wkb = nc.dram_tensor("wkb", [E, E], BF16)
    wvb = nc.dram_tensor("wvb", [E, E], BF16)
    wgb = nc.dram_tensor("wgb", [HID, E], BF16)
    wub = nc.dram_tensor("wub", [HID, E], BF16)
    wdb = nc.dram_tensor("wdb", [E, HID], BF16)
    qh = nc.dram_tensor("qh", [NCH, 128, QT], BF16)
    xlh = nc.dram_tensor("xlh", [NCH, 128, S], BF16)
    x1h = nc.dram_tensor("x1h", [QT, E], F32)

    PH = float(os.environ.get("KPHASES", "9"))
    KDBG = int(os.environ.get("KDBG", "0"))
    if KDBG:
        dbg_xlh = nc.dram_tensor("dbg_xlh", [NCH, 128, S], BF16, kind="ExternalOutput")
        dbg_qh = nc.dram_tensor("dbg_qh", [NCH, 128, QT], BF16, kind="ExternalOutput")
        dbg_k = nc.dram_tensor("dbg_k", [128, NCH * S], BF16, kind="ExternalOutput")
        dbg_v = nc.dram_tensor("dbg_v", [128, NKC * VROW], BF16, kind="ExternalOutput")
        dbg_x1 = nc.dram_tensor("dbg_x1", [QT, E], F32, kind="ExternalOutput")
    KSUB = int(os.environ.get("KSUB", "9"))
    KSER = int(os.environ.get("KSER", "0"))
    KWT = int(os.environ.get("KWT", "1"))
    cur_phase = [0]
    phase_tot = {}
    sched = []

    def on(key):
        def deco(fn):
            sched.append((key, fn, cur_phase[0]))
        return deco

    with ExitStack() as ctx:
        Eo = ctx.enter_context

        gb_sb = Eo(nc.sbuf_tensor("gb_sb", [128, E], F32, side="left"))
        bb_sb = Eo(nc.sbuf_tensor("bb_sb", [128, E], F32, side="left"))
        id_sb = Eo(nc.sbuf_tensor("id_sb", [128, 128], BF16, side="left"))
        cs_sb = Eo(nc.sbuf_tensor("cs_sb", [128, 8], F32, side="left"))
        st3_sb = Eo(nc.sbuf_tensor("st3_sb", [128, 64], F32, side="left"))
        skv = ExitStack()
        k_sb = skv.enter_context(nc.sbuf_tensor("k_sb", [128, NCH * S], BF16, side="left"))
        v_sb = skv.enter_context(nc.sbuf_tensor("v_sb", [128, NKC * VROW], BF16, side="left"))

        nc.const_aps.register(F32, EPS, cs_sb[:, 0:1])
        nc.const_aps.register(F32, -0.5, cs_sb[:, 1:2])
        nc.const_aps.register(F32, 1.0 / 32.0, cs_sb[:, 2:3])
        nc.const_aps.register(F32, SC127, cs_sb[:, 3:4])
        nc.const_aps.register(F32, -1.0, cs_sb[:, 4:5])

        sem_t = Eo(nc.semaphore("sem_t")); nt = [0]
        sem_v = Eo(nc.semaphore("sem_v")); nv = [0]
        sem_s = Eo(nc.semaphore("sem_s")); ns = [0]
        sem_g = Eo(nc.semaphore("sem_g")); ng = [0]

        streams = {}

        def mksem(name):
            sem = Eo(nc.semaphore(name))
            ctr = [0]
            streams[name] = (sem, ctr)
            return sem, ctr

        s_c0, n_c0 = mksem("s_c0")
        s_castq, n_castq = mksem("s_castq")
        s_castv, n_castv = mksem("s_castv")
        s_castf, n_castf = mksem("s_castf")
        s_wq, n_wq = mksem("s_wq")
        s_wk, n_wk = mksem("s_wk")
        s_wv, n_wv = mksem("s_wv")
        s_w3, n_w3 = mksem("s_w3")
        s_w3u, n_w3u = mksem("s_w3u")
        s_w3b, n_w3b = mksem("s_w3b")
        s_w3c, n_w3c = mksem("s_w3c")
        s_w3d, n_w3d = mksem("s_w3d")
        s_x0, n_x0 = mksem("s_x0")
        s_x1l, n_x1l = mksem("s_x1l")
        s_xlh0, n_xlh0 = mksem("s_xlh0")
        s_xlh1, n_xlh1 = mksem("s_xlh1")
        s_qst, n_qst = mksem("s_qst")
        s_xfb0, n_xfb0 = mksem("s_xfb0")
        s_xfb1, n_xfb1 = mksem("s_xfb1")
        s_qc, n_qc = mksem("s_qc")
        s_distG = [mksem(f"s_dist{dg}") for dg in range(4)]
        s_xr0, n_xr0 = mksem("s_xr0")
        s_xr1, n_xr1 = mksem("s_xr1")
        s_xs0, n_xs0 = mksem("s_xs0")
        s_xs1, n_xs1 = mksem("s_xs1")
        s_x13, n_x13 = mksem("s_x13")
        s_y, n_y = mksem("s_y")
        s_dbg, n_dbg = mksem("s_dbg")
        s_xP = [(s_x0, n_x0), (s_x1l, n_x1l)]
        s_xlhP = [(s_xlh0, n_xlh0), (s_xlh1, n_xlh1)]
        s_xfbP = [(s_xfb0, n_xfb0), (s_xfb1, n_xfb1)]
        s_xrP = [(s_xr0, n_xr0), (s_xr1, n_xr1)]
        s_xsP = [(s_xs0, n_xs0), (s_xs1, n_xs1)]

        def bump(c, k=1):
            c[0] += k
            return c[0]

        def vec_lnstats(e, src, sq, s1, m_ap, s2_ap):
            e.reduce_sum(s1, src, axis=AX)
            e.tensor_tensor(sq, src, src, OP.mult)
            e.reduce_sum(s2_ap, sq, axis=AX)
            e.tensor_scalar(m_ap, s1, 1.0 / E, None, OP.mult)
            e.tensor_scalar(s2_ap, s2_ap, 1.0 / E, None, OP.mult)
            return e.scalar_tensor_tensor(s2_ap, m_ap, m_ap, s2_ap, OP.mult, OP.subtract)

        def scl_rstd(e, var_neg, r_ap):
            e.activation(r_ap, var_neg, AF.Ln, scale=-1.0, bias=EPS)
            return e.activation(r_ap, r_ap, AF.Exp, scale=-0.5)

        @on("sy")
        def _(e):
            e.dma_start(cs_sb[:, :], cst[:, :]).then_inc(s_c0, 16)
            e.dma_start(id_sb[:, :], idn[:, :]).then_inc(s_c0, 16)
            e.dma_start(gb_sb[:, :], gbc[:, :]).then_inc(s_c0, 16)
            e.dma_start(bb_sb[:, :], bbc[:, :]).then_inc(s_c0, 16)
        p0_a = bump(n_c0, 64)

        @on("g")
        def _(e):
            e.dma_start(wqb[:, :], wqi[:, :]).then_inc(s_castq, 16)
            e.dma_start(wkb[:, :], wki[:, :]).then_inc(s_castq, 16)
            e.dma_start(wvb[:, :], wvi[:, :]).then_inc(s_castv, 16)
            e.dma_start(wgb[:, :], wgi[:, :]).then_inc(s_castf, 16)
            e.dma_start(wub[:, :], wui[:, :]).then_inc(s_castf, 16)
            e.dma_start(wdb[:, :], wdi[:, :]).then_inc(s_castf, 16)
            e.memset(v_sb[:, :].rearrange("p (k h d) -> p k h d", k=NKC, h=H)[:, :, :, 64:65],
                     1.0).then_inc(sem_g, 1)
        memset_g = bump(ng, 1)
        qkv_cast = bump(n_castq, 32)
        castv_c = bump(n_castv, 16)
        ffn_cast = bump(n_castf, 48)
        phase_tot[0.5] = {k: c[0] for k, (s, c) in streams.items()}
        cur_phase[0] = 0.5

        s1a = ExitStack()
        A = s1a.enter_context
        wqT = A(nc.sbuf_tensor("wqT", [128, NCH * E], BF16, side="right"))
        wkT = A(nc.sbuf_tensor("wkT", [128, NCH * E], BF16, side="right"))
        x_t = A(nc.sbuf_tensor("x_t", [128, 2 * E], F32, side="right"))
        sq_t = A(nc.sbuf_tensor("sq_t", [128, E], BF16, side="right"))
        xln_t = A(nc.sbuf_tensor("xln_t", [128, 2 * E], BF16, side="right"))
        xfm_t = A(nc.sbuf_tensor("xfm_t", [128, 2 * 1024], BF16, side="right"))
        qev_t = A(nc.sbuf_tensor("qev_t", [128, 1024], BF16, side="right"))
        stat_t = A(nc.sbuf_tensor("stat_t", [128, 8], F32, side="right"))
        ptr = A(nc.psum_tensor("ptr", [128, 2048], F32))
        pk = A(nc.psum_tensor("pk", [128, 2048], F32))

        @on("sy")
        def _(e):
            if not KWT:
                return
            e.wait_ge(s_castq, qkv_cast)
            for fi in range(NCH):
                e.dma_start_transpose(wkT[:, fi * E:(fi + 1) * E], wkb[:, fi * 128:(fi + 1) * 128]).then_inc(s_wk, 16)
            for fi in range(NCH):
                e.dma_start_transpose(wqT[:, fi * E:(fi + 1) * E], wqb[:, fi * 128:(fi + 1) * 128]).then_inc(s_wq, 16)
        wk_a = bump(n_wk, NCH * 16) if KWT else 0
        wq_a = bump(n_wq, NCH * 16) if KWT else 0

        x_a = [0] * NT1
        stats_v = [0] * NT1
        rstd_s = [0] * NT1
        apply_v = [0] * NT1
        tr_t = [[0, 0] for _ in range(NT1)]
        fm_s = [[0, 0] for _ in range(NT1)]
        xlh_o = [0] * NT1
        kmm_t = [[0, 0] for _ in range(NT1)]
        kev_s = [[0, 0] for _ in range(NT1)]
        qmm_t = [[0, 0] for _ in range(NT1)]
        qev_s = [[0, 0] for _ in range(NT1)]
        qst_o = [0] * NT1

        k3 = k_sb[:, :].rearrange("p (c s) -> p c s", s=S)
        pk3 = pk[:, :].rearrange("p (c t) -> p c t", t=512)
        ptr3v = ptr[:, :].rearrange("p (c t) -> p c t", t=512)

        for i in range(NT1):
            sl = i % 2
            own = i < NOWN

            @on("sy")
            def _(e, i=i, sl=sl):
                if i >= 2:
                    e.wait_ge(sem_v, apply_v[i - 2])
                e.dma_start(x_t[:, sl * E:(sl + 1) * E], xs[i * 128:(i + 1) * 128, :]).then_inc(s_xP[sl][0], 16)
            x_a[i] = bump(s_xP[sl][1], 16)

            @on("v")
            def _(e, i=i, sl=sl):
                e.wait_ge(s_xP[sl][0], x_a[i])
                if i >= 1:
                    e.wait_ge(sem_s, rstd_s[i - 1])
                vec_lnstats(e, x_t[:, sl * E:(sl + 1) * E], sq_t[:, :], stat_t[:, 0:1],
                            stat_t[:, 2:3], stat_t[:, 1:2]).then_inc(sem_v, 1)
            stats_v[i] = bump(nv)

            @on("s")
            def _(e, i=i):
                if i == 0:
                    e.wait_ge(s_c0, p0_a)
                e.wait_ge(sem_v, stats_v[i])
                scl_rstd(e, stat_t[:, 1:2], stat_t[:, 3:4]).then_inc(sem_s, 1)
            rstd_s[i] = bump(ns)

            @on("v")
            def _(e, i=i, sl=sl):
                e.wait_ge(sem_s, rstd_s[i])
                if i >= 2:
                    e.wait_ge(sem_t, tr_t[i - 2][1])
                e.tensor_scalar(xln_t[:, sl * E:(sl + 1) * E], x_t[:, sl * E:(sl + 1) * E],
                                stat_t[:, 2:3], stat_t[:, 3:4], OP.subtract, OP.mult).then_inc(sem_v, 1)
            apply_v[i] = bump(nv)

            if KSUB < 2:
                continue

            fsl = i % 2
            for r in range(2):
                @on("t")
                def _(e, i=i, sl=sl, r=r):
                    if r == 0:
                        if i == 0:
                            e.wait_ge(s_c0, p0_a)
                        e.wait_ge(sem_v, apply_v[i])
                        if i >= 1:
                            e.wait_ge(sem_s, fm_s[i - 1][1])
                    else:
                        e.wait_ge(sem_s, fm_s[i][0])
                    mm = None
                    for j in range(4 * r, 4 * r + 4):
                        mm = e.matmul(ptr[:, (j % 4) * 512:(j % 4) * 512 + 128],
                                      xln_t[:, sl * E + j * 128: sl * E + (j + 1) * 128],
                                      id_sb[:, :], start=True, stop=True)
                    mm.then_inc(sem_t, 1)
                tr_t[i][r] = bump(nt)

                @on("s")
                def _(e, i=i, fsl=fsl, r=r):
                    e.wait_ge(sem_t, tr_t[i][r])
                    if r == 0 and i >= 2:
                        e.wait_ge(sem_t, (qmm_t if i - 2 < NOWN else kmm_t)[i - 2][1])
                        e.wait_ge(s_xlhP[i % 2][0], xlh_o[i - 2])
                    e.copy(xfm_t[:, fsl * 1024 + r * 512: fsl * 1024 + (r + 1) * 512],
                           ptr3v[:, :, 0:128]).then_inc(sem_s, 1)
                fm_s[i][r] = bump(ns)

            @on("sy")
            def _(e, i=i, fsl=fsl):
                e.wait_ge(sem_s, fm_s[i][1])
                src = xfm_t[:, fsl * 1024:(fsl + 1) * 1024].rearrange("p (c t) -> p c t", c=NCH)
                d = xlh[:, :, i * 128:(i + 1) * 128].transpose([1, 0, 2])
                e.dma_start(d, src).then_inc(s_xlhP[i % 2][0], 16)
            xlh_o[i] = bump(s_xlhP[i % 2][1], 16)

            if KSUB < 3:
                continue

            for h in range(2):
                @on("t")
                def _(e, i=i, fsl=fsl, h=h):
                    if h == 0:
                        e.wait_ge(sem_s, fm_s[i][1])
                        e.wait_ge(s_wk, wk_a)
                        if i >= 1:
                            e.wait_ge(sem_s, (qev_s if i - 1 < NOWN else kev_s)[i - 1][1])
                    else:
                        e.wait_ge(sem_s, kev_s[i][0])
                    xf = xfm_t[:, fsl * 1024:(fsl + 1) * 1024]
                    mm = None
                    for fo in range(4 * h, 4 * h + 4):
                        for fi in range(NCH):
                            mm = e.matmul(pk[:, (fo % 4) * 512:(fo % 4) * 512 + 128],
                                          wkT[:, fi * E + fo * 128: fi * E + (fo + 1) * 128],
                                          xf[:, fi * 128:(fi + 1) * 128],
                                          start=(fi == 0), stop=(fi == NCH - 1))
                    mm.then_inc(sem_t, 1)
                kmm_t[i][h] = bump(nt)

                @on("s")
                def _(e, i=i, h=h):
                    e.wait_ge(sem_t, kmm_t[i][h])
                    d = k3[:, 4 * h:4 * h + 4, i * 128:(i + 1) * 128]
                    e.copy(d, pk3[:, :, 0:128]).then_inc(sem_s, 1)
                kev_s[i][h] = bump(ns)

            if own and KSUB >= 4:
                for h in range(2):
                    @on("t")
                    def _(e, i=i, fsl=fsl, h=h):
                        e.wait_ge(sem_s, kev_s[i][1] if h == 0 else qev_s[i][0])
                        if h == 0 and i == 0:
                            e.wait_ge(s_wq, wq_a)
                        xf = xfm_t[:, fsl * 1024:(fsl + 1) * 1024]
                        mm = None
                        for fo in range(4 * h, 4 * h + 4):
                            for fi in range(NCH):
                                mm = e.matmul(pk[:, (fo % 4) * 512:(fo % 4) * 512 + 128],
                                              wqT[:, fi * E + fo * 128: fi * E + (fo + 1) * 128],
                                              xf[:, fi * 128:(fi + 1) * 128],
                                              start=(fi == 0), stop=(fi == NCH - 1))
                        mm.then_inc(sem_t, 1)
                    qmm_t[i][h] = bump(nt)

                    @on("s")
                    def _(e, i=i, h=h):
                        e.wait_ge(sem_t, qmm_t[i][h])
                        if i >= 1 and h == 0:
                            e.wait_ge(s_qst, qst_o[i - 1])
                        e.copy(qev_t[:, h * 512:(h + 1) * 512].rearrange("p (c t) -> p c t", t=128),
                               pk3[:, :, 0:128]).then_inc(sem_s, 1)
                    qev_s[i][h] = bump(ns)

                @on("sy")
                def _(e, i=i):
                    e.wait_ge(sem_s, qev_s[i][1])
                    src = qev_t[:, :].rearrange("p (c t) -> p c t", c=NCH)
                    d = qh[:, :, i * 128:(i + 1) * 128].transpose([1, 0, 2])
                    e.dma_start(d, src).then_inc(s_qst, 16)
                qst_o[i] = bump(n_qst, 16)

        bar1_t = nt[0]
        bar1_v = nv[0]
        bar1_s = ns[0]
        phase_tot[1] = {k: c[0] for k, (s, c) in streams.items()}
        cur_phase[0] = 1

        s1a.close()

        s1b = ExitStack()
        A = s1b.enter_context
        wvT = A(nc.sbuf_tensor("wvT", [128, NCH * E], BF16, side="right"))
        xfb_t = A(nc.sbuf_tensor("xfb_t", [128, 2 * 1024], BF16, side="right"))
        pvA = A(nc.psum_tensor("pvA", [128, 512], F32))
        pvB = A(nc.psum_tensor("pvB", [128, 512], F32))

        v4 = v_sb[:, :].rearrange("p (k h d) -> p k h d", k=NKC, h=H)

        @on("sy")
        def _(e):
            e.wait_ge(sem_t, bar1_t)
            e.wait_ge(sem_v, bar1_v)
            e.wait_ge(sem_s, bar1_s)
            e.wait_ge(s_castv, castv_c)
            for fi in range(NCH):
                e.dma_start_transpose(wvT[:, fi * E:(fi + 1) * E], wvb[:, fi * 128:(fi + 1) * 128]).then_inc(s_wv, 16)
        wv_a = bump(n_wv, NCH * 16)

        xfb_a = [0] * NT1
        vmm_t = [[0, 0] for _ in range(NT1)]
        vev_s = [[0, 0] for _ in range(NT1)]

        for i in range(NT1):
            sl = i % 2

            @on("sy")
            def _(e, i=i, sl=sl):
                e.wait_ge(s_xlhP[i % 2][0], xlh_o[i])
                if i >= 2:
                    e.wait_ge(sem_t, vmm_t[i - 2][1])
                s = xlh[:, :, i * 128:(i + 1) * 128].transpose([1, 0, 2])
                e.dma_start(xfb_t[:, sl * 1024:(sl + 1) * 1024].rearrange("p (c t) -> p c t", c=NCH),
                            s).then_inc(s_xfbP[sl][0], 16)
            xfb_a[i] = bump(s_xfbP[sl][1], 16)

            for h in range(2):
                @on("t")
                def _(e, i=i, sl=sl, h=h):
                    if h == 0:
                        e.wait_ge(s_xfbP[sl][0], xfb_a[i])
                        e.wait_ge(s_wv, wv_a)
                    if i >= 1:
                        e.wait_ge(sem_s, vev_s[i - 1][h])
                    bank = pvA if h == 0 else pvB
                    mm = None
                    for fi in range(NCH):
                        mm = e.matmul(bank[:, :],
                                      xfb_t[:, sl * 1024 + fi * 128: sl * 1024 + (fi + 1) * 128],
                                      wvT[:, fi * E + h * 512: fi * E + h * 512 + 512],
                                      start=(fi == 0), stop=(fi == NCH - 1))
                    mm.then_inc(sem_t, 1)
                vmm_t[i][h] = bump(nt)

                @on("s")
                def _(e, i=i, h=h):
                    e.wait_ge(sem_t, vmm_t[i][h])
                    if i == 0 and h == 0:
                        e.wait_ge(sem_g, memset_g)
                    bank = pvA if h == 0 else pvB
                    d = v4[:, i, 8 * h:8 * h + 8, 0:64]
                    s = bank[:, :].rearrange("p (h d) -> p h d", d=64)
                    e.copy(d, s).then_inc(sem_s, 1)
                vev_s[i][h] = bump(ns)

        bar2_t = nt[0]
        bar2_v = nv[0]
        bar2_s = ns[0]
        phase_tot[2] = {k: c[0] for k, (s, c) in streams.items()}
        cur_phase[0] = 2

        s1b.close()

        s2 = ExitStack()
        A = s2.enter_context
        dist_t = A(nc.sbuf_tensor("dist_t", [128, NKC * QC], BF16, side="right"))
        probs_t = A(nc.sbuf_tensor("probs_t", [128, 4096], BF16, side="right"))
        qc_t = A(nc.sbuf_tensor("qc_t", [128, 2 * 2048], BF16, side="right"))
        o_t = A(nc.sbuf_tensor("o_t", [128, 2048], F32, side="right"))
        xr_t = A(nc.sbuf_tensor("xr_t", [128, 2 * E], F32, side="right"))
        x1_t = A(nc.sbuf_tensor("x1_t", [128, 2 * E], F32, side="right"))
        t1_t = A(nc.sbuf_tensor("t1_t", [128, E], F32, side="right"))
        st2_t = A(nc.sbuf_tensor("st2_t", [128, 16], F32, side="right"))
        pars = A(nc.psum_tensor("pars", [128, 4096], F32))

        NH = H
        qc_a = [0] * NQC
        dist_a = [0] * NQC
        xr_a = [0] * NQC
        qk_t, mult_v, exp_s, av_t, av_last, evac_v = {}, {}, {}, {}, {}, {}
        ln2_v = [0] * NQC
        x1st_o = [0] * NQC

        def av_col(kc):
            return ((kc // 4) % 3) * 1024 + (kc % 2) * 512 + ((kc % 4) // 2) * 256

        for qc in range(NQC):
            qsl = qc % 2

            @on("sy")
            def _(e, qc=qc):
                if qc == 0:
                    e.wait_ge(sem_t, bar2_t)
                    e.wait_ge(sem_v, bar2_v)
                    e.wait_ge(sem_s, bar2_s)
                    e.wait_ge(s_qst, qst_o[NOWN - 1])
                if qc >= 2:
                    e.wait_ge(sem_t, qk_t[(qc - 2, NH - 1, NKC - 1)])
                s = qh[:, :, qc * QC:(qc + 1) * QC].transpose([1, 0, 2])
                e.dma_start(qc_t[:, (qc % 2) * 2048:(qc % 2 + 1) * 2048].rearrange("p (c q) -> p c q", c=NCH),
                            s).then_inc(s_qc, 16)
            qc_a[qc] = bump(n_qc, 16)

            for dg in range(4):
                @on("sy")
                def _(e, qc=qc, dg=dg):
                    if qc >= 1:
                        e.wait_ge(sem_v, mult_v[(qc - 1, NH - 1, 4 * dg + 3)])
                    e.dma_start(dist_t[:, dg * 8 * QC:(dg + 1) * 8 * QC],
                                dst[qc * 4 + dg, :, :]).then_inc(s_distG[dg][0], 16)
                bump(s_distG[dg][1], 16)
            dist_a[qc] = [s_distG[dg][1][0] for dg in range(4)]

            @on("sy")
            def _(e, qc=qc, qsl=qsl):
                if qc >= 2:
                    e.wait_ge(sem_v, ln2_v[qc - 2])
                    e.wait_ge(s_xsP[qc % 2][0], x1st_o[qc - 2])
                e.dma_start(xr_t[:, qsl * E:(qsl + 1) * E], xs[qc * QC:qc * QC + 128, :]).then_inc(s_xrP[qsl][0], 16)
                e.dma_start(x1_t[:, qsl * E:(qsl + 1) * E], xs[qc * QC + 128:qc * QC + 256, :]).then_inc(s_xrP[qsl][0], 16)
            xr_a[qc] = bump(s_xrP[qsl][1], 32)

            for h in range(NH):
                Gi = qc * NH + h
                hh = (h % 2) * 64
                g = h // 2

                def em_av(kc, qc=qc, h=h, Gi=Gi):
                    Qi = Gi * 8 + kc // 4

                    @on("t")
                    def _(e, qc=qc, h=h, kc=kc, Qi=Qi, Gi=Gi):
                        if kc % 4 == 0:
                            e.wait_ge(sem_s, exp_s[Qi])
                        if kc == 0 and Gi >= 1:
                            e.wait_ge(sem_v, evac_v[Gi - 1])
                        pcol = (Qi % 4) * 1024 + (kc % 2) * 512 + ((kc % 4) // 2) * 256
                        mm = None
                        for qh2 in range(2):
                            st = probs_t[:, pcol + qh2 * 128: pcol + qh2 * 128 + 128]
                            mv = v_sb[:, kc * VROW + h * 65: kc * VROW + (h + 1) * 65]
                            mm = e.matmul(pars[:, 3072 + qh2 * 512: 3072 + qh2 * 512 + 65], st, mv,
                                          start=(kc == 0), stop=(kc == NKC - 1))
                        mm.then_inc(sem_t, 1)
                    av_t[(qc, h, kc)] = bump(nt)
                    if kc % 4 == 3:
                        av_last[Qi] = av_t[(qc, h, kc)]

                for kc in range(NKC):
                    Qi = Gi * 8 + kc // 4
                    col = av_col(kc)

                    @on("t")
                    def _(e, qc=qc, h=h, kc=kc, col=col, Qi=Qi, hh=hh, g=g):
                        if kc == 0 and h == 0:
                            e.wait_ge(s_qc, qc_a[qc])
                        if kc % 4 == 0 and Qi >= 3:
                            e.wait_ge(sem_s, exp_s[Qi - 3])
                        st = k_sb[hh:hh + 64, g * S + kc * 128: g * S + kc * 128 + 128]
                        mv = qc_t[hh:hh + 64, (qc % 2) * 2048 + g * QC: (qc % 2) * 2048 + (g + 1) * QC]
                        e.matmul(pars[:, col: col + QC], st, mv, start=True, stop=True,
                                 tile_position=(hh, 0)).then_inc(sem_t, 1)
                    qk_t[(qc, h, kc)] = bump(nt)

                    if kc % 2 == 1:
                        p = kc // 2

                        @on("v")
                        def _(e, qc=qc, h=h, p=p, kc=kc):
                            e.wait_ge(sem_t, qk_t[(qc, h, kc)])
                            if h == 0 and kc % 8 == 1:
                                e.wait_ge(s_distG[kc // 8][0], dist_a[qc][kc // 8])
                            base = ((kc // 4) % 3) * 1024 + (p % 2) * 256
                            sc2 = bass.AP(pars, base, [[4096, 128], [512, 2], [1, QC]])
                            rep = bass.AP(dist_t, (2 * p) * QC, [[NKC * QC, 128], [QC, 2], [1, QC]])
                            e.tensor_tensor(sc2, sc2, rep, OP.mult).then_inc(sem_v, 1)
                        mult_v[(qc, h, p)] = bump(nv)

                    if kc % 4 == 3:
                        @on("s")
                        def _(e, qc=qc, h=h, kc=kc, Qi=Qi):
                            e.wait_ge(sem_v, mult_v[(qc, h, kc // 2)])
                            if Qi >= 4:
                                e.wait_ge(sem_t, av_last[Qi - 4])
                            qs3 = (kc // 4) % 3
                            src = bass.AP(pars, qs3 * 1024, [[4096, 128], [512, 2], [256, 2], [1, QC]])
                            dsx = bass.AP(probs_t, (Qi % 4) * 1024, [[4096, 128], [512, 2], [256, 2], [1, QC]])
                            e.activation(dsx, src, AF.Exp, scale=1.0 / 32.0).then_inc(sem_s, 1)
                        exp_s[Qi] = bump(ns)
                        if kc // 4 >= 2:
                            for k2 in range(4 * (kc // 4 - 2), 4 * (kc // 4 - 2) + 4):
                                em_av(k2)

                for k2 in range(NKC - 8, NKC):
                    em_av(k2)

                @on("v")
                def _(e, qc=qc, h=h, Gi=Gi, w=av_t[(qc, h, NKC - 1)], wrc=nv[0] + 1):
                    e.wait_ge(sem_t, w)
                    if h == 0 and qc >= 1:
                        e.wait_ge(sem_v, ln2_v[qc - 1])
                    src = bass.AP(pars, 3072 + 64, [[4096, 128], [512, 2], [1, 1]])
                    e.reciprocal(st2_t[:, 8:10], src).then_inc(sem_v, 1)
                    e.wait_ge(sem_v, wrc)
                    mm = None
                    for qh2 in range(2):
                        mm = e.tensor_scalar(o_t[:, qh2 * 1024 + h * 64: qh2 * 1024 + (h + 1) * 64],
                                             pars[:, 3072 + qh2 * 512: 3072 + qh2 * 512 + 64],
                                             st2_t[:, 8 + qh2: 9 + qh2], None, OP.mult)
                    mm.then_inc(sem_v, 1)
                bump(nv)
                evac_v[Gi] = bump(nv)

            ln2sub = [0, 0]
            for ts in range(2):
                @on("v")
                def _(e, qc=qc, ts=ts):
                    if qc == 0 and ts == 0:
                        e.wait_ge(s_c0, p0_a)
                    e.wait_ge(s_xrP[qc % 2][0], xr_a[qc])
                    src = o_t[:, ts * 1024:(ts + 1) * 1024]
                    vec_lnstats(e, src, t1_t[:, :], st2_t[:, 0:1],
                                st2_t[:, 2:3], st2_t[:, 1:2]).then_inc(sem_v, 1)
                st2v = bump(nv)

                @on("s")
                def _(e, w=st2v):
                    e.wait_ge(sem_v, w)
                    scl_rstd(e, st2_t[:, 1:2], st2_t[:, 3:4]).then_inc(sem_s, 1)
                st2s = bump(ns)

                @on("v")
                def _(e, qc=qc, qsl=qsl, ts=ts, w=st2s):
                    e.wait_ge(sem_s, w)
                    src = o_t[:, ts * 1024:(ts + 1) * 1024]
                    xbuf = xr_t if ts == 0 else x1_t
                    xap = xbuf[:, qsl * E:(qsl + 1) * E]
                    e.tensor_scalar(t1_t[:, :], src, st2_t[:, 2:3], st2_t[:, 3:4], OP.subtract, OP.mult)
                    e.tensor_tensor(t1_t[:, :], t1_t[:, :], gb_sb[:, :], OP.mult)
                    e.tensor_tensor(t1_t[:, :], t1_t[:, :], bb_sb[:, :], OP.add)
                    e.tensor_tensor(xap, t1_t[:, :], xap, OP.add)
                    ti = qc * 2 + ts
                    vec_lnstats(e, xap, t1_t[:, :], st2_t[:, 4:5],
                                st3_sb[:, ti * 2: ti * 2 + 1], st2_t[:, 5:6]).then_inc(sem_v, 1)
                st3v = bump(nv)
                ln2sub[ts] = st3v

                @on("s")
                def _(e, qc=qc, ts=ts, w=st3v):
                    e.wait_ge(sem_v, w)
                    ti = qc * 2 + ts
                    scl_rstd(e, st2_t[:, 5:6], st3_sb[:, ti * 2 + 1: ti * 2 + 2]).then_inc(sem_s, 1)
                bump(ns)

                @on("sy")
                def _(e, qc=qc, qsl=qsl, ts=ts, w=st3v):
                    e.wait_ge(sem_v, w)
                    xbuf = xr_t if ts == 0 else x1_t
                    e.dma_start(x1h[qc * QC + ts * 128: qc * QC + (ts + 1) * 128, :],
                                xbuf[:, qsl * E:(qsl + 1) * E]).then_inc(s_xsP[qc % 2][0], 16)
                x1st_o[qc] = bump(s_xsP[qc % 2][1], 16)
            ln2_v[qc] = ln2sub[1]

        bar3_t = nt[0]
        bar3_v = nv[0]
        bar3_s = ns[0]

        if KDBG:
            @on("sy")
            def _(e):
                e.wait_ge(sem_t, bar3_t)
                e.wait_ge(sem_v, bar3_v)
                e.wait_ge(sem_s, bar3_s)
                e.dma_start(dbg_k[:, :], k_sb[:, :]).then_inc(s_dbg, 16)
                e.dma_start(dbg_v[:, :], v_sb[:, :]).then_inc(s_dbg, 16)
                e.dma_start(dbg_xlh[:, :, :], xlh[:, :, :]).then_inc(s_dbg, 16)
                e.dma_start(dbg_qh[:, :, :], qh[:, :, :]).then_inc(s_dbg, 16)
                e.dma_start(dbg_x1[:, :], x1h[:, :]).then_inc(s_dbg, 16)
            bump(n_dbg, 80)
        phase_tot[3] = {k: c[0] for k, (s, c) in streams.items()}
        cur_phase[0] = 3

        s2.close()
        skv.close()

        s3 = ExitStack()
        A = s3.enter_context
        wgT = A(nc.sbuf_tensor("wgT", [128, NCH * HID], BF16, side="right"))
        wuT = A(nc.sbuf_tensor("wuT", [128, NCH * HID], BF16, side="right"))
        wdT = A(nc.sbuf_tensor("wdT", [128, NHC * E], BF16, side="right"))
        x13_t = A(nc.sbuf_tensor("x13_t", [128, 2 * E], F32, side="right"))
        xl3_t = A(nc.sbuf_tensor("xl3_t", [128, 2 * E], BF16, side="right"))
        xf3_t = A(nc.sbuf_tensor("xf3_t", [128, 2 * 2048], BF16, side="right"))
        act_t = A(nc.sbuf_tensor("act_t", [128, 2 * NHC * 256], BF16, side="right"))
        ssg_t = A(nc.sbuf_tensor("ssg_t", [128, 256], F32, side="right"))
        y_t = A(nc.sbuf_tensor("y_t", [128, 2 * E], F32, side="right"))
        ptr3 = A(nc.psum_tensor("ptr3", [128, 1024], F32))
        pg3 = [A(nc.psum_tensor(f"pg3{p}", [128, 256], F32)) for p in range(2)]
        pu3 = [A(nc.psum_tensor(f"pu3{p}", [128, 256], F32)) for p in range(2)]
        pd3 = [A(nc.psum_tensor(f"pd3{p}", [128, 512], F32)) for p in range(2)]

        @on("sy")
        def _(e):
            e.wait_ge(sem_t, bar3_t)
            e.wait_ge(sem_v, bar3_v)
            e.wait_ge(sem_s, bar3_s)
            e.wait_ge(s_castf, ffn_cast)
            HH = HID // 2
            for fi in range(NCH):
                e.dma_start_transpose(wgT[:, fi * HID:fi * HID + HH], wgb[0:HH, fi * 128:(fi + 1) * 128]).then_inc(s_w3, 16)
            for fi in range(NCH):
                e.dma_start_transpose(wuT[:, fi * HID:fi * HID + HH], wub[0:HH, fi * 128:(fi + 1) * 128]).then_inc(s_w3u, 16)
            for fi in range(NCH):
                e.dma_start_transpose(wgT[:, fi * HID + HH:(fi + 1) * HID], wgb[HH:HID, fi * 128:(fi + 1) * 128]).then_inc(s_w3b, 16)
            for fi in range(NCH):
                e.dma_start_transpose(wuT[:, fi * HID + HH:(fi + 1) * HID], wub[HH:HID, fi * 128:(fi + 1) * 128]).then_inc(s_w3c, 16)
            for hc in range(NHC):
                e.dma_start_transpose(wdT[:, hc * E:(hc + 1) * E], wdb[:, hc * 128:(hc + 1) * 128]).then_inc(s_w3d, 16)
        wg_a = bump(n_w3, NCH * 16)
        wu_a = bump(n_w3u, NCH * 16)
        wgb_a = bump(n_w3b, NCH * 16)
        wub_a = bump(n_w3c, NCH * 16)
        wd_a = bump(n_w3d, NHC * 16)

        x13_a = [0] * NTC3
        ln3_v = [[0, 0] for _ in range(NTC3)]
        tr3_t = {}
        fm3_s = {}
        gmm_t, umm_t, sil_s, act_v = {}, {}, {}, {}
        dmm_t, dev_v = {}, {}
        yst_o = [0] * NTC3

        for tc in range(NTC3):
            tsl = tc % 2

            @on("sy")
            def _(e, tc=tc):
                if tc == 0:
                    e.wait_ge(sem_t, bar3_t)
                    e.wait_ge(sem_v, bar3_v)
                    e.wait_ge(sem_s, bar3_s)
                e.wait_ge(s_xsP[tc % 2][0], x1st_o[tc])
                if tc >= 1:
                    e.wait_ge(sem_v, dev_v[(tc - 1, 1, 1)])
                e.dma_start(x13_t[:, 0:E], x1h[tc * 256:tc * 256 + 128, :]).then_inc(s_x13, 16)
                e.dma_start(x13_t[:, E:2 * E], x1h[tc * 256 + 128:tc * 256 + 256, :]).then_inc(s_x13, 16)
            x13_a[tc] = bump(n_x13, 32)

            for ts in range(2):
                @on("v")
                def _(e, tc=tc, ts=ts):
                    e.wait_ge(s_x13, x13_a[tc])
                    if tc >= 1:
                        e.wait_ge(sem_t, tr3_t[(tc - 1, ts, 3)])
                    ti = tc * 2 + ts
                    e.tensor_scalar(xl3_t[:, ts * E:(ts + 1) * E], x13_t[:, ts * E:(ts + 1) * E],
                                    st3_sb[:, ti * 2:ti * 2 + 1], st3_sb[:, ti * 2 + 1:ti * 2 + 2],
                                    OP.subtract, OP.mult).then_inc(sem_v, 1)
                ln3_v[tc][ts] = bump(nv)

                for r in range(4):
                    @on("t")
                    def _(e, tc=tc, ts=ts, r=r):
                        if r == 0:
                            e.wait_ge(sem_v, ln3_v[tc][ts])
                            if tc >= 1 or ts == 1:
                                pkey = (tc, 0, 3) if ts == 1 else (tc - 1, 1, 3)
                                e.wait_ge(sem_s, fm3_s[pkey])
                        else:
                            e.wait_ge(sem_s, fm3_s[(tc, ts, r - 1)])
                        mm = None
                        for j in (2 * r, 2 * r + 1):
                            mm = e.matmul(ptr3[:, (j % 2) * 512:(j % 2) * 512 + 128],
                                          xl3_t[:, ts * E + j * 128: ts * E + (j + 1) * 128],
                                          id_sb[:, :], start=True, stop=True)
                        mm.then_inc(sem_t, 1)
                    tr3_t[(tc, ts, r)] = bump(nt)

                    @on("s")
                    def _(e, tc=tc, tsl=tsl, ts=ts, r=r):
                        e.wait_ge(sem_t, tr3_t[(tc, ts, r)])
                        if tc >= 2 and ts == 0 and r == 0:
                            e.wait_ge(sem_t, umm_t[(tc - 2, NHC - 1)])
                        d = bass.AP(xf3_t, tsl * 2048 + (2 * r) * 256 + ts * 128,
                                    [[2 * 2048, 128], [256, 2], [1, 128]])
                        s = ptr3[:, :].rearrange("p (c t) -> p c t", t=512)[:, :, 0:128]
                        e.copy(d, s).then_inc(sem_s, 1)
                    fm3_s[(tc, ts, r)] = bump(ns)

            for hc in range(NHC):
                pp = hc % 2

                @on("t")
                def _(e, tc=tc, tsl=tsl, hc=hc, pp=pp):
                    if hc == 0:
                        e.wait_ge(sem_s, fm3_s[(tc, 1, 3)])
                        if tc == 0:
                            e.wait_ge(s_w3, wg_a)
                    if hc == NHC // 2 and tc == 0:
                        e.wait_ge(s_w3b, wgb_a)
                        if tc >= 2:
                            e.wait_ge(sem_t, dmm_t[(tc - 2, 1, 1)])
                    if hc >= 2:
                        e.wait_ge(sem_s, sil_s[(tc, hc - 2)])
                    xf = xf3_t[:, tsl * 2048:(tsl + 1) * 2048]
                    mm = None
                    for fi in range(NCH):
                        mm = e.matmul(pg3[pp][:, :],
                                      wgT[:, fi * HID + hc * 128: fi * HID + (hc + 1) * 128],
                                      xf[:, fi * 256:(fi + 1) * 256],
                                      start=(fi == 0), stop=(fi == NCH - 1))
                    mm.then_inc(sem_t, 1)
                gmm_t[(tc, hc)] = bump(nt)

                @on("t")
                def _(e, tc=tc, tsl=tsl, hc=hc, pp=pp):
                    if hc == 0 and tc == 0:
                        e.wait_ge(s_w3u, wu_a)
                    if hc == NHC // 2 and tc == 0:
                        e.wait_ge(s_w3c, wub_a)
                    if hc >= 2:
                        e.wait_ge(sem_v, act_v[(tc, hc - 2)])
                    xf = xf3_t[:, tsl * 2048:(tsl + 1) * 2048]
                    mm = None
                    for fi in range(NCH):
                        mm = e.matmul(pu3[pp][:, :],
                                      wuT[:, fi * HID + hc * 128: fi * HID + (hc + 1) * 128],
                                      xf[:, fi * 256:(fi + 1) * 256],
                                      start=(fi == 0), stop=(fi == NCH - 1))
                    mm.then_inc(sem_t, 1)
                umm_t[(tc, hc)] = bump(nt)

                @on("s")
                def _(e, tc=tc, hc=hc, pp=pp):
                    e.wait_ge(sem_t, gmm_t[(tc, hc)])
                    if hc >= 1:
                        e.wait_ge(sem_v, act_v[(tc, hc - 1)])
                    e.activation(ssg_t[:, :], pg3[pp][:, :], AF.Silu, scale=SC127).then_inc(sem_s, 1)
                sil_s[(tc, hc)] = bump(ns)

                @on("v")
                def _(e, tc=tc, hc=hc, pp=pp):
                    e.wait_ge(sem_s, sil_s[(tc, hc)])
                    e.wait_ge(sem_t, umm_t[(tc, hc)])
                    e.tensor_tensor(act_t[:, (tc % 2) * NHC * 256 + hc * 256: (tc % 2) * NHC * 256 + (hc + 1) * 256],
                                    ssg_t[:, :], pu3[pp][:, :], OP.mult).then_inc(sem_v, 1)
                act_v[(tc, hc)] = bump(nv)

            for ts in range(2):
                for fb in range(2):
                    pp = fb

                    @on("t")
                    def _(e, tc=tc, ts=ts, fb=fb, pp=pp):
                        if ts == 0 and fb == 0:
                            e.wait_ge(sem_v, act_v[(tc, NHC - 1)])
                            if tc == 0:
                                e.wait_ge(s_w3d, wd_a)
                        prev = (tc, ts - 1, fb) if ts >= 1 else (tc - 1, 1, fb)
                        if prev in dev_v:
                            e.wait_ge(sem_v, dev_v[prev])
                        mm = None
                        for hc in range(NHC):
                            mm = e.matmul(pd3[pp][:, :],
                                          act_t[:, (tc % 2) * NHC * 256 + hc * 256 + ts * 128: (tc % 2) * NHC * 256 + hc * 256 + ts * 128 + 128],
                                          wdT[:, hc * E + fb * 512: hc * E + fb * 512 + 512],
                                          start=(hc == 0), stop=(hc == NHC - 1))
                        mm.then_inc(sem_t, 1)
                    dmm_t[(tc, ts, fb)] = bump(nt)

                    @on("v")
                    def _(e, tc=tc, ts=ts, fb=fb, pp=pp):
                        e.wait_ge(sem_t, dmm_t[(tc, ts, fb)])
                        if fb == 0 and ts == 0 and tc >= 1:
                            e.wait_ge(s_y, yst_o[tc - 1])
                        e.scalar_tensor_tensor(y_t[:, ts * E + fb * 512: ts * E + fb * 512 + 512],
                                               pd3[pp][:, :], SC2,
                                               x13_t[:, ts * E + fb * 512: ts * E + fb * 512 + 512],
                                               OP.mult, OP.add).then_inc(sem_v, 1)
                    dev_v[(tc, ts, fb)] = bump(nv)

                @on("sy")
                def _(e, tc=tc, ts=ts):
                    e.wait_ge(sem_v, dev_v[(tc, ts, 1)])
                    e.dma_start(y[tc * 256 + ts * 128: tc * 256 + (ts + 1) * 128, :],
                                y_t[:, ts * E:(ts + 1) * E]).then_inc(s_y, 16)
                yst_o[tc] = bump(n_y, 16)

        @on("sy")
        def _(e, wy=n_y[0], wd2=n_dbg[0]):
            e.wait_ge(s_y, wy)
            if wd2:
                e.wait_ge(s_dbg, wd2)

        s3.close()

        phase_tot[9] = {k: c[0] for k, (s, c) in streams.items()}
        blk = Eo(nc.Block())
        keys = {"t": blk.tensor, "v": blk.vector, "s": blk.scalar, "sy": blk.sync, "g": blk.gpsimd}
        trunc = phase_tot.get(PH) if (PH in phase_tot and PH < 9) else None
        for key in ["sy", "g", "t", "v", "s"]:
            fns = [fn for k, fn, ph in sched if k == key and (PH >= 9 or ph < PH)]
            if trunc is not None and key == "sy":
                eng_snap = trunc.pop("__eng__", None)

                def fin(e, snap=dict(trunc)):
                    for nm, val in snap.items():
                        if val:
                            e.wait_ge(streams[nm][0], val)
                fns = fns + [fin]

            def run_all(e, fns=fns):
                for fn in fns:
                    fn(e)
            keys[key](run_all)

    return nc


_CACHE = {}


def _get_nc():
    if "nc" not in _CACHE:
        _CACHE["nc"] = build_program()
    return _CACHE["nc"]


def kernel(L0_attn_x, L0_attn_Wq, L0_attn_Wk, L0_attn_Wv, L0_attn_ln_gamma,
           L0_attn_ln_beta, L0_ffn_w_gate, L0_ffn_w_up, L0_ffn_w_down):
    x = np.ascontiguousarray(np.asarray(L0_attn_x, dtype=np.float32))
    wq = np.ascontiguousarray(np.asarray(L0_attn_Wq, dtype=np.float32))
    wk = np.ascontiguousarray(np.asarray(L0_attn_Wk, dtype=np.float32))
    wv = np.ascontiguousarray(np.asarray(L0_attn_Wv, dtype=np.float32))
    g = np.asarray(L0_attn_ln_gamma, dtype=np.float32)
    be = np.asarray(L0_attn_ln_beta, dtype=np.float32)
    wg = np.ascontiguousarray(np.asarray(L0_ffn_w_gate, dtype=np.int32))
    wu = np.ascontiguousarray(np.asarray(L0_ffn_w_up, dtype=np.int32))
    wd = np.ascontiguousarray(np.asarray(L0_ffn_w_down, dtype=np.int32))

    gbc = np.ascontiguousarray(np.broadcast_to(g[None, :], (128, E)))
    bbc = np.ascontiguousarray(np.broadcast_to(be[None, :], (128, E)))
    idn = np.eye(128, dtype=bf16)
    cstv = np.zeros((128, 8), np.float32)
    cstv[:, 0] = EPS
    cstv[:, 1] = -0.5
    cstv[:, 2] = 1.0 / 32.0
    cstv[:, 3] = SC127
    cstv[:, 4] = -1.0

    idx = np.arange(S, dtype=np.float64)
    in_maps = []
    for c in range(8):
        b, hq = divmod(c, 2)
        qlo = hq * QT
        perm = np.concatenate([np.arange(qlo, qlo + QT), np.arange(0, qlo),
                               np.arange(qlo + QT, S)])
        xp = np.ascontiguousarray(x[b][perm])
        dmat = np.abs(idx[perm][:, None] - idx[None, qlo:qlo + QT]) / S
        dd = dmat.astype(bf16).reshape(4, 8, 128, NQC, QC).transpose(3, 0, 2, 1, 4)
        in_maps.append(dict(
            xs=xp, wqi=wq, wki=wk, wvi=wv, gbc=gbc, bbc=bbc,
            wgi=wg, wui=wu, wdi=wd,
            dst=np.ascontiguousarray(dd).reshape(NQC * 4, 128, 8 * QC),
            idn=idn, cst=cstv,
        ))

    nc = _get_nc()
    res = run_bass_kernel_spmd(nc, in_maps, core_ids=list(range(8)))
    out = np.empty((B, S, E), dtype=np.float32)
    for c in range(8):
        b, hq = divmod(c, 2)
        out[b, hq * QT:(hq + 1) * QT, :] = res.results[c]["y"]
    return out


if __name__ == "__main__":
    nc = build_program()
    print("built ok; instructions:", nc.n_instructions())
